# version 14
# baseline (speedup 1.0000x reference)
import contextlib
import numpy as np
import ml_dtypes
import concourse.bass as bass
import concourse.mybir as mybir
from concourse.bass_utils import run_bass_kernel_spmd

F32 = mybir.dt.float32
BF16 = mybir.dt.bfloat16
I32 = mybir.dt.int32
AF = mybir.ActivationFunctionType
ALU = mybir.AluOpType
AX = mybir.AxisListType.X

D = 1024
T = 2048
NC_ = 8
R = 64
RS = 16
WI = 256
NIB = T // WI
SC = 512
NSC = T // SC
F32EPS = float(np.finfo(np.float32).eps)

BF16NP = ml_dtypes.bfloat16
B2_NIB = NIB
B2_LVL = 6
B2_PATHS = 2
B3_ON = True
B4_ON = True
B5_ON = True
B5_SER = False
CAP = 768
JC = CAP // 128
SUBS = ((0, 512), (512, 256))
TAG_B2_FULL = True
NIBG = None



class Ctx:

    def __init__(self, nc):
        self.nc = nc
        self.cnt = {}
        self.sems = {}
        self.marks = {}
        self.emitting = False
        self.cur = None

    def sem(self, name):
        h = self.nc.alloc_semaphore(name)
        self.sems[name] = h
        self.cnt[name] = 0
        return h

    def op(self, eng, fn, inc=None, n=1, mark=None):
        if not self.emitting:
            if inc is not None:
                self.cnt[inc] += n
                if mark is not None:
                    self.marks[mark] = self.cnt[inc]
                if self.cur is not None:
                    self.cur.append(("inc", inc, n, mark))
            return None
        instr = fn(eng)
        if inc is not None:
            self.cnt[inc] += n
            instr.then_inc(self.sems[inc], n)
        return instr

    def dma(self, eng, out, in_, inc, mark=None):
        return self.op(eng, lambda e: e.dma_start(out=out, in_=in_),
                       inc=inc, n=16, mark=mark)

    def wait(self, eng, sem, mark=None, value=None):
        if not self.emitting:
            if self.cur is not None:
                self.cur.append(("wait", mark if mark is not None else value,
                                 sem, mark))
            return
        v = value if value is not None else self.marks[mark]
        eng.wait_ge(self.sems[sem], v)

    def mark(self, key, sem):
        if not self.emitting:
            self.marks[key] = self.cnt[sem]

    def run_block(self, bodies):
        saved = dict(self.cnt)
        self.emitting = False
        self.rec = []
        self.cur = None
        for name, fn in bodies:
            self.cur = []
            self.rec.append((name, self.cur))
            fn(self, None)
        self.cur = None
        self._check_deadlock(saved)
        p1end = dict(self.cnt)
        self.cnt = saved
        self.emitting = True
        with self.nc.Block() as block:
            for (name, fn), (_, ev) in zip(bodies, self.rec):
                if not ev:
                    continue
                reg = getattr(block, name)
                reg(lambda eng, fn=fn: fn(self, eng))
        self.emitting = False
        if self.cnt != p1end:
            diff = {k: (p1end.get(k), self.cnt.get(k))
                    for k in set(p1end) | set(self.cnt)
                    if p1end.get(k) != self.cnt.get(k)}
            raise RuntimeError(f"pass1/pass2 count divergence: {diff}")

    def _check_deadlock(self, base):
        streams = {}
        for name, ev in self.rec:
            streams.setdefault(name, []).extend(ev)
        pos = {k: 0 for k in streams}
        val = dict(base)
        progress = True
        while progress:
            progress = False
            for k, evs in streams.items():
                while pos[k] < len(evs):
                    e = evs[pos[k]]
                    if e[0] == "inc":
                        val[e[1]] = val.get(e[1], 0) + e[2]
                        pos[k] += 1
                        progress = True
                    else:
                        want = self.marks[e[1]] if isinstance(e[1], str) else e[1]
                        sem = e[2]
                        if val.get(sem, 0) >= want:
                            pos[k] += 1
                            progress = True
                        else:
                            break
        stuck = {k: pos[k] for k in streams if pos[k] < len(streams[k])}
        if stuck:
            lines = []
            for k in stuck:
                e = streams[k][pos[k]]
                want = self.marks[e[1]] if isinstance(e[1], str) else e[1]
                lines.append(f"  {k}@{pos[k]}/{len(streams[k])}: waiting "
                             f"{e[2]} >= {want} (now {val.get(e[2], 0)}) "
                             f"mark={e[3] if len(e) > 3 else ''}")
            raise RuntimeError("DEADLOCK detected in block:\n" + "\n".join(lines))



def build_nc(taps=()):
    global NIBG
    NIBG = B2_NIB * NC_
    nc = bass.Bass("TRN2", target_bir_lowering=False)
    ctx = Ctx(nc)
    for s in ["sSP", "sPL", "sPE", "sDVE", "sACT", "sPLc",
              "sLD0", "sLD1", "sST0", "sST1", "sPLr", "sPLd",
              "sYS0", "sYS1", "sPOOL", "sIDX", "sGA", "sSA",
              "sSH0", "sSH1", "sXT0", "sXT1", "sAC0", "sAC1", "sIN", "sRT"]:
        ctx.sem(s)

    dt_in = {}

    def din(name, shape, dtype=F32):
        dt_in[name] = nc.dram_tensor(name, shape, dtype, kind="ExternalInput").ap()
        return dt_in[name]

    xT = din("xT", [D, T])
    w1cat = din("w1cat", [D, 512])
    w2pair = [din(f"w2pair{i}", [128, D]) for i in range(4)]
    wacat = din("wacat", [D, 3 * RS])
    bdcat = din("bdcat", [D, 51])
    woT = din("woT", [D, D], BF16)
    wg = din("wg", [D, 8], BF16)
    w1 = din("w1", [D, 4096], BF16)
    w2 = din("w2", [4096, D], BF16)
    l1w = [din(f"l1w{i}", [D, D], BF16) for i in range(4)]
    woutT = din("woutT", [D, D], BF16)
    log_a = din("log_a", [128, NC_])
    ln_g = din("ln_g", [128, NC_])
    ln_b = din("ln_b", [128, NC_])
    router = din("router", [D, 8])
    guw = din("guw", [8, D, 8192], BF16)
    dnw = din("dnw", [8, 4096, D], BF16)
    sguw = din("sguw", [D, 2048], BF16)
    sdnw = din("sdnw", [D, D], BF16)

    y_out = nc.dram_tensor("y", [D, T], F32, kind="ExternalOutput").ap()

    d_spre = nc.dram_tensor("d_spre", [D, T], BF16, kind="Internal").ap()
    d_x1 = nc.dram_tensor("d_x1", [D, T], F32, kind="Internal").ap()
    d_s1 = nc.dram_tensor("d_s1", [D, T], BF16, kind="Internal").ap()
    d_recipn = nc.dram_tensor("d_recipn", [1, T], F32, kind="Internal").ap()
    d_xnl = nc.dram_tensor("d_xnl", [T + 1, D], BF16, kind="Internal").ap()
    d_acc = nc.dram_tensor("d_acc", [T + 1, D], F32, kind="Internal").ap()

    ident_np = np.eye(128, dtype=BF16NP)
    ident_t = nc.inline_tensor(ident_np, name="identbf")
    ones_f_t = nc.inline_tensor(np.ones((128, 1), np.float32), name="onesf")
    ones_b_t = nc.inline_tensor(np.ones((128, 1), BF16NP), name="onesb")
    onesrow_f_t = nc.inline_tensor(np.ones((1, 128), np.float32), name="onesrowf")
    cvals_np = np.broadcast_to(
        np.array([1e-5, 1e-8, 1e-4, 10.0], np.float32), (128, 4)).copy()
    cvals_t = nc.inline_tensor(cvals_np, name="cvals_src")
    identf_t = nc.inline_tensor(np.eye(128, dtype=np.float32), name="identf_src")
    utf_t = nc.inline_tensor(
        np.triu(np.ones((128, 128), np.float32)).T.copy() if False else
        (np.arange(128)[:, None] <= np.arange(128)[None, :]).astype(np.float32),
        name="utf_src")
    idsc_t = nc.inline_tensor(
        (np.arange(16)[None, :] * 128 + np.arange(128)[:, None])
        .astype(np.float32).copy(), name="idsc_src")
    iot_t = nc.inline_tensor(
        np.broadcast_to(np.arange(128, dtype=np.float32), (128, 128)).copy(),
        name="iot_src")
    zrow_t = nc.inline_tensor(np.zeros((1, 1024), BF16NP), name="zrow_src")
    from concourse.bass import IndirectOffsetOnAxis

    tap_bufs = {}
    dbg_outs = {}
    for tname, shape, dtype in taps:
        dbg_outs[tname] = nc.dram_tensor("dbg_" + tname, shape, dtype,
                                         kind="ExternalOutput").ap()

    with contextlib.ExitStack() as st_top:
        sb = lambda st, name, shape, dt: st.enter_context(
            nc.sbuf_tensor(name, shape, dt))
        ps = lambda st, name, shape: st.enter_context(
            nc.psum_tensor(name, shape, F32))
        ident = sb(st_top, "ident", [128, 128], BF16)
        ones_f = sb(st_top, "ones_f", [128, 1], F32)
        ones_b = sb(st_top, "ones_b", [128, 1], BF16)
        onesrow_f = sb(st_top, "onesrow_f", [1, 128], F32)
        cvals = sb(st_top, "cvals", [128, 4], F32)
        nrm = sb(st_top, "nrm", [128, NC_ * T], BF16)
        identf = sb(st_top, "identf", [128, 128], F32)
        with contextlib.ExitStack() as st0:
            xfull = sb(st0, "xfull", [128, NC_ * T], F32)
            sqf = sb(st0, "sqf", [128, NC_ * T], BF16)
            arow = sb(st0, "arow", [1, T], F32)
            brow = sb(st0, "brow", [1, T], F32)
            stat = sb(st0, "stat", [1, 4 * SC], F32)
            ab_sb = sb(st0, "ab_sb", [128, SC], F32)
            bb_sb = sb(st0, "bb_sb", [128, SC], F32)
            xnt = sb(st0, "xnt", [128, 2 * SC], F32)
            ps_sum = ps(st0, "ps_sum", [128, 512])
            ps_sq = ps(st0, "ps_sq", [128, 512])
            ps_a = ps(st0, "ps_a", [128, 512])
            ps_b = ps(st0, "ps_b", [128, 512])
            def b0_sp(c, eng):
                c.dma(eng, ident[:, :], ident_t.ap(), "sSP")
                c.dma(eng, ones_f[:, :], ones_f_t.ap(), "sSP")
                c.dma(eng, ones_b[:, :], ones_b_t.ap(), "sSP")
                c.dma(eng, onesrow_f[:, :], onesrow_f_t.ap(), "sSP")
                c.dma(eng, cvals[:, :], cvals_t.ap(), "sSP")
                for dc in range(NC_):
                    c.dma(eng, xfull[:, dc * T:(dc + 1) * T],
                          xT[dc * 128:(dc + 1) * 128, :], "sSP",
                          mark=("b0:all" if dc == NC_ - 1 else None))

            def b0_act(c, eng):
                c.wait(eng, "sSP", "b0:all")
                for dc in range(NC_):
                    c.op(eng, lambda e, dc=dc: e.activation(
                        sqf[:, dc * T:(dc + 1) * T], xfull[:, dc * T:(dc + 1) * T],
                        AF.Square), inc="sACT", mark=f"sq{dc}")
                for sc in range(NSC):
                    c.wait(eng, "sDVE", f"var{sc}")
                    c.op(eng, lambda e, sc=sc: e.activation(
                        stat[0:1, 3 * SC:4 * SC], stat[0:1, 2 * SC:3 * SC],
                        AF.Sqrt, bias=cvals[0:1, 0:1]), inc="sACT", mark=f"sqrt{sc}")

            def b0_pe(c, eng):
                for sc in range(NSC):
                    s0 = sc * SC
                    if sc > 0:
                        c.wait(eng, "sDVE", f"mean{sc - 1}")
                        c.wait(eng, "sDVE", f"var{sc - 1}")
                        c.wait(eng, "sDVE", f"absb{sc - 1}")
                    c.wait(eng, "sSP", "b0:all")
                    for dc in range(NC_):
                        c.op(eng, lambda e, dc=dc, s0=s0: e.matmul(
                            ps_sum[0:1, 0:SC], ones_f[:, 0:1],
                            xfull[:, dc * T + s0: dc * T + s0 + SC],
                            start=(dc == 0), stop=(dc == NC_ - 1)),
                            inc="sPE", mark=(f"sum{sc}" if dc == NC_ - 1 else None))
                    for dc in range(NC_):
                        c.wait(eng, "sACT", f"sq{dc}")
                        c.op(eng, lambda e, dc=dc, s0=s0: e.matmul(
                            ps_sq[0:1, 0:SC], ones_b[:, 0:1],
                            sqf[:, dc * T + s0: dc * T + s0 + SC],
                            start=(dc == 0), stop=(dc == NC_ - 1)),
                            inc="sPE", mark=(f"sumsq{sc}" if dc == NC_ - 1 else None))
                    c.wait(eng, "sDVE", f"ab{sc}")
                    c.op(eng, lambda e, s0=s0: e.matmul(
                        ps_a[:, 0:SC], onesrow_f[0:1, :], arow[0:1, s0:s0 + SC],
                        start=True, stop=True), inc="sPE")
                    c.op(eng, lambda e, s0=s0: e.matmul(
                        ps_b[:, 0:SC], onesrow_f[0:1, :], brow[0:1, s0:s0 + SC],
                        start=True, stop=True), inc="sPE", mark=f"bc{sc}")

            def b0_dve(c, eng):
                for sc in range(NSC):
                    s0 = sc * SC
                    c.wait(eng, "sPE", f"sum{sc}")
                    c.op(eng, lambda e: e.tensor_scalar(
                        stat[0:1, 0:SC], ps_sum[0:1, 0:SC], 1.0 / D, None,
                        ALU.mult), inc="sDVE", mark=f"mean{sc}")
                    c.wait(eng, "sDVE", f"mean{sc}")
                    c.op(eng, lambda e: e.tensor_tensor(
                        stat[0:1, SC:2 * SC], stat[0:1, 0:SC], stat[0:1, 0:SC],
                        ALU.mult), inc="sDVE", mark=f"m2_{sc}")
                    c.wait(eng, "sPE", f"sumsq{sc}")
                    c.wait(eng, "sDVE", f"m2_{sc}")
                    c.op(eng, lambda e: e.scalar_tensor_tensor(
                        stat[0:1, 2 * SC:3 * SC], ps_sq[0:1, 0:SC], 1.0 / D,
                        stat[0:1, SC:2 * SC], ALU.mult, ALU.subtract), inc="sDVE",
                        mark=f"var{sc}")
                    c.wait(eng, "sACT", f"sqrt{sc}")
                    c.op(eng, lambda e, s0=s0: e.reciprocal(
                        arow[0:1, s0:s0 + SC], stat[0:1, 3 * SC:4 * SC]),
                        inc="sDVE", mark=f"r{sc}")
                    c.wait(eng, "sDVE", f"r{sc}")
                    c.op(eng, lambda e, s0=s0: e.scalar_tensor_tensor(
                        brow[0:1, s0:s0 + SC], stat[0:1, 0:SC], -1.0,
                        arow[0:1, s0:s0 + SC], ALU.mult, ALU.mult), inc="sDVE",
                        mark=f"ab{sc}")
                    c.wait(eng, "sPE", f"bc{sc}")
                    c.op(eng, lambda e: e.tensor_copy(ab_sb[:, 0:SC], ps_a[:, 0:SC]),
                         inc="sDVE")
                    c.op(eng, lambda e: e.tensor_copy(bb_sb[:, 0:SC], ps_b[:, 0:SC]),
                         inc="sDVE", mark=f"absb{sc}")
                    c.wait(eng, "sDVE", f"absb{sc}")
                    for dc in range(NC_):
                        o = dc * T + s0
                        p = (dc % 2) * SC
                        c.op(eng, lambda e, o=o, p=p: e.tensor_tensor(
                            xnt[:, p:p + SC], xfull[:, o:o + SC], ab_sb[:, 0:SC],
                            ALU.mult), inc="sDVE", mark=f"xm{sc}_{dc}")
                        c.wait(eng, "sDVE", f"xm{sc}_{dc}")
                        c.op(eng, lambda e, o=o, p=p: e.tensor_tensor(
                            nrm[:, o:o + SC], xnt[:, p:p + SC], bb_sb[:, 0:SC],
                            ALU.add), inc="sDVE", mark=f"xn{sc}_{dc}")

            ctx.run_block([
                ("sync", b0_sp),
                ("scalar", b0_act),
                ("tensor", b0_pe),
                ("vector", b0_dve),
            ])

            if "xn" in dbg_outs:
                def tdma(c, eng):
                    c.dma(eng, dbg_outs["xn"], nrm[:, :], "sPL", mark="tapxn")
                    c.wait(eng, "sPL", "tapxn")
                ctx.run_block([("gpsimd", tdma)])

        if "xn" not in dbg_outs:
          with contextlib.ExitStack() as stL0:
            p1_all = sb(stL0, "p1_all", [128, 4 * T], BF16)
            kb = sb(stL0, "kb", [128, NC_ * T], BF16)
            ksb = sb(stL0, "ksb", [128, NC_ * T], BF16)
            v_tm = sb(stL0, "v_tm", [128, 16 * D], BF16)
            w2p = sb(stL0, "w2p", [128, 4 * D], BF16)
            apack = sb(stL0, "apack", [128, T], BF16)
            bm_tm = sb(stL0, "bm_tm", [128, 16 * 48], BF16)
            d_tm = sb(stL0, "d_tm", [128, 16 * 3], F32)
            zbp = sb(stL0, "zbp", [128, WI], BF16)
            zbufT = sb(stL0, "zbufT", [128, 16 * WI], BF16)
            zbufT2 = sb(stL0, "zbufT2", [128, 16 * WI], BF16)
            pbuf = sb(stL0, "pbuf", [128, NC_ * WI], BF16)
            scr = sb(stL0, "scr", [128, 2048], F32)
            sqb = sb(stL0, "sqb", [128, 8 * WI], BF16)
            spre_t = sb(stL0, "spre_t", [128, 2 * WI], BF16)
            nstat = sb(stL0, "nstat", [1, 3 * WI], F32)
            spb = sb(stL0, "spb", [128, 4 * WI], BF16)
            spb2 = sb(stL0, "spb2", [128, 4 * WI], BF16)
            P0 = ps(stL0, "P0", [128, 512])
            P1 = ps(stL0, "P1", [128, 512])
            P2 = ps(stL0, "P2", [128, 512])
            P3 = ps(stL0, "P3", [128, 512])
            P4 = ps(stL0, "P4", [128, 512])
            P5 = ps(stL0, "P5", [128, 512])
            P6 = ps(stL0, "P6", [128, 512])
            P7 = ps(stL0, "P7", [128, 512])
            if True:
                w1b = v_tm
                wab = pbuf
                BDO = 512
                def b1_sp(c, eng):
                    pieces = []
                    for i in range(4):
                        pieces.append([(w1cat[(2 * i) * 128:(2 * i + 1) * 128, :], 0, 512),
                                       (w1cat[(2 * i + 1) * 128:(2 * i + 2) * 128, :], 512, 512)])
                    for p in range(4):
                        pieces.append([(w2pair[p], 0, 1024)])
                    pieces.append([(wacat[dc * 128:(dc + 1) * 128, :], dc * 48, 48)
                                   for dc in range(NC_)])
                    pieces.append([(bdcat[dc * 128:(dc + 1) * 128, :], dc * 51, 51)
                                   for dc in range(NC_)])
                    for i, dmas in enumerate(pieces):
                        h = (i % 2) * 1024
                        lane = f"sLD{i % 2}"
                        if i >= 2:
                            c.wait(eng, "sDVE", f"b1:cv{i - 2}")
                        for src_ap, off, w in dmas:
                            c.dma(eng, scr[:, h + off:h + off + w], src_ap,
                                  lane, mark=f"b1:ld{i}")

                def b1_cv(c, eng):
                    dsts = [
                        (w1b, 0, 1024), (w1b, 1024, 1024),
                        (w1b, 2048, 1024), (w1b, 3072, 1024),
                        (w2p, 0, 1024), (w2p, 1024, 1024),
                        (w2p, 2048, 1024), (w2p, 3072, 1024),
                        (wab, 0, 384), (pbuf, BDO, 408),
                    ]
                    for i, (dst, off, w) in enumerate(dsts):
                        h = (i % 2) * 1024
                        c.wait(eng, f"sLD{i % 2}", f"b1:ld{i}")
                        c.op(eng, lambda e, dst=dst, off=off, w=w, h=h:
                             e.tensor_copy(dst[:, off:off + w], scr[:, h:h + w]),
                             inc="sDVE", mark=f"b1:cv{i}")
                    if True:
                        c.mark("b1:w1b", "sDVE")
                        if not c.emitting:
                            c.marks["b1:w1b"] = c.marks["b1:cv3"]
                            c.marks["b1:w2p"] = c.marks["b1:cv7"]
                            c.marks["b1:wab"] = c.marks["b1:cv8"]
                            c.marks["b1:bdb"] = c.marks["b1:cv9"]

                def b1_pe(c, eng):
                    c.wait(eng, "sDVE", "b1:w1b")
                    gi = 0
                    for mc in range(4):
                        for sc in range(NSC):
                            pb = [P0, P1][gi % 2]
                            if gi >= 2:
                                c.wait(eng, "sDVE", f"b1:p1e{gi - 2}")
                            for dc in range(NC_):
                                c.op(eng, lambda e, pb=pb, mc=mc, sc=sc, dc=dc:
                                     e.matmul(
                                     pb[:, 0:512],
                                     w1b[:, dc * 512 + mc * 128: dc * 512 + (mc + 1) * 128],
                                     nrm[:, dc * T + sc * 512: dc * T + (sc + 1) * 512],
                                     start=(dc == 0), stop=(dc == NC_ - 1)),
                                     inc="sPE",
                                     mark=(f"b1:p1m{gi}" if dc == NC_ - 1 else None))
                            gi += 1
                    c.wait(eng, "sDVE", "b1:w2p")
                    c.wait(eng, "sDVE", "b1:p1e7")
                    gi = 0
                    for half in range(2):
                        r0 = half * 64
                        for oc in range(NC_):
                            for sc in range(NSC):
                                pb = [P2, P3][gi % 2]
                                if gi >= 2:
                                    c.wait(eng, "sDVE", f"b1:ke{gi - 2}")
                                c.op(eng, lambda e, pb=pb, r0=r0, oc=oc, sc=sc:
                                     e.matmul(
                                     pb[:, 0:512],
                                     w2p[r0:r0 + 64, D + oc * 128: D + (oc + 1) * 128],
                                     p1_all[r0:r0 + 64, T + sc * 512: T + (sc + 1) * 512],
                                     start=True, stop=True),
                                     inc="sPE", mark=f"b1:km{gi}")
                                gi += 1
                    c.wait(eng, "sDVE", "b1:p1e15")
                    gi = 64
                    for jc in range(16):
                        for half in range(2):
                            pb = [P2, P3][gi % 2]
                            c.wait(eng, "sDVE", f"b1:ke{gi - 2}")
                            c.op(eng, lambda e, pb=pb, jc=jc, half=half: e.matmul(
                                 pb[:, 0:512],
                                 p1_all[0:64, 3 * T + jc * 128: 3 * T + (jc + 1) * 128],
                                 w2p[0:64, 3 * D + half * 512: 3 * D + (half + 1) * 512],
                                 start=True, stop=True),
                                 inc="sPE", mark=f"b1:km{gi}")
                            gi += 1
                    c.wait(eng, "sDVE", "b1:wab")
                    gi = 0
                    for path in range(3):
                        base = [0, 32, 64][path]
                        for sc in range(NSC):
                            pb = [P4, P5][gi % 2]
                            if gi >= 2:
                                c.wait(eng, "sDVE", f"b1:ae{gi - 2}")
                            for dc in range(NC_):
                                c.op(eng, lambda e, pb=pb, base=base, path=path,
                                     sc=sc, dc=dc: e.matmul(
                                     pb[base:base + 16, 0:512],
                                     wab[:, dc * 48 + path * 16: dc * 48 + (path + 1) * 16],
                                     nrm[:, dc * T + sc * 512: dc * T + (sc + 1) * 512],
                                     start=(dc == 0), stop=(dc == NC_ - 1)),
                                     inc="sPE",
                                     mark=(f"b1:am{gi}" if dc == NC_ - 1 else None))
                            gi += 1
                    c.wait(eng, "sDVE", "b1:bdb")
                    for jc in range(16):
                        pb = [P6, P7][jc % 2]
                        if jc >= 2:
                            c.wait(eng, "sDVE", f"b1:bde{jc - 2}")
                        for dc in range(NC_):
                            c.op(eng, lambda e, pb=pb, jc=jc, dc=dc: e.matmul(
                                 pb[:, 0:51],
                                 nrm[:, dc * T + jc * 128: dc * T + (jc + 1) * 128],
                                 pbuf[:, BDO + dc * 51: BDO + dc * 51 + 51],
                                 start=(dc == 0), stop=(dc == NC_ - 1)),
                                 inc="sPE",
                                 mark=(f"b1:bdm{jc}" if dc == NC_ - 1 else None))

                def b1_dve(c, eng):
                    gi = 0
                    for mc in range(4):
                        for sc in range(NSC):
                            pb = [P0, P1][gi % 2]
                            c.wait(eng, "sPE", f"b1:p1m{gi}")
                            c.op(eng, lambda e, pb=pb, mc=mc, sc=sc: e.tensor_copy(
                                 p1_all[:, mc * T + sc * 512: mc * T + (sc + 1) * 512],
                                 pb[:, 0:512]), inc="sDVE", mark=f"b1:p1e{gi}")
                            gi += 1
                    gi = 0
                    for half in range(2):
                        dst = [kb, ksb][half]
                        for oc in range(NC_):
                            for sc in range(NSC):
                                pb = [P2, P3][gi % 2]
                                c.wait(eng, "sPE", f"b1:km{gi}")
                                c.op(eng, lambda e, pb=pb, dst=dst, oc=oc, sc=sc:
                                     e.tensor_copy(
                                     dst[:, oc * T + sc * 512: oc * T + (sc + 1) * 512],
                                     pb[:, 0:512]), inc="sDVE", mark=f"b1:ke{gi}")
                                gi += 1
                    for jc in range(16):
                        for half in range(2):
                            pb = [P2, P3][gi % 2]
                            c.wait(eng, "sPE", f"b1:km{gi}")
                            c.op(eng, lambda e, pb=pb, jc=jc, half=half:
                                 e.tensor_copy(
                                 v_tm[:, jc * D + half * 512: jc * D + (half + 1) * 512],
                                 pb[:, 0:512]), inc="sDVE", mark=f"b1:ke{gi}")
                            gi += 1
                    gi = 0
                    for path in range(3):
                        base = [0, 32, 64][path]
                        for sc in range(NSC):
                            pb = [P4, P5][gi % 2]
                            c.wait(eng, "sPE", f"b1:am{gi}")
                            c.op(eng, lambda e, pb=pb, base=base, sc=sc:
                                 e.tensor_copy(
                                 apack[base:base + 16, sc * 512:(sc + 1) * 512],
                                 pb[base:base + 16, 0:512]), inc="sDVE",
                                 mark=f"b1:ae{gi}")
                            gi += 1
                    for jc in range(16):
                        pb = [P6, P7][jc % 2]
                        c.wait(eng, "sPE", f"b1:bdm{jc}")
                        c.op(eng, lambda e, pb=pb, jc=jc: e.tensor_copy(
                             bm_tm[:, jc * 48:(jc + 1) * 48], pb[:, 0:48]),
                             inc="sDVE")
                        c.op(eng, lambda e, pb=pb, jc=jc: e.tensor_copy(
                             d_tm[:, jc * 3:(jc + 1) * 3], pb[:, 48:51]),
                             inc="sDVE", mark=f"b1:bde{jc}")

                def b1_vec(c, eng):
                    b1_cv(c, eng)
                    b1_dve(c, eng)

                ctx.run_block([
                    ("sync", b1_sp),
                    ("vector", b1_vec),
                    ("tensor", b1_pe),
                ])

            if "kb" in dbg_outs:
                def tdmab1(c, eng):
                    c.dma(eng, dbg_outs["kb"], kb[:, :], "sPL", mark="t1")
                    c.dma(eng, dbg_outs["ksb"], ksb[:, :], "sPL", mark="t2")
                    c.dma(eng, dbg_outs["v_tm"], v_tm[:, :], "sPL", mark="t3")
                    c.dma(eng, dbg_outs["p1_all"], p1_all[:, :], "sPL", mark="t4")
                    c.dma(eng, dbg_outs["apack"], apack[:, :], "sPL", mark="t5")
                    c.dma(eng, dbg_outs["bm_tm"], bm_tm[:, :], "sPL", mark="t6")
                    c.dma(eng, dbg_outs["d_tm"], d_tm[:, :], "sPL", mark="t7")
                    c.wait(eng, "sPL", "t7")
                ctx.run_block([("gpsimd", tdmab1)])

            def b2_pe(c, eng):
                if B2_LVL < 1:
                    return
                for ib in range(B2_NIB):
                    i0 = ib * WI
                    for path in range(B2_PATHS):
                        mc = [2, 0][path]
                        pr = ["s", "m"][path]
                        kbuf = [ksb, kb][path]
                        zdst = [zbufT2, zbufT][path]
                        if path == 0 and ib > 0:
                            if B2_LVL >= 6:
                                c.wait(eng, "sDVE", f"{ib - 1}:spre{NC_ - 1}")
                                c.wait(eng, "sACT", f"{ib - 1}:mvoe{NC_ - 1}")
                            else:
                                lm = {1: ("sDVE", f"{ib - 1}:mpb{NC_ - 1}"),
                                      2: ("sACT", f"{ib - 1}:mze15"),
                                      3: ("sDVE", f"{ib - 1}:mh15"),
                                      4: ("sDVE", f"{ib - 1}:nrec"),
                                      5: ("sACT", f"{ib - 1}:mvoe{NC_ - 1}")}[B2_LVL]
                                c.wait(eng, lm[0], lm[1])
                        if path == 1:
                            if B2_LVL >= 2:
                                c.wait(eng, "sACT", f"{ib}:sze15")
                            else:
                                c.wait(eng, "sDVE", f"{ib}:spb{NC_ - 1}")
                        for oc in range(NC_):
                            if oc >= 2:
                                c.wait(eng, "sDVE", f"{ib}:{pr}pb{oc - 2}")
                            pq = [P0, P1][oc % 2]
                            pm = [P2, P3][oc % 2]
                            c.op(eng, lambda e, pq=pq, mc=mc, oc=oc, i0=i0:
                                 e.matmul(
                                 pq[:, 0:WI],
                                 w2p[0:64, mc * D + oc * 128: mc * D + (oc + 1) * 128],
                                 p1_all[0:64, mc * T + i0: mc * T + i0 + WI],
                                 start=True, stop=True), inc="sPE",
                                 mark=f"{ib}:{pr}q{oc}")
                            c.op(eng, lambda e, pm=pm, mc=mc, oc=oc, i0=i0:
                                 e.matmul(
                                 pm[:, 0:WI],
                                 w2p[64:128, mc * D + oc * 128: mc * D + (oc + 1) * 128],
                                 p1_all[64:128, mc * T + i0: mc * T + i0 + WI],
                                 start=True, stop=True), inc="sPE",
                                 mark=f"{ib}:{pr}g{oc}")
                        if B2_LVL < 2:
                            continue
                        c.wait(eng, "sDVE", f"{ib}:{pr}pb{NC_ - 1}")
                        for jc in range(16):
                            pz = [P0, P1][jc % 2]
                            if jc >= 2:
                                c.wait(eng, "sACT", f"{ib}:{pr}ze{jc - 2}")
                            for dc in range(NC_):
                                c.op(eng, lambda e, pz=pz, kbuf=kbuf, jc=jc, dc=dc:
                                     e.matmul(
                                     pz[:, 0:WI],
                                     kbuf[:, dc * T + jc * 128: dc * T + (jc + 1) * 128],
                                     pbuf[:, dc * WI:(dc + 1) * WI],
                                     start=(dc == 0), stop=(dc == NC_ - 1)),
                                     inc="sPE",
                                     mark=(f"{ib}:{pr}z{jc}" if dc == NC_ - 1 else None))
                        if B2_LVL < 3:
                            continue
                        abase = [32, 0][path]
                        bsel = [1, 0][path]
                        c.wait(eng, "sACT", f"{ib}:{pr}ze15")
                        for jc in range(16):
                            c.op(eng, lambda e, jc=jc, bsel=bsel, zdst=zdst,
                                 abase=abase: e.matmul(
                                 P4[abase:abase + 16, 0:WI],
                                 bm_tm[:, jc * 48 + bsel * 16: jc * 48 + (bsel + 1) * 16],
                                 zdst[:, jc * WI:(jc + 1) * WI],
                                 start=(jc == 0), stop=(jc == 15)),
                                 inc="sPE",
                                 mark=(f"{ib}:{pr}zbm" if jc == 15 else None))
                        c.wait(eng, "sACT", f"{ib}:{pr}zbe")

                        def nacc_emit(jcn, ib=ib):
                            c.wait(eng, "sACT", f"{ib}:nsq{jcn}")
                            c.op(eng, lambda e, jcn=jcn: e.matmul(
                                 P7[0:1, 0:WI], ones_b[:, 0:1],
                                 sqb[:, (jcn % 8) * WI:(jcn % 8 + 1) * WI],
                                 start=(jcn == 0), stop=(jcn == 15)),
                                 inc="sPE", mark=f"{ib}:nacc{jcn}")
                        for jc in range(16):
                            pa = [P5, P6][jc % 2]
                            if jc >= 2:
                                c.wait(eng, "sDVE", f"{ib}:{pr}h{jc - 2}")
                            c.op(eng, lambda e, pa=pa, abase=abase, jc=jc:
                                 e.matmul(
                                 pa[:, 0:WI],
                                 apack[abase:abase + 16, jc * 128:(jc + 1) * 128],
                                 zbp[abase:abase + 16, 0:WI],
                                 start=True, stop=True), inc="sPE",
                                 mark=f"{ib}:{pr}at{jc}")
                            if path == 1 and B2_LVL >= 4 and jc >= 8:
                                nacc_emit(jc - 8)
                        if path == 1 and B2_LVL >= 4:
                            for jcn in range(8, 16):
                                nacc_emit(jcn)
                    if B2_LVL < 5:
                        continue
                    c.wait(eng, "sDVE", f"{ib}:h15")
                    c.wait(eng, "sACT", f"{ib}:mzbe")
                    for jc in range(16):
                        c.op(eng, lambda e, jc=jc: e.matmul(
                             P4[64:80, 0:WI],
                             bm_tm[:, jc * 48 + 32: jc * 48 + 48],
                             zbufT[:, jc * WI:(jc + 1) * WI],
                             start=(jc == 0), stop=(jc == 15)),
                             inc="sPE", mark=(f"{ib}:zb2m" if jc == 15 else None))
                    c.wait(eng, "sACT", f"{ib}:zb2e")
                    for jc in range(16):
                        pa = [P5, P6][jc % 2]
                        if jc >= 2:
                            c.wait(eng, "sDVE", f"{ib}:ro{jc - 2}")
                        c.op(eng, lambda e, pa=pa, jc=jc: e.matmul(
                             pa[:, 0:WI],
                             apack[64:80, jc * 128:(jc + 1) * 128],
                             zbp[64:80, 0:WI],
                             start=True, stop=True), inc="sPE",
                             mark=f"{ib}:rat{jc}")
                    for oc in range(NC_):
                        if oc >= 2:
                            c.wait(eng, "sACT", f"{ib}:mvoe{oc - 2}")
                        pq = [P0, P1][oc % 2]
                        c.op(eng, lambda e, pq=pq, oc=oc, i0=i0: e.matmul(
                             pq[:, 0:WI],
                             w2p[64:128, 3 * D + oc * 128: 3 * D + (oc + 1) * 128],
                             p1_all[64:128, 3 * T + i0: 3 * T + i0 + WI],
                             start=True, stop=True), inc="sPE",
                             mark=f"{ib}:mvo{oc}")
                    if B2_LVL < 6:
                        continue
                    c.wait(eng, "sDVE", f"{ib}:ro15")
                    for dc in range(NC_):
                        pz = [P2, P3][dc % 2]
                        if dc >= 2:
                            c.wait(eng, "sDVE", f"{ib}:spre{dc - 2}")
                        for jc in range(16):
                            c.op(eng, lambda e, pz=pz, dc=dc, jc=jc: e.matmul(
                                 pz[:, 0:WI],
                                 v_tm[:, jc * D + dc * 128: jc * D + (dc + 1) * 128],
                                 zbufT2[:, jc * WI:(jc + 1) * WI],
                                 start=(jc == 0), stop=(jc == 15)),
                                 inc="sPE",
                                 mark=(f"{ib}:ctx{dc}" if jc == 15 else None))

            def b2_dve(c, eng):
                if B2_LVL < 1:
                    return
                for ib in range(B2_NIB):
                    i0 = ib * WI
                    for path in range(B2_PATHS):
                        pr = ["s", "m"][path]
                        zdst = [zbufT2, zbufT][path]
                        dcol = [1, 0][path]
                        for oc in range(NC_):
                            pq = [P0, P1][oc % 2]
                            pm = [P2, P3][oc % 2]
                            so = (oc % 2) * WI
                            c.wait(eng, "sPE", f"{ib}:{pr}q{oc}")
                            c.op(eng, lambda e, pq=pq, so=so: e.tensor_copy(
                                 scr[:, so:so + WI], pq[:, 0:WI]), inc="sDVE",
                                 mark=f"{ib}:{pr}qc{oc}")
                            c.wait(eng, "sPE", f"{ib}:{pr}g{oc}")
                            c.wait(eng, "sDVE", f"{ib}:{pr}qc{oc}")
                            c.op(eng, lambda e, pm=pm, so=so, oc=oc: e.tensor_tensor(
                                 pbuf[:, oc * WI:(oc + 1) * WI], scr[:, so:so + WI],
                                 pm[:, 0:WI], ALU.mult), inc="sDVE",
                                 mark=f"{ib}:{pr}pb{oc}")
                        if B2_LVL < 3:
                            continue
                        for jc in range(16):
                            pa = [P5, P6][jc % 2]
                            c.wait(eng, "sPE", f"{ib}:{pr}at{jc}")
                            c.op(eng, lambda e, pa=pa, jc=jc, zdst=zdst, dcol=dcol:
                                 e.scalar_tensor_tensor(
                                 zdst[:, jc * WI:(jc + 1) * WI],
                                 zdst[:, jc * WI:(jc + 1) * WI],
                                 d_tm[:, jc * 3 + dcol: jc * 3 + dcol + 1],
                                 pa[:, 0:WI], ALU.mult, ALU.add), inc="sDVE",
                                 mark=f"{ib}:{pr}h{jc}")
                        if B2_LVL < 4:
                            continue
                        if path == 0:
                            pass
                        else:
                            for jc in range(16):
                                c.wait(eng, "sDVE", f"{ib}:mh{jc}")
                                c.wait(eng, "sACT", f"{ib}:nsq{jc}")
                                c.wait(eng, "sPOOL", f"sp_sp{ib * 16 + jc}")
                                c.op(eng, lambda e, jc=jc: e.scalar_tensor_tensor(
                                     zbufT[:, jc * WI:(jc + 1) * WI],
                                     zbufT[:, jc * WI:(jc + 1) * WI], 0.0,
                                     zbufT2[:, jc * WI:(jc + 1) * WI],
                                     ALU.max, ALU.mult), inc="sDVE",
                                     mark=f"{ib}:h{jc}")
                            c.wait(eng, "sACT", f"{ib}:nsqrt")
                            c.op(eng, lambda e: e.tensor_scalar(
                                 nstat[0:1, WI:2 * WI], nstat[0:1, 0:WI], 1e-8,
                                 None, ALU.add), inc="sDVE", mark=f"{ib}:nadd")
                            c.wait(eng, "sDVE", f"{ib}:nadd")
                            if ib > 0:
                                c.wait(eng, "sPLr", f"recdma{ib - 1}")
                            c.op(eng, lambda e: e.reciprocal(
                                 nstat[0:1, 2 * WI:3 * WI], nstat[0:1, WI:2 * WI]),
                                 inc="sDVE", mark=f"{ib}:nrec")
                    if B2_LVL < 5:
                        continue
                    for jc in range(16):
                        pa = [P5, P6][jc % 2]
                        c.wait(eng, "sPE", f"{ib}:rat{jc}")
                        c.op(eng, lambda e, pa=pa, jc=jc: e.scalar_tensor_tensor(
                             zbufT2[:, jc * WI:(jc + 1) * WI],
                             zbufT[:, jc * WI:(jc + 1) * WI],
                             d_tm[:, jc * 3 + 2: jc * 3 + 3],
                             pa[:, 0:WI], ALU.mult, ALU.add), inc="sDVE",
                             mark=f"{ib}:ro{jc}")
                    if B2_LVL < 6:
                        continue
                    for dc in range(NC_):
                        pz = [P2, P3][dc % 2]
                        gdc = ib * NC_ + dc
                        c.wait(eng, "sACT", f"{ib}:mvoe{dc}")
                        c.wait(eng, "sPE", f"{ib}:ctx{dc}")
                        if gdc >= 2:
                            c.wait(eng, f"sST{gdc % 2}", f"spdma{gdc - 2}")
                        c.op(eng, lambda e, pz=pz, dc=dc: e.tensor_tensor(
                             spre_t[:, (dc % 2) * WI:(dc % 2 + 1) * WI],
                             pz[:, 0:WI], pbuf[:, dc * WI:(dc + 1) * WI],
                             ALU.mult), inc="sDVE", mark=f"{ib}:spre{dc}")

            def b2_act(c, eng):
                if B2_LVL < 2:
                    return
                for ib in range(B2_NIB):
                    for path in range(B2_PATHS):
                        pr = ["s", "m"][path]
                        zdst = [zbufT2, zbufT][path]
                        for jc in range(16):
                            pz = [P0, P1][jc % 2]
                            c.wait(eng, "sPE", f"{ib}:{pr}z{jc}")
                            c.op(eng, lambda e, pz=pz, jc=jc, zdst=zdst:
                                 e.copy(zdst[:, jc * WI:(jc + 1) * WI],
                                        pz[:, 0:WI]),
                                 inc="sACT", mark=f"{ib}:{pr}ze{jc}")
                        if B2_LVL < 3:
                            continue
                        abase2 = [32, 0][path]
                        c.wait(eng, "sPE", f"{ib}:{pr}zbm")
                        c.op(eng, lambda e, abase2=abase2: e.copy(
                             zbp[abase2:abase2 + 16, 0:WI],
                             P4[abase2:abase2 + 16, 0:WI]),
                             inc="sACT", mark=f"{ib}:{pr}zbe")
                        if B2_LVL < 4:
                            continue
                        if path == 0:
                            for jc in range(16):
                                seq = ib * 16 + jc
                                o = (jc % 2) * 2 * WI
                                c.wait(eng, "sDVE", f"{ib}:sh{jc}")
                                if seq > 1:
                                    c.wait(eng, "sPOOL", f"sp_sp{seq - 2}")
                                c.op(eng, lambda e, jc=jc, o=o: e.activation(
                                     spb[:, o:o + WI],
                                     zbufT2[:, jc * WI:(jc + 1) * WI],
                                     AF.Abs), inc="sACT", mark=f"sp_u{seq}")
                                c.op(eng, lambda e, jc=jc, o=o: e.activation(
                                     spb[:, o + WI:o + 2 * WI],
                                     zbufT2[:, jc * WI:(jc + 1) * WI],
                                     AF.Relu), inc="sACT", mark=f"sp_m{seq}")
                                c.wait(eng, "sACT", f"sp_u{seq}")
                                c.op(eng, lambda e, o=o: e.activation(
                                     spb2[:, o:o + WI], spb[:, o:o + WI], AF.Exp,
                                     scale=-1.0), inc="sACT", mark=f"sp_e{seq}")
                                c.wait(eng, "sACT", f"sp_e{seq}")
                                c.op(eng, lambda e, o=o: e.activation(
                                     spb2[:, o + WI:o + 2 * WI], spb2[:, o:o + WI],
                                     AF.Ln, bias=1.0), inc="sACT",
                                     mark=f"sp_l{seq}")
                        else:
                            for jc in range(16):
                                c.wait(eng, "sDVE", f"{ib}:mh{jc}")
                                if ib * 16 + jc >= 8:
                                    pjc = jc - 8 if jc >= 8 else 8 + jc
                                    pib = ib if jc >= 8 else ib - 1
                                    c.wait(eng, "sPE", f"{pib}:nacc{pjc}")
                                c.op(eng, lambda e, jc=jc: e.activation(
                                     sqb[:, (jc % 8) * WI:(jc % 8 + 1) * WI],
                                     zbufT[:, jc * WI:(jc + 1) * WI], AF.Square),
                                     inc="sACT", mark=f"{ib}:nsq{jc}")
                            c.wait(eng, "sPE", f"{ib}:nacc15")
                            c.op(eng, lambda e: e.activation(
                                 nstat[0:1, 0:WI], P7[0:1, 0:WI], AF.Sqrt),
                                 inc="sACT", mark=f"{ib}:nsqrt")
                    if B2_LVL < 5:
                        continue
                    c.wait(eng, "sPE", f"{ib}:zb2m")
                    c.op(eng, lambda e: e.copy(
                         zbp[64:80, 0:WI], P4[64:80, 0:WI]),
                         inc="sACT", mark=f"{ib}:zb2e")
                    if B2_LVL < 6:
                        continue
                    for oc in range(NC_):
                        pq = [P0, P1][oc % 2]
                        c.wait(eng, "sPE", f"{ib}:mvo{oc}")
                        c.op(eng, lambda e, pq=pq, oc=oc: e.copy(
                             pbuf[:, oc * WI:(oc + 1) * WI], pq[:, 0:WI]),
                             inc="sACT", mark=f"{ib}:mvoe{oc}")

            def b2_pl(c, eng):
                for ib in range(B2_NIB):
                    i0 = ib * WI
                    if B2_LVL < 4:
                        continue
                    if B2_PATHS >= 1:
                        for jc in range(16):
                            seq = ib * 16 + jc
                            o = (jc % 2) * 2 * WI
                            c.wait(eng, "sACT", f"sp_l{seq}")
                            c.op(eng, lambda e, jc=jc, o=o: e.tensor_tensor(
                                 zbufT2[:, jc * WI:(jc + 1) * WI],
                                 spb[:, o + WI:o + 2 * WI],
                                 spb2[:, o + WI:o + 2 * WI],
                                 ALU.add), inc="sPOOL", mark=f"sp_sp{seq}")
                    c.wait(eng, "sDVE", f"{ib}:nrec")
                    c.dma(eng, d_recipn[0:1, i0:i0 + WI],
                          nstat[0:1, 2 * WI:3 * WI], "sPLr", mark=f"recdma{ib}")
                    if B2_LVL < 6:
                        continue
                    for dc in range(NC_):
                        gdc = ib * NC_ + dc
                        c.wait(eng, "sDVE", f"{ib}:spre{dc}")
                        c.dma(eng, d_spre[dc * 128:(dc + 1) * 128, i0:i0 + WI],
                              spre_t[:, (dc % 2) * WI:(dc % 2 + 1) * WI],
                              f"sST{gdc % 2}", mark=f"spdma{gdc}")

            if "kb" not in dbg_outs:
                ctx.run_block([
                    ("tensor", b2_pe),
                    ("vector", b2_dve),
                    ("scalar", b2_act),
                    ("gpsimd", b2_pl),
                ])

            if "spre" in dbg_outs:
                def tdma2(c, eng):
                    if B2_LVL >= 6:
                        c.wait(eng, "sST0", f"spdma{NIBG - 2}")
                        c.wait(eng, "sST1", f"spdma{NIBG - 1}")
                    if B2_LVL >= 4:
                        c.wait(eng, "sPLr", f"recdma{B2_NIB - 1}")
                    for dc in range(NC_):
                        c.dma(eng, kb[:, 0:T], d_spre[dc * 128:(dc + 1) * 128, :],
                              "sPL", mark=f"tl{dc}")
                        c.wait(eng, "sPL", f"tl{dc}")
                        c.dma(eng, dbg_outs["spre"][dc * 128:(dc + 1) * 128, :],
                              kb[:, 0:T], "sPL", mark=f"ts{dc}")
                        c.wait(eng, "sPL", f"ts{dc}")
                    c.dma(eng, scr[0:1, 0:T], d_recipn, "sPL", mark="trl")
                    c.wait(eng, "sPL", "trl")
                    c.dma(eng, dbg_outs["recipn"], scr[0:1, 0:T], "sPL",
                          mark="taprecipn")
                    c.wait(eng, "sPL", "taprecipn")
                ctx.run_block([("gpsimd", tdma2)])

        if "xn" not in dbg_outs and B3_ON:
          with contextlib.ExitStack() as stB3:
            wotb = sb(stB3, "wotb", [128, NC_ * 1024], BF16)
            wgb = sb(stB3, "wgb", [128, NC_ * 8], BF16)
            stg = sb(stB3, "stg", [128, T], F32)
            wsb = sb(stB3, "wsb", [128, 4 * 4096], BF16)
            hid = sb(stB3, "hid", [128, 32 * SC], BF16)
            s_sb = sb(stB3, "s_sb", [128, NC_ * SC], BF16)
            sprl = sb(stB3, "sprl", [128, NC_ * SC], BF16)
            xrl = sb(stB3, "xrl", [128, 2 * SC], F32)
            rcp_sb = sb(stB3, "rcp_sb", [1, T], F32)
            lg_sb = sb(stB3, "lg_sb", [8, SC], F32)
            gt = sb(stB3, "gt", [128, 4 * 8], F32)
            gsc = sb(stB3, "gsc", [128, 16], F32)
            srow = sb(stB3, "srow", [1, SC], F32)
            scb = sb(stB3, "scb", [128, SC], F32)
            rnb = sb(stB3, "rnb", [128, SC], F32)
            ab2 = sb(stB3, "ab2", [128, 2 * SC], F32)
            x1sc = sb(stB3, "x1sc", [128, NC_ * SC], F32)
            sqt = sb(stB3, "sqt", [128, 2 * SC], BF16)
            st2 = sb(stB3, "st2", [1, 4 * SC], F32)
            gct = sb(stB3, "gct", [128, 4 * SC], F32)
            Q = [ps(stB3, f"Q{i}", [128, 512]) for i in range(8)]

            def b3_drain(c, eng):
                if TAG_B2_FULL:
                    c.wait(eng, "sST0", f"spdma{NIBG - 2}")
                    c.wait(eng, "sST1", f"spdma{NIBG - 1}")
                    c.wait(eng, "sPLr", f"recdma{B2_NIB - 1}")

            def slab_src(g):
                sc, j = divmod(g, 16)
                if j < 8:
                    return w1.rearrange("(dc p) f -> p dc f", p=128)[
                        :, :, j * 512:(j + 1) * 512]
                jj = j - 8
                return w2.rearrange("(fc p) d -> p fc d", p=128)[
                    :, 4 * jj:4 * jj + 4, :]

            def b3_sp(c, eng):
                b3_drain(c, eng)
                items = [("wot0", woT.rearrange("(dc p) f -> p dc f", p=128)[:, 0:4, :]),
                         ("wot1", woT.rearrange("(dc p) f -> p dc f", p=128)[:, 4:8, :]),
                         ("wg", wg.rearrange("(dc p) f -> p dc f", p=128))]
                for g in range(16 * NSC):
                    items.append((f"sl{g}", slab_src(g)))
                for i, (nm, srcap) in enumerate(items):
                    lane = f"sLD{i % 2}"
                    w = 8 if nm == "wg" else (1024 if (nm.startswith("wot") or
                         (nm.startswith("sl") and int(nm[2:]) % 16 >= 8)) else 512)
                    nch = 4 if (nm.startswith("wot") or
                                (nm.startswith("sl") and int(nm[2:]) % 16 >= 8))                         else 8
                    if nm == "wot0":
                        dst = wotb[:, 0:4096]
                    elif nm == "wot1":
                        dst = wotb[:, 4096:8192]
                    elif nm == "wg":
                        dst = wgb[:, 0:64]
                    else:
                        if i >= 7:
                            c.wait(eng, "sPE", f"b3:wdone{i - 4}")
                        dst = wsb[:, (i % 4) * 4096:(i % 4) * 4096 + nch * w]
                    c.dma(eng, dst.rearrange("p (a b) -> p a b", a=nch),
                          srcap, lane, mark=f"b3:ld{i}")

            def b3_pl(c, eng):
                if TAG_B2_FULL:
                    c.wait(eng, "sST0", f"spdma{NIBG - 2}")
                    c.wait(eng, "sST1", f"spdma{NIBG - 1}")
                    c.wait(eng, "sPLr", f"recdma{B2_NIB - 1}")
                c.dma(eng, rcp_sb[0:1, :], d_recipn, "sPLc", mark="b3:rcp")
                for sc in range(NSC):
                    if sc > 0:
                        c.wait(eng, "sPE", f"b3:sundone{sc - 1}")
                    c.dma(eng, sprl.rearrange("p (a b) -> p a b", a=NC_),
                          d_spre.rearrange("(dc p) t -> p dc t", p=128)[
                              :, :, sc * SC:(sc + 1) * SC],
                          "sPLr", mark=f"b3:sp{sc}")
                    for dd in range(NC_):
                        gx = sc * NC_ + dd
                        if gx >= 2:
                            c.wait(eng, "sDVE", f"b3:cmb{(gx - 2) // NC_}_{(gx - 2) % NC_}")
                        c.dma(eng, xrl[:, (dd % 2) * SC:(dd % 2 + 1) * SC],
                              xT[dd * 128:(dd + 1) * 128, sc * SC:(sc + 1) * SC],
                              f"sST{gx % 2}", mark=f"b3:x{gx}")
                    for dc in range(NC_):
                        c.wait(eng, "sDVE", f"b3:cmb{sc}_{dc}")
                        c.dma(eng, d_x1[dc * 128:(dc + 1) * 128, sc * SC:(sc + 1) * SC],
                              x1sc[:, dc * SC:(dc + 1) * SC], "sPL",
                              mark=f"b3:x1s{sc}_{dc}")

            def b3_pe(c, eng):
                b3_drain(c, eng)
                for sc in range(NSC):
                    s0 = sc * SC
                    if sc > 0:
                        c.wait(eng, "sDVE", f"b3:xn2{sc - 1}_{NC_ - 1}")
                        c.wait(eng, "sPL", f"b3:x1s{sc - 1}_{NC_ - 1}")
                    for j in range(8):
                        gi = 3 + sc * 16 + j
                        c.wait(eng, f"sLD{gi % 2}", f"b3:ld{gi}")
                        for fcw in range(4):
                            fc = j * 4 + fcw
                            pq = Q[fc % 2]
                            if fc >= 2:
                                c.wait(eng, "sACT", f"b3:he{sc}_{fc - 2}")
                            for dc in range(NC_):
                                c.op(eng, lambda e, pq=pq, gi=gi, fcw=fcw, dc=dc,
                                     s0=s0: e.matmul(
                                     pq[:, 0:SC],
                                     wsb[:, (gi % 4) * 4096 + dc * 512 + fcw * 128:
                                         (gi % 4) * 4096 + dc * 512 + (fcw + 1) * 128],
                                     nrm[:, dc * T + s0: dc * T + s0 + SC],
                                     start=(dc == 0), stop=(dc == NC_ - 1)),
                                     inc="sPE",
                                     mark=(f"b3:hm{sc}_{fc}" if dc == NC_ - 1 else None))
                        c.mark(f"b3:wdone{gi}", "sPE")
                    c.wait(eng, "sLD0", "b3:ld2")
                    for dc in range(NC_):
                        c.op(eng, lambda e, dc=dc, s0=s0: e.matmul(
                             Q[2][0:8, 0:SC], wgb[:, dc * 8:(dc + 1) * 8],
                             nrm[:, dc * T + s0: dc * T + s0 + SC],
                             start=(dc == 0), stop=(dc == NC_ - 1)),
                             inc="sPE", mark=(f"b3:lgm{sc}" if dc == NC_ - 1 else None))
                    c.wait(eng, "sDVE", f"b3:lge{sc}")
                    for tt_ in range(4):
                        pq = Q[3 + tt_ % 2]
                        if tt_ >= 2:
                            c.wait(eng, "sDVE", f"b3:gte{sc}_{tt_ - 2}")
                        c.op(eng, lambda e, pq=pq, tt_=tt_: e.transpose(
                             pq[:, 0:8], lg_sb[0:8, tt_ * 128:(tt_ + 1) * 128],
                             identf[0:8, 0:8]), inc="sPE", mark=f"b3:tr{sc}_{tt_}")
                    for tt_ in range(4):
                        c.wait(eng, "sDVE", f"b3:scl{sc}_{tt_}")
                        c.op(eng, lambda e, tt_=tt_: e.transpose(
                             Q[5][0:1, tt_ * 128:(tt_ + 1) * 128],
                             gsc[:, tt_: tt_ + 1], identf[:, :]),
                             inc="sPE", mark=(f"b3:str{sc}" if tt_ == 3 else None))
                    c.wait(eng, "sACT", f"b3:sre{sc}")
                    c.op(eng, lambda e: e.matmul(
                         Q[6][:, 0:SC], onesrow_f[0:1, :], srow[0:1, 0:SC],
                         start=True, stop=True), inc="sPE", mark=f"b3:bc1{sc}")
                    c.wait(eng, "sPLc", "b3:rcp")
                    c.op(eng, lambda e, s0=s0: e.matmul(
                         Q[7][:, 0:SC], onesrow_f[0:1, :], rcp_sb[0:1, s0:s0 + SC],
                         start=True, stop=True), inc="sPE", mark=f"b3:bc2{sc}")
                    c.wait(eng, "sPLr", f"b3:sp{sc}")
                    c.wait(eng, "sLD0", "b3:ld0")
                    c.wait(eng, "sLD1", "b3:ld1")
                    c.wait(eng, "sDVE", f"b3:rnbe{sc}")
                    c.wait(eng, "sACT", f"b3:he{sc}_{30}")
                    c.wait(eng, "sACT", f"b3:he{sc}_{31}")
                    for dd in range(NC_):
                        pq = Q[dd % 2]
                        if dd >= 2:
                            c.wait(eng, "sDVE", f"b3:sre2{sc}_{dd - 2}")
                        for dc in range(NC_):
                            c.op(eng, lambda e, pq=pq, dd=dd, dc=dc: e.matmul(
                                 pq[:, 0:SC],
                                 wotb[:, dc * 1024 + dd * 128: dc * 1024 + (dd + 1) * 128],
                                 sprl[:, dc * SC:(dc + 1) * SC],
                                 start=(dc == 0), stop=(dc == NC_ - 1)),
                                 inc="sPE",
                                 mark=(f"b3:sun{sc}_{dd}" if dc == NC_ - 1 else None))
                    c.mark(f"b3:sundone{sc}", "sPE")
                    c.wait(eng, "sDVE", f"b3:sre2{sc}_{NC_ - 2}")
                    c.wait(eng, "sDVE", f"b3:sre2{sc}_{NC_ - 1}")
                    for j8 in range(8):
                        gi = 3 + sc * 16 + 8 + j8
                        c.wait(eng, f"sLD{gi % 2}", f"b3:ld{gi}")
                        for fcw in range(4):
                            fc = j8 * 4 + fcw
                            for dd in range(NC_):
                                c.op(eng, lambda e, gi=gi, fcw=fcw, dd=dd, fc=fc:
                                     e.matmul(
                                     Q[dd][:, 0:SC],
                                     wsb[:, (gi % 4) * 4096 + fcw * 1024 + dd * 128:
                                         (gi % 4) * 4096 + fcw * 1024 + (dd + 1) * 128],
                                     hid[:, fc * SC:(fc + 1) * SC],
                                     start=(fc == 0), stop=(fc == 31)),
                                     inc="sPE",
                                     mark=(f"b3:dn{sc}_{dd}" if fc == 31 else None))
                        c.mark(f"b3:wdone{gi}", "sPE")
                    c.wait(eng, "sDVE", f"b3:cmb{sc}_{NC_ - 1}")
                    for dc in range(NC_):
                        c.op(eng, lambda e, dc=dc: e.matmul(
                             Q[0][0:1, 0:SC], ones_f[:, 0:1],
                             x1sc[:, dc * SC:(dc + 1) * SC],
                             start=(dc == 0), stop=(dc == NC_ - 1)),
                             inc="sPE", mark=(f"b3:ssum{sc}" if dc == NC_ - 1 else None))
                    for dc in range(NC_):
                        c.wait(eng, "sACT", f"b3:sq2{sc}_{dc}")
                        c.op(eng, lambda e, dc=dc: e.matmul(
                             Q[1][0:1, 0:SC], ones_b[:, 0:1],
                             sqt[:, (dc % 2) * SC:(dc % 2 + 1) * SC],
                             start=(dc == 0), stop=(dc == NC_ - 1)),
                             inc="sPE", mark=f"b3:sqm{sc}_{dc}")
                    c.mark(f"b3:ssq{sc}", "sPE")
                    c.wait(eng, "sDVE", f"b3:ab2{sc}")
                    c.op(eng, lambda e: e.matmul(
                         Q[2][:, 0:SC], onesrow_f[0:1, :], st2[0:1, 0:SC],
                         start=True, stop=True), inc="sPE")
                    c.op(eng, lambda e: e.matmul(
                         Q[3][:, 0:SC], onesrow_f[0:1, :], st2[0:1, SC:2 * SC],
                         start=True, stop=True), inc="sPE", mark=f"b3:bc3{sc}")

            def b3_dve(c, eng):
                b3_drain(c, eng)
                for sc in range(NSC):
                    s0 = sc * SC
                    c.wait(eng, "sPE", f"b3:lgm{sc}")
                    c.op(eng, lambda e: e.tensor_copy(lg_sb[0:8, 0:SC],
                                                      Q[2][0:8, 0:SC]),
                         inc="sDVE", mark=f"b3:lge{sc}")
                    for tt_ in range(4):
                        pq = Q[3 + tt_ % 2]
                        c.wait(eng, "sPE", f"b3:tr{sc}_{tt_}")
                        c.op(eng, lambda e, pq=pq, tt_=tt_: e.tensor_copy(
                             gt[:, tt_ * 8:(tt_ + 1) * 8], pq[:, 0:8]),
                             inc="sDVE", mark=f"b3:gte{sc}_{tt_}")
                        c.wait(eng, "sDVE", f"b3:gte{sc}_{tt_}")
                        g0 = lambda col: gsc[:, col:col + 1]
                        tile = lambda: gt[:, tt_ * 8:(tt_ + 1) * 8]
                        c.op(eng, lambda e: e.reduce_max(g0(8), tile(), axis=AX,
                                                         negate=True), inc="sDVE",
                             mark=f"b3:g1_{sc}_{tt_}")
                        c.wait(eng, "sACT", f"b3:gexp{sc}_{tt_}")
                        c.op(eng, lambda e: e.reduce_sum(g0(9), tile(), axis=AX),
                             inc="sDVE", mark=f"b3:g2_{sc}_{tt_}")
                        c.wait(eng, "sDVE", f"b3:g2_{sc}_{tt_}")
                        c.op(eng, lambda e: e.reciprocal(g0(10), g0(9)), inc="sDVE",
                             mark=f"b3:g3_{sc}_{tt_}")
                        c.wait(eng, "sDVE", f"b3:g3_{sc}_{tt_}")
                        c.op(eng, lambda e: e.tensor_scalar(tile(), tile(), g0(10),
                             None, ALU.mult), inc="sDVE", mark=f"b3:g4_{sc}_{tt_}")
                        c.wait(eng, "sDVE", f"b3:g4_{sc}_{tt_}")
                        c.op(eng, lambda e: e.reduce_max(g0(11), tile(), axis=AX),
                             inc="sDVE", mark=f"b3:g5_{sc}_{tt_}")
                        c.wait(eng, "sDVE", f"b3:g5_{sc}_{tt_}")
                        c.op(eng, lambda e: e.tensor_scalar(gct[:, 0:8], tile(),
                             g0(11), None, ALU.is_ge), inc="sDVE",
                             mark=f"b3:g6_{sc}_{tt_}")
                        c.wait(eng, "sDVE", f"b3:g6_{sc}_{tt_}")
                        c.op(eng, lambda e: e.scalar_tensor_tensor(
                             gct[:, 8:16], gct[:, 0:8], -1e30, tile(),
                             ALU.mult, ALU.add), inc="sDVE", mark=f"b3:g7_{sc}_{tt_}")
                        c.wait(eng, "sDVE", f"b3:g7_{sc}_{tt_}")
                        c.op(eng, lambda e: e.reduce_max(g0(12), gct[:, 8:16],
                             axis=AX), inc="sDVE", mark=f"b3:g8_{sc}_{tt_}")
                        c.wait(eng, "sDVE", f"b3:g8_{sc}_{tt_}")
                        c.op(eng, lambda e: e.tensor_tensor(g0(13), g0(11), g0(12),
                             ALU.add), inc="sDVE", mark=f"b3:g9_{sc}_{tt_}")
                        c.wait(eng, "sDVE", f"b3:g9_{sc}_{tt_}")
                        c.op(eng, lambda e: e.tensor_scalar(g0(14), g0(13),
                             -F32EPS, F32EPS, ALU.add, ALU.max) if False else
                             e.tensor_scalar(g0(14), g0(13), F32EPS, None,
                                             ALU.max), inc="sDVE") if False else None
                        c.op(eng, lambda e: e.tensor_scalar(g0(14), g0(13),
                             F32EPS, F32EPS, ALU.max, ALU.bypass), inc="sDVE")                             if False else None
                        c.op(eng, lambda e: e.tensor_scalar(g0(14), g0(13),
                             F32EPS, 0.0, ALU.max, ALU.add), inc="sDVE",
                             mark=f"b3:g10_{sc}_{tt_}")
                        c.wait(eng, "sDVE", f"b3:g10_{sc}_{tt_}")
                        c.op(eng, lambda e: e.reciprocal(g0(15), g0(14)), inc="sDVE",
                             mark=f"b3:g11_{sc}_{tt_}")
                        c.wait(eng, "sDVE", f"b3:g11_{sc}_{tt_}")
                        c.op(eng, lambda e, tt_=tt_: e.tensor_tensor(
                             gsc[:, tt_: tt_ + 1], g0(13), g0(15), ALU.mult),
                             inc="sDVE", mark=f"b3:scl{sc}_{tt_}")
                    c.wait(eng, "sPE", f"b3:bc1{sc}")
                    c.op(eng, lambda e: e.tensor_copy(scb[:, 0:SC], Q[6][:, 0:SC]),
                         inc="sDVE", mark=f"b3:scbe{sc}")
                    c.wait(eng, "sPE", f"b3:bc2{sc}")
                    c.op(eng, lambda e: e.tensor_copy(rnb[:, 0:SC], Q[7][:, 0:SC]),
                         inc="sDVE", mark=f"b3:rnbe{sc}")
                    for dd in range(NC_):
                        pq = Q[dd % 2]
                        c.wait(eng, "sPE", f"b3:sun{sc}_{dd}")
                        c.op(eng, lambda e, pq=pq, dd=dd: e.tensor_tensor(
                             s_sb[:, dd * SC:(dd + 1) * SC], pq[:, 0:SC],
                             rnb[:, 0:SC], ALU.mult), inc="sDVE",
                             mark=f"b3:sre2{sc}_{dd}")
                    for dd in range(NC_):
                        gx = sc * NC_ + dd
                        c.wait(eng, "sPE", f"b3:dn{sc}_{dd}")
                        c.op(eng, lambda e, dd=dd: e.tensor_tensor(
                             gct[:, (dd % 2) * SC:(dd % 2 + 1) * SC],
                             Q[dd][:, 0:SC], scb[:, 0:SC], ALU.mult), inc="sDVE",
                             mark=f"b3:t1_{sc}_{dd}")
                        c.wait(eng, "sACT", f"b3:sil{sc}_{dd}")
                        c.op(eng, lambda e, dd=dd: e.tensor_tensor(
                             gct[:, (dd % 2) * SC:(dd % 2 + 1) * SC],
                             gct[:, (2 + dd % 2) * SC:(3 + dd % 2) * SC],
                             s_sb[:, dd * SC:(dd + 1) * SC], ALU.mult), inc="sDVE",
                             mark=f"b3:t3_{sc}_{dd}")
                        c.wait(eng, f"sST{gx % 2}", f"b3:x{gx}")
                        c.wait(eng, "sDVE", f"b3:t3_{sc}_{dd}")
                        c.op(eng, lambda e, dd=dd: e.tensor_tensor(
                             x1sc[:, dd * SC:(dd + 1) * SC],
                             gct[:, (dd % 2) * SC:(dd % 2 + 1) * SC],
                             xrl[:, (dd % 2) * SC:(dd % 2 + 1) * SC], ALU.add),
                             inc="sDVE", mark=f"b3:cmb{sc}_{dd}")
                    c.wait(eng, "sPE", f"b3:ssum{sc}")
                    c.op(eng, lambda e: e.tensor_scalar(
                         st2[0:1, 2 * SC:3 * SC], Q[0][0:1, 0:SC], 1.0 / D, None,
                         ALU.mult), inc="sDVE", mark=f"b3:mn{sc}")
                    c.wait(eng, "sDVE", f"b3:mn{sc}")
                    c.op(eng, lambda e: e.tensor_tensor(
                         st2[0:1, 3 * SC:4 * SC], st2[0:1, 2 * SC:3 * SC],
                         st2[0:1, 2 * SC:3 * SC], ALU.mult), inc="sDVE",
                         mark=f"b3:m2{sc}")
                    c.wait(eng, "sPE", f"b3:ssq{sc}")
                    c.wait(eng, "sDVE", f"b3:m2{sc}")
                    c.op(eng, lambda e: e.scalar_tensor_tensor(
                         st2[0:1, 3 * SC:4 * SC], Q[1][0:1, 0:SC], 1.0 / D,
                         st2[0:1, 3 * SC:4 * SC], ALU.mult, ALU.subtract),
                         inc="sDVE", mark=f"b3:var{sc}")
                    c.wait(eng, "sACT", f"b3:sqr{sc}")
                    c.op(eng, lambda e: e.reciprocal(st2[0:1, 0:SC],
                                                     st2[0:1, SC:2 * SC]),
                         inc="sDVE", mark=f"b3:rs{sc}")
                    c.wait(eng, "sDVE", f"b3:rs{sc}")
                    c.op(eng, lambda e: e.scalar_tensor_tensor(
                         st2[0:1, SC:2 * SC], st2[0:1, 2 * SC:3 * SC], -1.0,
                         st2[0:1, 0:SC], ALU.mult, ALU.mult), inc="sDVE",
                         mark=f"b3:ab2{sc}")
                    c.wait(eng, "sPE", f"b3:bc3{sc}")
                    c.op(eng, lambda e: e.tensor_copy(ab2[:, 0:SC], Q[2][:, 0:SC]),
                         inc="sDVE")
                    c.op(eng, lambda e: e.tensor_copy(ab2[:, SC:2 * SC],
                                                      Q[3][:, 0:SC]),
                         inc="sDVE", mark=f"b3:abe{sc}")
                    c.wait(eng, "sDVE", f"b3:abe{sc}")
                    for dc in range(NC_):
                        p2 = (2 + dc % 2) * SC
                        c.op(eng, lambda e, dc=dc, p2=p2: e.tensor_tensor(
                             gct[:, p2:p2 + SC], x1sc[:, dc * SC:(dc + 1) * SC],
                             ab2[:, 0:SC], ALU.mult), inc="sDVE",
                             mark=f"b3:xa{sc}_{dc}")
                        c.wait(eng, "sDVE", f"b3:xa{sc}_{dc}")
                        c.op(eng, lambda e, dc=dc, s0=s0, p2=p2: e.tensor_tensor(
                             nrm[:, dc * T + s0: dc * T + s0 + SC],
                             gct[:, p2:p2 + SC], ab2[:, SC:2 * SC], ALU.add),
                             inc="sDVE", mark=f"b3:xn2{sc}_{dc}")

            def b3_act(c, eng):
                b3_drain(c, eng)
                for sc in range(NSC):
                    for fc in range(32):
                        pq = Q[fc % 2]
                        c.wait(eng, "sPE", f"b3:hm{sc}_{fc}")
                        c.op(eng, lambda e, pq=pq, fc=fc: e.activation(
                             hid[:, fc * SC:(fc + 1) * SC], pq[:, 0:SC], AF.Relu),
                             inc="sACT", mark=f"b3:he{sc}_{fc}")
                    for tt_ in range(4):
                        c.wait(eng, "sDVE", f"b3:g1_{sc}_{tt_}")
                        c.op(eng, lambda e, tt_=tt_: e.activation(
                             gt[:, tt_ * 8:(tt_ + 1) * 8],
                             gt[:, tt_ * 8:(tt_ + 1) * 8], AF.Exp,
                             bias=gsc[:, 8:9]), inc="sACT",
                             mark=f"b3:gexp{sc}_{tt_}")
                    c.wait(eng, "sPE", f"b3:str{sc}")
                    c.op(eng, lambda e: e.copy(srow[0:1, 0:SC], Q[5][0:1, 0:SC]),
                         inc="sACT", mark=f"b3:sre{sc}")
                    for dd in range(NC_):
                        c.wait(eng, "sDVE", f"b3:t1_{sc}_{dd}")
                        c.op(eng, lambda e, dd=dd: e.activation(
                             gct[:, (2 + dd % 2) * SC:(3 + dd % 2) * SC],
                             gct[:, (dd % 2) * SC:(dd % 2 + 1) * SC], AF.Silu),
                             inc="sACT", mark=f"b3:sil{sc}_{dd}")
                    for dc in range(NC_):
                        c.wait(eng, "sDVE", f"b3:cmb{sc}_{dc}")
                        gdc2 = sc * NC_ + dc
                        if gdc2 >= 2:
                            psc, pdc = divmod(gdc2 - 2, NC_)
                            c.wait(eng, "sPE", f"b3:sqm{psc}_{pdc}")
                        c.op(eng, lambda e, dc=dc: e.activation(
                             sqt[:, (dc % 2) * SC:(dc % 2 + 1) * SC],
                             x1sc[:, dc * SC:(dc + 1) * SC], AF.Square),
                             inc="sACT", mark=f"b3:sq2{sc}_{dc}")
                    c.wait(eng, "sDVE", f"b3:var{sc}")
                    c.op(eng, lambda e: e.activation(
                         st2[0:1, SC:2 * SC], st2[0:1, 3 * SC:4 * SC], AF.Sqrt,
                         bias=cvals[0:1, 0:1]), inc="sACT", mark=f"b3:sqr{sc}")

            def b3_init(c, eng):
                c.dma(eng, identf[:, :], identf_t.ap(), "sPL", mark="b3:idf")

            def b3_pe_pre(c, eng):
                c.wait(eng, "sPL", "b3:idf")

            ctx.run_block([
                ("gpsimd", b3_init),
                ("gpsimd", b3_pl),
                ("sync", b3_sp),
                ("tensor", b3_pe_pre),
                ("tensor", b3_pe),
                ("vector", b3_dve),
                ("scalar", b3_act),
            ])

            if "x1" in dbg_outs:
                def tdma3(c, eng):
                    c.wait(eng, "sPL", f"b3:x1s{NSC - 1}_{NC_ - 1}")
                    for dc in range(NC_):
                        c.dma(eng, stg[:, 0:T], d_x1[dc * 128:(dc + 1) * 128, :],
                              "sPL", mark=f"tx{dc}")
                        c.wait(eng, "sPL", f"tx{dc}")
                        c.dma(eng, dbg_outs["x1"][dc * 128:(dc + 1) * 128, :],
                              stg[:, 0:T], "sPL", mark=f"txo{dc}")
                        c.wait(eng, "sPL", f"txo{dc}")
                    c.dma(eng, dbg_outs["xn2"], nrm[:, :], "sPL", mark="txn2")
                    c.wait(eng, "sPL", "txn2")
                ctx.run_block([("gpsimd", tdma3)])

        if "xn" not in dbg_outs and B4_ON:
          with contextlib.ExitStack() as stB4:
            out_g = sb(stB4, "out_g", [128, NC_ * T], BF16)
            decay = sb(stB4, "decay", [128, T], F32)
            upd = sb(stB4, "upd", [128, T], F32)
            states = sb(stB4, "states", [128, T], F32)
            gatee = sb(stB4, "gatee", [128, T], F32)
            wsb4 = sb(stB4, "wsb4", [128, 4 * 1024], BF16)
            esc = sb(stB4, "esc", [128, 2 * SC], F32)
            spu4 = sb(stB4, "spu4", [128, 2 * SC], BF16)
            spe4 = sb(stB4, "spe4", [128, 2 * SC], BF16)
            ab4 = sb(stB4, "ab4", [128, 2 * NC_], F32)
            s1t = sb(stB4, "s1t", [128, 2 * SC], BF16)
            Q = [ps(stB4, f"QB{i}", [128, 512]) for i in range(4)]

            def b4_drain(c, eng):
                c.wait(eng, "sPL", f"b3:x1s{NSC - 1}_{NC_ - 1}")

            def b4_sp(c, eng):
                b4_drain(c, eng)
                for g in range(40):
                    lane = f"sLD{g % 2}"
                    if g >= 4:
                        c.wait(eng, "sPE", f"b4:wd{g - 4}")
                    if g < 32:
                        ec, wi = divmod(g, 4)
                        srcap = l1w[wi].rearrange("(dc p) e -> p dc e", p=128)[
                            :, :, ec * 128:(ec + 1) * 128]
                    else:
                        e2c = g - 32
                        srcap = woutT.rearrange("(ec p) e -> p ec e", p=128)[
                            :, :, e2c * 128:(e2c + 1) * 128]
                    c.dma(eng, wsb4[:, (g % 4) * 1024:(g % 4 + 1) * 1024]
                          .rearrange("p (a b) -> p a b", a=NC_),
                          srcap, lane, mark=f"b4:ld{g}")

            def b4_cv(c, eng):
                b4_drain(c, eng)
                c.dma(eng, ab4[:, NC_:2 * NC_], log_a, "sPLc", mark="b4:la")
                for e2c in range(NC_):
                    for sc in range(NSC):
                        gy = e2c * NSC + sc
                        c.wait(eng, "sDVE", f"b4:s1e{e2c}_{sc}")
                        c.dma(eng, d_s1[e2c * 128:(e2c + 1) * 128,
                                        sc * SC:(sc + 1) * SC],
                              s1t[:, (gy % 2) * SC:(gy % 2 + 1) * SC],
                              f"sST{gy % 2}", mark=f"b4:s1s{gy}")

            def b4_pe(c, eng):
                b4_drain(c, eng)
                for ec in range(NC_):
                    for wi in range(4):
                        g = ec * 4 + wi
                        c.wait(eng, f"sLD{g % 2}", f"b4:ld{g}")
                        for sc in range(NSC):
                            pq = Q[sc % 2]
                            if g + sc > 0:
                                pg, psc = (g, sc - 2) if sc >= 2 else                                     (g - 1, sc + 2)
                                if psc >= 0 and pg >= 0:
                                    c.wait(eng, "sACT", f"b4:ev{pg}_{psc}")
                            for dc in range(NC_):
                                c.op(eng, lambda e, pq=pq, g=g, dc=dc, sc=sc:
                                     e.matmul(
                                     pq[:, 0:SC],
                                     wsb4[:, (g % 4) * 1024 + dc * 128:
                                          (g % 4) * 1024 + (dc + 1) * 128],
                                     nrm[:, dc * T + sc * SC: dc * T + (sc + 1) * SC],
                                     start=(dc == 0), stop=(dc == NC_ - 1)),
                                     inc="sPE",
                                     mark=(f"b4:mm{g}_{sc}" if dc == NC_ - 1 else None))
                        c.mark(f"b4:wd{g}", "sPE")
                c.wait(eng, "sDVE", f"b4:og{NC_ - 1}")
                for e2c in range(NC_):
                    g = 32 + e2c
                    c.wait(eng, f"sLD{g % 2}", f"b4:ld{g}")
                    for sc in range(NSC):
                        gy = e2c * NSC + sc
                        pq = Q[2 + gy % 2]
                        if gy >= 2:
                            c.wait(eng, "sDVE", f"b4:s1e{(gy - 2) // NSC}_{(gy - 2) % NSC}")
                        for ec in range(NC_):
                            c.op(eng, lambda e, pq=pq, g=g, ec=ec, sc=sc: e.matmul(
                                 pq[:, 0:SC],
                                 wsb4[:, (g % 4) * 1024 + ec * 128:
                                      (g % 4) * 1024 + (ec + 1) * 128],
                                 out_g[:, ec * T + sc * SC: ec * T + (sc + 1) * SC],
                                 start=(ec == 0), stop=(ec == NC_ - 1)),
                                 inc="sPE",
                                 mark=(f"b4:ym{e2c}_{sc}" if ec == NC_ - 1 else None))
                    c.mark(f"b4:wd{g}", "sPE")

            def b4_dve(c, eng):
                b4_drain(c, eng)
                c.wait(eng, "sACT", "b4:sig_a")
                c.op(eng, lambda e: e.tensor_scalar(
                     ab4[:, 0:NC_], ab4[:, NC_:2 * NC_], -1.0, None, ALU.mult),
                     inc="sDVE", mark="b4:nega")
                for ec in range(NC_):
                    for wi in range(4):
                        g = ec * 4 + wi
                        if wi == 0:
                            for sc in range(NSC):
                                c.wait(eng, "sACT", f"b4:l_{ec}_{sc}")
                                c.op(eng, lambda e, sc=sc: e.tensor_tensor(
                                     esc[:, (sc % 2) * SC:(sc % 2 + 1) * SC],
                                     esc[:, (sc % 2) * SC:(sc % 2 + 1) * SC],
                                     spe4[:, SC:2 * SC], ALU.add), inc="sDVE",
                                     mark=f"b4:d0_{ec}_{sc}")
                                c.wait(eng, "sDVE", f"b4:d0_{ec}_{sc}")
                                c.op(eng, lambda e, sc=sc: e.tensor_scalar(
                                     esc[:, (sc % 2) * SC:(sc % 2 + 1) * SC],
                                     esc[:, (sc % 2) * SC:(sc % 2 + 1) * SC],
                                     1e-4, 0.0, ALU.max, ALU.add), inc="sDVE",
                                     mark=f"b4:dm_{ec}_{sc}")
                        if wi == 2:
                            for sc in range(NSC):
                                c.wait(eng, "sACT", f"b4:tanh_{ec}_{sc}")
                                c.op(eng, lambda e, sc=sc: e.tensor_tensor(
                                     upd[:, sc * SC:(sc + 1) * SC],
                                     upd[:, sc * SC:(sc + 1) * SC],
                                     esc[:, (sc % 2) * SC:(sc % 2 + 1) * SC],
                                     ALU.mult), inc="sDVE",
                                     mark=f"b4:um_{ec}_{sc}")
                    c.wait(eng, "sACT", f"b4:dec_{ec}_{NSC - 1}")
                    c.wait(eng, "sDVE", f"b4:um_{ec}_{NSC - 1}")
                    c.op(eng, lambda e: e.tensor_tensor_scan(
                         states[:, 0:T], decay[:, 0:T], upd[:, 0:T], 0.0,
                         ALU.mult, ALU.add), inc="sDVE", mark=f"b4:scan{ec}")
                    c.wait(eng, "sDVE", f"b4:scan{ec}")
                    c.wait(eng, "sACT", f"b4:ev{ec * 4 + 3}_{NSC - 1}")
                    c.op(eng, lambda e, ec=ec: e.tensor_tensor(
                         out_g[:, ec * T:(ec + 1) * T], gatee[:, 0:T],
                         states[:, 0:T], ALU.mult), inc="sDVE",
                         mark=f"b4:og{ec}")
                for e2c in range(NC_):
                    for sc in range(NSC):
                        gy = e2c * NSC + sc
                        pq = Q[2 + gy % 2]
                        c.wait(eng, "sPE", f"b4:ym{e2c}_{sc}")
                        if gy >= 2:
                            c.wait(eng, f"sST{gy % 2}", f"b4:s1s{gy - 2}")
                        c.op(eng, lambda e, pq=pq, gy=gy: e.tensor_copy(
                             s1t[:, (gy % 2) * SC:(gy % 2 + 1) * SC], pq[:, 0:SC]),
                             inc="sDVE", mark=f"b4:s1e{e2c}_{sc}")

            def b4_act(c, eng):
                b4_drain(c, eng)
                c.wait(eng, "sPLc", "b4:la")
                c.op(eng, lambda e: e.activation(
                     ab4[:, NC_:2 * NC_], ab4[:, NC_:2 * NC_], AF.Sigmoid),
                     inc="sACT", mark="b4:sig_a")
                for ec in range(NC_):
                    for wi in range(4):
                        g = ec * 4 + wi
                        for sc in range(NSC):
                            pq = Q[sc % 2]
                            if wi == 0 and sc == 0 and ec > 0:
                                c.wait(eng, "sDVE", f"b4:og{ec - 1}")
                            c.wait(eng, "sPE", f"b4:mm{g}_{sc}")
                            if wi == 0:
                                c.op(eng, lambda e, pq=pq, sc=sc: e.activation(
                                     spu4[:, (sc % 2) * SC:(sc % 2 + 1) * SC],
                                     pq[:, 0:SC], AF.Abs), inc="sACT",
                                     mark=f"b4:u_{ec}_{sc}")
                                c.op(eng, lambda e, pq=pq, sc=sc: e.activation(
                                     esc[:, (sc % 2) * SC:(sc % 2 + 1) * SC],
                                     pq[:, 0:SC], AF.Relu), inc="sACT",
                                     mark=f"b4:ev{g}_{sc}")
                                c.wait(eng, "sACT", f"b4:u_{ec}_{sc}")
                                c.op(eng, lambda e, sc=sc: e.activation(
                                     spe4[:, 0:SC],
                                     spu4[:, (sc % 2) * SC:(sc % 2 + 1) * SC],
                                     AF.Exp, scale=-1.0), inc="sACT",
                                     mark=f"b4:e_{ec}_{sc}")
                                c.wait(eng, "sACT", f"b4:e_{ec}_{sc}")
                                c.op(eng, lambda e: e.activation(
                                     spe4[:, SC:2 * SC], spe4[:, 0:SC],
                                     AF.Ln, bias=1.0), inc="sACT",
                                     mark=f"b4:l_{ec}_{sc}")
                                c.wait(eng, "sDVE", f"b4:dm_{ec}_{sc}")
                                c.op(eng, lambda e, ec=ec, sc=sc: e.activation(
                                     decay[:, sc * SC:(sc + 1) * SC],
                                     esc[:, (sc % 2) * SC:(sc % 2 + 1) * SC],
                                     AF.Exp, scale=ab4[:, ec:ec + 1]), inc="sACT",
                                     mark=f"b4:dec_{ec}_{sc}")
                            elif wi == 1:
                                c.op(eng, lambda e, pq=pq, sc=sc: e.activation(
                                     upd[:, sc * SC:(sc + 1) * SC], pq[:, 0:SC],
                                     AF.Sigmoid), inc="sACT",
                                     mark=f"b4:ev{g}_{sc}")
                            elif wi == 2:
                                if sc >= 2:
                                    c.wait(eng, "sDVE", f"b4:um_{ec}_{sc - 2}")
                                c.op(eng, lambda e, pq=pq, sc=sc: e.activation(
                                     esc[:, (sc % 2) * SC:(sc % 2 + 1) * SC],
                                     pq[:, 0:SC], AF.Tanh), inc="sACT",
                                     mark=f"b4:ev{g}_{sc}")
                                c.mark(f"b4:tanh_{ec}_{sc}", "sACT")
                            else:
                                c.op(eng, lambda e, pq=pq, sc=sc: e.activation(
                                     gatee[:, sc * SC:(sc + 1) * SC], pq[:, 0:SC],
                                     AF.Sigmoid), inc="sACT",
                                     mark=f"b4:ev{g}_{sc}")

            ctx.run_block([
                ("sync", b4_sp),
                ("gpsimd", b4_cv),
                ("tensor", b4_pe),
                ("vector", b4_dve),
                ("scalar", b4_act),
            ])

            if "s1" in dbg_outs:
                def tdma4(c, eng):
                    c.wait(eng, "sST0", f"b4:s1s{NC_ * NSC - 2}")
                    c.wait(eng, "sST1", f"b4:s1s{NC_ * NSC - 1}")
                    for dc in range(NC_):
                        c.dma(eng, out_g[:, 0:T], d_s1[dc * 128:(dc + 1) * 128, :],
                              "sPL", mark=f"t4{dc}")
                        c.wait(eng, "sPL", f"t4{dc}")
                        c.dma(eng, dbg_outs["s1"][dc * 128:(dc + 1) * 128, :],
                              out_g[:, 0:T], "sPL", mark=f"t4o{dc}")
                        c.wait(eng, "sPL", f"t4o{dc}")
                ctx.run_block([("gpsimd", tdma4)])

        if "xn" not in dbg_outs and B5_ON:
          with contextlib.ExitStack() as stB5:
            bfA = sb(stB5, "bfA", [128, 24576], BF16)
            stg5 = sb(stB5, "stg5", [128, 2304], F32)
            wsb5 = sb(stB5, "wsb5", [128, 2 * 4096], BF16)
            wsd5 = sb(stB5, "wsd5", [128, 2 * 4096], BF16)
            comb_tm = sb(stB5, "comb_tm", [128, 16 * 8], F32)
            routerb = sb(stB5, "routerb", [128, NC_ * 8], BF16)
            lg5 = sb(stB5, "lg5", [8, SC], F32)
            lngb = sb(stB5, "lngb", [128, 2 * NC_], F32)
            gt5 = sb(stB5, "gt5", [128, 4 * 8], F32)
            gs5 = sb(stB5, "gs5", [128, 16], F32)
            gq5 = sb(stB5, "gq5", [128, 2 * 8], F32)
            sqt5 = sb(stB5, "sqt5", [128, 2 * SC], BF16)
            st5 = sb(stB5, "st5", [1, 4 * SC], F32)
            ab5 = sb(stB5, "ab5", [128, 2 * SC], F32)
            cl5 = sb(stB5, "cl5", [128, 2 * SC], F32)
            sg5 = sb(stB5, "sg5", [128, 2 * SC], F32)
            xt5 = sb(stB5, "xt5", [128, 2 * SC], F32)
            y_tm = sb(stB5, "y_tm", [128, JC * 1024], F32)
            xstg = sb(stB5, "xstg", [128, 2 * 1024], BF16)
            shstg = sb(stB5, "shstg", [128, 2 * 512], F32)
            mloc = sb(stB5, "mloc", [128, 64], F32)
            mltmp = sb(stB5, "mltmp", [128, 16], F32)
            silb = sb(stB5, "silb", [128, 4 * 128], F32)
            iot = sb(stB5, "iot", [128, 128], F32)
            Pbuf = sb(stB5, "Pbuf", [128, 4 * 128], F32)
            rhs3 = sb(stB5, "rhs3", [128, 3 * 16], F32)
            tcs = sb(stB5, "tcs", [128, 8 * JC * 2], F32)
            tidi5 = sb(stB5, "tidi5", [128, 8 * JC], I32)
            utf = sb(stB5, "utf", [128, 128], F32)
            idsc = sb(stB5, "idsc", [128, 16], F32)
            strow = sb(stB5, "strow", [1, 16], F32)
            Q = [ps(stB5, f"QC{i}", [128, 512]) for i in range(6)]
            QT = [stB5.enter_context(nc.psum_tensor(f"QT{i}", [128, 512], BF16))
                  for i in range(2)]

            items5 = [{"nm": "router", "cls": "r"}]
            items5 += [{"nm": f"sh_gu{p}", "cls": "gu"} for p in range(4)]
            items5 += [{"nm": "sh_d0", "cls": "d"}, {"nm": "sh_d1", "cls": "d"}]
            for e in range(8):
                for fb in range(4):
                    items5 += [{"nm": f"gu{e}_{fb}_{p}", "cls": "gu"}
                               for p in range(4)]
                    items5 += [{"nm": f"d{e}_{fb}_0", "cls": "d"},
                               {"nm": f"d{e}_{fb}_1", "cls": "d"}]
            hist5 = {"gu": [], "d": []}
            for q, it in enumerate(items5):
                if it["cls"] == "r":
                    it["gate"] = None
                    continue
                h = hist5[it["cls"]]
                it["slot"] = len(h) % 2
                it["gate"] = h[-2] if len(h) >= 2 else None
                h.append(q)

            def qb_of(e, fb):
                return 7 + 6 * (e * 4 + fb)
            QBS = 1

            def item_src(nm):
                if nm == "router":
                    return [(router.rearrange("(dc p) f -> p dc f", p=128),
                             8, 0, 8)]
                if nm.startswith("sh_gu"):
                    p_ = int(nm[5])
                    sgv = sguw.rearrange("(dc p) f -> p dc f", p=128)
                    return [(sgv[:, :, p_ * 256:(p_ + 1) * 256], 8, 0, 256),
                            (sgv[:, :, 1024 + p_ * 256:1024 + (p_ + 1) * 256],
                             8, 256, 512)]
                if nm.startswith("sh_d"):
                    half = int(nm[4])
                    return [(sdnw.rearrange("(fc p) d -> p fc d", p=128)[
                        :, half * 4:(half + 1) * 4, :], 4, 0, 1024)]
                if nm.startswith("gu"):
                    e, fb, p_ = nm[2:].split("_")
                    e, fb, p_ = int(e), int(fb), int(p_)
                    gv = guw[e].rearrange("(dc p) f -> p dc f", p=128)
                    c0 = fb * 1024 + p_ * 256
                    return [(gv[:, :, c0:c0 + 256], 8, 0, 256),
                            (gv[:, :, 4096 + c0:4096 + c0 + 256],
                             8, 256, 512)]
                e, fb, half = nm[1:].split("_")
                e, fb, half = int(e), int(fb), int(half)
                r0 = fb * 8 + half * 4
                return [(dnw[e].rearrange("(fc p) d -> p fc d", p=128)[
                    :, r0:r0 + 4, :], 4, 0, 1024)]

            def b5_drain(c, eng):
                c.wait(eng, "sST0", f"b4:s1s{NC_ * NSC - 2}")
                c.wait(eng, "sST1", f"b4:s1s{NC_ * NSC - 1}")
                c.wait(eng, "sPL", f"b3:x1s{NSC - 1}_{NC_ - 1}")

            def b5_sp(c, eng):
                b5_drain(c, eng)
                for q, it in enumerate(items5):
                    pieces = item_src(it["nm"])
                    if q == 0:
                        srcap, a, c0, c1 = pieces[0]
                        c.dma(eng, stg5[:, 0:a * (c1 - c0)]
                              .rearrange("p (a b) -> p a b", a=a),
                              srcap, "sRT", mark=f"b5:ld{q}")
                        continue
                    if it["gate"] is not None:
                        c.wait(eng, "sPE", f"b5:wd{it['gate']}")
                    buf = wsb5 if it["cls"] == "gu" else wsd5
                    s0_ = it["slot"] * 4096
                    for srcap, a, c0, c1 in pieces:
                        c.dma(eng, buf[:, s0_:s0_ + 4096]
                              .rearrange("p (a b) -> p a b", a=a)[:, :, c0:c1],
                              srcap, f"sLD{q % 2}", mark=f"b5:ld{q}")
                for ddc in range(NC_):
                    if ddc >= 3:
                        c.wait(eng, "sACT", f"b5:gsil{ddc - 3}_{15}")
                    if ddc == 0:
                        c.wait(eng, "sSA", f"b5:sa7_{JC - 1}")
                    c.dma(eng, y_tm[:, (ddc % 3) * 2048:(ddc % 3 + 1) * 2048]
                          .rearrange("p (st d) -> p st d", st=16),
                          d_acc[0:T, :].rearrange("(st p) d -> p st d", p=128)[
                              :, :, ddc * 128:(ddc + 1) * 128],
                          ["sAC0", "sAC1"][ddc % 2], mark=f"b5:accl{ddc}")
                    for sc in range(NSC):
                        gy = ddc * NSC + sc
                        if gy >= 2:
                            c.wait(eng, "sDVE",
                                   f"b5:cm2_{(gy - 2) // NSC}_{(gy - 2) % NSC}")
                        c.dma(eng, stg5[:, 256 + (gy % 2) * SC:
                                         256 + (gy % 2 + 1) * SC],
                              d_x1[ddc * 128:(ddc + 1) * 128,
                                   sc * SC:(sc + 1) * SC],
                              ["sAC0", "sAC1"][gy % 2], mark=f"b5:x1l{gy}")

            def b5_cv(c, eng):
                b5_drain(c, eng)
                c.dma(eng, lngb[:, 0:NC_], ln_g, "sPLc")
                c.dma(eng, lngb[:, NC_:2 * NC_], ln_b, "sPLc", mark="b5:lng")
                c.dma(eng, utf[:, :], utf_t.ap(), "sPLc")
                c.dma(eng, iot[:, :], iot_t.ap(), "sPLc")
                c.dma(eng, idsc[:, :], idsc_t.ap(), "sPLc", mark="b5:cst")

            def b5_pl(c, eng):
                c.wait(eng, "sPLc", "b5:cst")
                c.dma(eng, d_xnl[T:T + 1, :], zrow_t.ap(), "sIN",
                      mark="b5:xz2048")
                for e in range(8):
                    c.wait(eng, "sDVE", f"b5:tidc{e}")
                    if e < 2:
                        c.wait(eng, "sIN", "b5:xz2048")
                        c.wait(eng, "sXT0", "b5:xts14")
                        c.wait(eng, "sXT1", "b5:xts15")
                        c.wait(eng, "sPE", "b5:dn8_0_31")
                    else:
                        c.wait(eng, "sPE", f"b5:xgt{e - 2}_{JC * NC_ - 1}")
                    for jc in range(JC):
                        c.op(eng, lambda ee, e=e, jc=jc: ee.indirect_dma_start(
                             out=bfA[:, (e % 2) * JC * 1024 + jc * 1024:
                                     (e % 2) * JC * 1024 + (jc + 1) * 1024],
                             out_offset=None,
                             in_=d_xnl,
                             in_offset=IndirectOffsetOnAxis(
                                 tidi5[:, e * JC + jc: e * JC + jc + 1], 0)),
                             inc="sGA", n=16, mark=f"b5:xg{e}_{jc}")
                    c.wait(eng, "sGA", f"b5:xg{e}_{JC - 1}")
                    if e >= 1:
                        pe_ = e - 1
                        if pe_ == 0:
                            c.wait(eng, "sSH0", "b5:shs30")
                            c.wait(eng, "sSH1", "b5:shs31")
                        else:
                            c.wait(eng, "sSA", f"b5:sa{pe_ - 1}_{JC - 1}")
                        c.wait(eng, "sDVE", f"b5:ysc{pe_}_{JC - 1}")
                        for jc in range(JC):
                            c.op(eng, lambda ee, pe_=pe_, jc=jc:
                                 ee.indirect_dma_start(
                                 out=d_acc,
                                 out_offset=IndirectOffsetOnAxis(
                                     tidi5[:, pe_ * JC + jc:
                                           pe_ * JC + jc + 1], 0),
                                 in_=y_tm[:, jc * 1024:(jc + 1) * 1024],
                                 in_offset=None,
                                 compute_op=ALU.add),
                                 inc="sSA", n=16, mark=f"b5:sa{pe_}_{jc}")
                            if B5_SER or jc == JC - 1:
                                c.wait(eng, "sSA", f"b5:sa{pe_}_{jc}")
                c.wait(eng, "sDVE", f"b5:ysc7_{JC - 1}")
                c.wait(eng, "sSA", f"b5:sa6_{JC - 1}")
                for jc in range(JC):
                    c.op(eng, lambda ee, jc=jc: ee.indirect_dma_start(
                         out=d_acc,
                         out_offset=IndirectOffsetOnAxis(
                             tidi5[:, 7 * JC + jc:7 * JC + jc + 1], 0),
                         in_=y_tm[:, jc * 1024:(jc + 1) * 1024],
                         in_offset=None,
                         compute_op=ALU.add),
                         inc="sSA", n=16, mark=f"b5:sa7_{jc}")
                    if B5_SER or jc == JC - 1:
                        c.wait(eng, "sSA", f"b5:sa7_{jc}")
                for ddc in range(NC_):
                    for sc in range(NSC):
                        gy = ddc * NSC + sc
                        if gy >= 2:
                            c.wait(eng, "sDVE", f"b5:cm1_{(gy - 2) // NSC}_{(gy - 2) % NSC}")
                        c.dma(eng, bfA[:, 2048 + (gy % 2) * SC:
                                        2048 + (gy % 2 + 1) * SC],
                              d_s1[ddc * 128:(ddc + 1) * 128, sc * SC:(sc + 1) * SC],
                              f"sST{gy % 2}", mark=f"b5:s1l{gy}")
                        c.wait(eng, "sDVE", f"b5:yv{ddc}_{sc}")
                        c.dma(eng, y_out[ddc * 128:(ddc + 1) * 128,
                                         sc * SC:(sc + 1) * SC],
                              stg5[:, 1280 + (gy % 2) * SC:
                                   1280 + (gy % 2 + 1) * SC],
                              f"sYS{gy % 2}", mark=f"b5:ys{gy}")
                c.wait(eng, "sYS0", f"b5:ys{NC_ * NSC - 2}")
                c.wait(eng, "sYS1", f"b5:ys{NC_ * NSC - 1}")

            def b5_pe(c, eng):
                b5_drain(c, eng)
                for sc in range(NSC):
                    s0 = sc * SC
                    if sc > 0:
                        c.wait(eng, "sDVE", f"b5:xnl{sc - 1}_{NC_ - 1}")
                    for dc in range(NC_):
                        c.op(eng, lambda e, dc=dc, s0=s0: e.matmul(
                             Q[0][0:1, 0:SC], ones_b[:, 0:1],
                             nrm[:, dc * T + s0: dc * T + s0 + SC],
                             start=(dc == 0), stop=(dc == NC_ - 1)),
                             inc="sPE", mark=(f"b5:ssum{sc}" if dc == NC_ - 1 else None))
                    for dc in range(NC_):
                        c.wait(eng, "sACT", f"b5:sq{sc}_{dc}")
                        c.op(eng, lambda e, dc=dc: e.matmul(
                             Q[1][0:1, 0:SC], ones_b[:, 0:1],
                             sqt5[:, (dc % 2) * SC:(dc % 2 + 1) * SC],
                             start=(dc == 0), stop=(dc == NC_ - 1)),
                             inc="sPE", mark=f"b5:sqm{sc}_{dc}")
                    c.wait(eng, "sDVE", f"b5:ab{sc}")
                    c.op(eng, lambda e: e.matmul(
                         Q[2][:, 0:SC], onesrow_f[0:1, :], st5[0:1, 0:SC],
                         start=True, stop=True), inc="sPE")
                    c.op(eng, lambda e: e.matmul(
                         Q[3][:, 0:SC], onesrow_f[0:1, :], st5[0:1, SC:2 * SC],
                         start=True, stop=True), inc="sPE", mark=f"b5:bc{sc}")
                    c.wait(eng, "sDVE", "b5:cv0")
                    c.wait(eng, "sDVE", f"b5:xnl{sc}_{NC_ - 1}")
                    for dc in range(NC_):
                        c.op(eng, lambda e, dc=dc, s0=s0: e.matmul(
                             Q[4][0:8, 0:SC], routerb[:, dc * 8:(dc + 1) * 8],
                             nrm[:, dc * T + s0: dc * T + s0 + SC],
                             start=(dc == 0), stop=(dc == NC_ - 1)),
                             inc="sPE", mark=(f"b5:lgm{sc}" if dc == NC_ - 1 else None))
                    c.wait(eng, "sDVE", f"b5:lge{sc}")
                    for tt_ in range(4):
                        pq = Q[4 + tt_ % 2]
                        if tt_ >= 2:
                            c.wait(eng, "sDVE", f"b5:gte{sc}_{tt_ - 2}")
                        c.op(eng, lambda e, pq=pq, tt_=tt_: e.transpose(
                             pq[:, 0:8],
                             lg5[0:8, tt_ * 128:(tt_ + 1) * 128],
                             identf[0:8, 0:8]), inc="sPE",
                             mark=f"b5:tr{sc}_{tt_}")
                    if sc == NSC - 1:
                        c.mark("b5:wd0", "sPE")
                c.wait(eng, "sPLc", "b5:cst")
                for e in range(8):
                    c.wait(eng, "sDVE", f"b5:me{e}")
                    if e > 0:
                        c.wait(eng, "sDVE", f"b5:loce{e - 1}")
                    c.op(eng, lambda ee, e=e: ee.matmul(
                         Q[0][:, 0:16], utf[:, :], mloc[:, 0:16],
                         start=True, stop=True), inc="sPE")
                    c.op(eng, lambda ee, e=e: ee.matmul(
                         Q[1][0:1, 0:16], ones_f[:, 0:1], mloc[:, 0:16],
                         start=True, stop=True), inc="sPE", mark=f"b5:pfm{e}")
                    c.wait(eng, "sDVE", f"b5:exb{e}")
                    c.op(eng, lambda ee, e=e: ee.matmul(
                         Q[2][:, 0:16], onesrow_f[0:1, :], strow[0:1, 0:16],
                         start=True, stop=True), inc="sPE", mark=f"b5:bcb{e}")
                    c.wait(eng, "sDVE", f"b5:rh{e}")
                    for sc5 in range(JC):
                        gs = e * JC + sc5
                        if gs >= 2:
                            c.wait(eng, "sDVE",
                                   f"b5:tcv{(gs - 2) // JC}_{(gs - 2) % JC}")
                        for kt in range(16):
                            g4 = gs * 16 + kt
                            c.wait(eng, "sDVE", f"b5:pb{e}_{sc5}_{kt}")
                            c.op(eng, lambda ee, e=e, sc5=sc5, kt=kt, g4=g4:
                                 ee.matmul(
                                 Q[4][:, (sc5 % 2) * 3:(sc5 % 2) * 3 + 3],
                                 Pbuf[:, (g4 % 4) * 128:(g4 % 4 + 1) * 128],
                                 rhs3[:, 3 * kt:3 * kt + 3],
                                 start=(kt == 0), stop=(kt == 15)),
                                 inc="sPE", mark=f"b5:tm{e}_{sc5}_{kt}")
                c.wait(eng, "sDVE", f"b5:loce7")
                for st in range(16):
                    for dc in range(NC_):
                        g = st * NC_ + dc
                        pq = QT[g % 2]
                        if g >= 2:
                            c.wait(eng, "sACT", f"b5:xge{(g - 2) // NC_}_{(g - 2) % NC_}")
                        c.op(eng, lambda ee, pq=pq, st=st, dc=dc: ee.transpose(
                             pq[:, 0:128],
                             nrm[:, dc * T + st * 128: dc * T + (st + 1) * 128],
                             ident[:, :]), inc="sPE", mark=f"b5:xtr{st}_{dc}")
                qb = QBS
                for fcl in range(8):
                    qt = qb + fcl // 2
                    sl = ((fcl // 2) % 2) * 4096
                    if fcl % 2 == 0:
                        c.wait(eng, f"sLD{qt % 2}", f"b5:ld{qt}")
                    for sc in range(NSC):
                        gg = fcl * NSC + sc
                        pg = Q[gg % 2]
                        pu = Q[2 + gg % 2]
                        if gg >= 2:
                            c.wait(eng, "sDVE", f"b5:cl8_0_{gg - 2}")
                            c.wait(eng, "sDVE", f"b5:hm8_0_{gg - 2}")
                        for dc in range(NC_):
                            c.op(eng, lambda e, pg=pg, sl=sl, fcl=fcl, dc=dc,
                                 sc=sc: e.matmul(
                                 pg[:, 0:SC],
                                 wsb5[:, sl + dc * 512 +
                                      (fcl % 2) * 128:
                                      sl + dc * 512 +
                                      (fcl % 2 + 1) * 128],
                                 nrm[:, dc * T + sc * SC: dc * T + (sc + 1) * SC],
                                 start=(dc == 0), stop=(dc == NC_ - 1)),
                                 inc="sPE",
                                 mark=(f"b5:gm8_0_{gg}" if dc == NC_ - 1 else None))
                        for dc in range(NC_):
                            c.op(eng, lambda e, pu=pu, sl=sl, fcl=fcl, dc=dc,
                                 sc=sc: e.matmul(
                                 pu[:, 0:SC],
                                 wsb5[:, sl + dc * 512 + 256 +
                                      (fcl % 2) * 128:
                                      sl + dc * 512 + 256 +
                                      (fcl % 2 + 1) * 128],
                                 nrm[:, dc * T + sc * SC: dc * T + (sc + 1) * SC],
                                 start=(dc == 0), stop=(dc == NC_ - 1)),
                                 inc="sPE",
                                 mark=(f"b5:um8_0_{gg}" if dc == NC_ - 1 else None))
                    if fcl % 2 == 1:
                        c.mark(f"b5:wd{qt}", "sPE")
                c.wait(eng, f"sLD{qb % 2}", f"b5:ld{qb + 4}")
                c.wait(eng, f"sLD{(qb + 1) % 2}", f"b5:ld{qb + 5}")
                c.wait(eng, "sDVE", f"b5:hm8_0_{31}")
                for st in range(16):
                    for ddh in range(2):
                        gd = st * 2 + ddh
                        pd = Q[4 + gd % 2]
                        if gd >= 2:
                            c.wait(eng, "sDVE", f"b5:she{gd - 2}")
                        for fcl in range(8):
                            c.op(eng, lambda e, pd=pd, fcl=fcl, st=st,
                                 ddh=ddh: e.matmul(
                                 pd[:, 0:SC],
                                 bfA[:, fcl * T + st * 128:
                                      fcl * T + (st + 1) * 128],
                                 wsd5[:, (fcl // 4) * 4096 + (fcl % 4) * 1024 +
                                      ddh * 512:
                                      (fcl // 4) * 4096 + (fcl % 4) * 1024 +
                                      (ddh + 1) * 512],
                                 start=(fcl == 0), stop=(fcl == 7)),
                                 inc="sPE",
                                 mark=(f"b5:dn8_0_{gd}" if fcl == 7 else None))
                c.mark(f"b5:wd{qb + 4}", "sPE")
                c.mark(f"b5:wd{qb + 5}", "sPE")
                for e_ in range(8):
                    c.wait(eng, "sGA", f"b5:xg{e_}_{JC - 1}")
                    for jc in range(JC):
                        for dc in range(NC_):
                            g = jc * NC_ + dc
                            pq = QT[g % 2]
                            if g >= 2:
                                c.wait(eng, "sDVE", f"b5:xfe{e_}_{g - 2}")
                            elif e_ > 0:
                                c.wait(eng, "sDVE",
                                       f"b5:xfe{e_ - 1}_{JC * NC_ - 2 + g}")
                            else:
                                c.wait(eng, "sACT", f"b5:xge15_{NC_ - 2 + g}")
                            c.op(eng, lambda ee, pq=pq, e_=e_, jc=jc, dc=dc:
                                 ee.transpose(
                                 pq[:, 0:128],
                                 bfA[:, (e_ % 2) * JC * 1024 + jc * 1024 +
                                     dc * 128:
                                     (e_ % 2) * JC * 1024 + jc * 1024 +
                                     (dc + 1) * 128],
                                 ident[:, :]), inc="sPE",
                                 mark=f"b5:xgt{e_}_{jc * NC_ + dc}")
                    c.wait(eng, "sDVE", f"b5:xfe{e_}_{JC * NC_ - 1}")
                    for fb in range(4):
                        qb = qb_of(e_, fb)
                        for fcl in range(8):
                            qt = qb + fcl // 2
                            sl = ((fcl // 2) % 2) * 4096
                            if fcl % 2 == 0:
                                c.wait(eng, f"sLD{qt % 2}", f"b5:ld{qt}")
                            for sub in range(2):
                                coff, cw = SUBS[sub]
                                if fcl > 0:
                                    c.wait(eng, "sDVE", f"b5:cl{e_}_{fb}_{fcl - 1}_{sub}")
                                    c.wait(eng, "sDVE", f"b5:hm{e_}_{fb}_{fcl - 1}_{sub}")
                                elif fb > 0:
                                    c.wait(eng, "sDVE", f"b5:cl{e_}_{fb - 1}_7_{sub}")
                                    c.wait(eng, "sDVE", f"b5:hm{e_}_{fb - 1}_7_{sub}")
                                elif e_ > 0:
                                    c.wait(eng, "sDVE", f"b5:cl{e_ - 1}_3_7_{sub}")
                                    c.wait(eng, "sDVE", f"b5:hm{e_ - 1}_3_7_{sub}")
                                elif sub == 0:
                                    c.wait(eng, "sDVE", "b5:she31")
                                for dc in range(NC_):
                                    c.op(eng, lambda ee, sl=sl, fcl=fcl, dc=dc,
                                         coff=coff, cw=cw: ee.matmul(
                                         Q[0 if coff == 0 else 1][:, 0:cw],
                                         wsb5[:, sl + dc * 512 +
                                              (fcl % 2) * 128:
                                              sl + dc * 512 +
                                              (fcl % 2 + 1) * 128],
                                         bfA[:, 12288 + dc * CAP + coff:
                                             12288 + dc * CAP + coff + cw],
                                         start=(dc == 0), stop=(dc == NC_ - 1)),
                                         inc="sPE",
                                         mark=(f"b5:gm{e_}_{fb}_{fcl}_{sub}"
                                               if dc == NC_ - 1 else None))
                                for dc in range(NC_):
                                    c.op(eng, lambda ee, sl=sl, fcl=fcl, dc=dc,
                                         coff=coff, cw=cw: ee.matmul(
                                         Q[2 if coff == 0 else 3][:, 0:cw],
                                         wsb5[:, sl + dc * 512 +
                                              256 + (fcl % 2) * 128:
                                              sl + dc * 512 +
                                              256 + (fcl % 2 + 1) * 128],
                                         bfA[:, 12288 + dc * CAP + coff:
                                             12288 + dc * CAP + coff + cw],
                                         start=(dc == 0), stop=(dc == NC_ - 1)),
                                         inc="sPE",
                                         mark=(f"b5:um{e_}_{fb}_{fcl}_{sub}"
                                               if dc == NC_ - 1 else None))
                            if fcl % 2 == 1:
                                c.mark(f"b5:wd{qt}", "sPE")
                        c.wait(eng, f"sLD{qb % 2}", f"b5:ld{qb + 4}")
                        c.wait(eng, f"sLD{(qb + 1) % 2}", f"b5:ld{qb + 5}")
                        c.wait(eng, "sDVE", f"b5:hm{e_}_{fb}_7_1")
                        for jsl in range(JC):
                            for ddh in range(2):
                                gd = jsl * 2 + ddh
                                pd = Q[4 + gd % 2]
                                if gd >= 2:
                                    c.wait(eng, "sDVE",
                                           f"b5:ye{e_}_{fb}_{gd - 2}")
                                for fcl in range(8):
                                    c.op(eng, lambda ee, pd=pd, fcl=fcl,
                                         jsl=jsl, ddh=ddh: ee.matmul(
                                         pd[:, 0:512],
                                         bfA[:, 18432 + fcl * CAP + jsl * 128:
                                             18432 + fcl * CAP + (jsl + 1) * 128],
                                         wsd5[:, (fcl // 4) * 4096 +
                                              (fcl % 4) * 1024 + ddh * 512:
                                              (fcl // 4) * 4096 +
                                              (fcl % 4) * 1024 + (ddh + 1) * 512],
                                         start=(fcl == 0), stop=(fcl == 7)),
                                         inc="sPE",
                                         mark=(f"b5:dn{e_}_{fb}_{gd}"
                                               if fcl == 7 else None))
                        c.mark(f"b5:wd{qb + 4}", "sPE")
                        c.mark(f"b5:wd{qb + 5}", "sPE")
                for ddc in range(NC_):
                    for st in range(16):
                        gb = ddc * 16 + st
                        pq = Q[4 + gb % 2]
                        c.wait(eng, "sACT", f"b5:gsil{ddc}_{st}")
                        if gb >= 2:
                            c.wait(eng, "sDVE", f"b5:gfe{(gb - 2) // 16}_{(gb - 2) % 16}")
                        c.op(eng, lambda e, pq=pq, st=st: e.transpose(
                             pq[0:128, 0:128],
                             silb[:, (st % 4) * 128:(st % 4 + 1) * 128],
                             identf[:, :]), inc="sPE", mark=f"b5:gtr{ddc}_{st}")

            def b5_dve(c, eng):
                b5_drain(c, eng)
                c.wait(eng, "sRT", "b5:ld0")
                c.op(eng, lambda e: e.tensor_copy(routerb[:, 0:64],
                                                  stg5[:, 0:64]),
                     inc="sDVE", mark="b5:cv0")
                for sc in range(NSC):
                    s0 = sc * SC
                    c.wait(eng, "sPE", f"b5:ssum{sc}")
                    c.op(eng, lambda e: e.tensor_scalar(
                         st5[0:1, 2 * SC:3 * SC], Q[0][0:1, 0:SC], 1.0 / D, None,
                         ALU.mult), inc="sDVE", mark=f"b5:mn{sc}")
                    c.wait(eng, "sDVE", f"b5:mn{sc}")
                    c.op(eng, lambda e: e.tensor_tensor(
                         st5[0:1, 3 * SC:4 * SC], st5[0:1, 2 * SC:3 * SC],
                         st5[0:1, 2 * SC:3 * SC], ALU.mult), inc="sDVE",
                         mark=f"b5:m2{sc}")
                    c.wait(eng, "sPE", f"b5:sqm{sc}_{NC_ - 1}")
                    c.wait(eng, "sDVE", f"b5:m2{sc}")
                    c.op(eng, lambda e: e.scalar_tensor_tensor(
                         st5[0:1, 3 * SC:4 * SC], Q[1][0:1, 0:SC], 1.0 / D,
                         st5[0:1, 3 * SC:4 * SC], ALU.mult, ALU.subtract),
                         inc="sDVE", mark=f"b5:var{sc}")
                    c.wait(eng, "sACT", f"b5:sqr{sc}")
                    c.op(eng, lambda e: e.reciprocal(st5[0:1, 0:SC],
                                                     st5[0:1, SC:2 * SC]),
                         inc="sDVE", mark=f"b5:rs{sc}")
                    c.wait(eng, "sDVE", f"b5:rs{sc}")
                    c.op(eng, lambda e: e.scalar_tensor_tensor(
                         st5[0:1, SC:2 * SC], st5[0:1, 2 * SC:3 * SC], -1.0,
                         st5[0:1, 0:SC], ALU.mult, ALU.mult), inc="sDVE",
                         mark=f"b5:ab{sc}")
                    c.wait(eng, "sPE", f"b5:bc{sc}")
                    c.op(eng, lambda e: e.tensor_copy(ab5[:, 0:SC], Q[2][:, 0:SC]),
                         inc="sDVE")
                    c.op(eng, lambda e: e.tensor_copy(ab5[:, SC:2 * SC],
                                                      Q[3][:, 0:SC]),
                         inc="sDVE", mark=f"b5:abe{sc}")
                    c.wait(eng, "sDVE", f"b5:abe{sc}")
                    c.wait(eng, "sPLc", "b5:cst")
                    for dc in range(NC_):
                        p2 = (dc % 2) * SC
                        c.op(eng, lambda e, dc=dc, p2=p2, s0=s0: e.tensor_tensor(
                             cl5[:, p2:p2 + SC], nrm[:, dc * T + s0: dc * T + s0 + SC],
                             ab5[:, 0:SC], ALU.mult), inc="sDVE",
                             mark=f"b5:xa{sc}_{dc}")
                        c.wait(eng, "sDVE", f"b5:xa{sc}_{dc}")
                        c.op(eng, lambda e, dc=dc, p2=p2: e.tensor_tensor(
                             sg5[:, p2:p2 + SC], cl5[:, p2:p2 + SC],
                             ab5[:, SC:2 * SC], ALU.add), inc="sDVE",
                             mark=f"b5:xb{sc}_{dc}")
                        c.wait(eng, "sDVE", f"b5:xb{sc}_{dc}")
                        c.op(eng, lambda e, dc=dc, p2=p2, s0=s0: e.tensor_scalar(
                             nrm[:, dc * T + s0: dc * T + s0 + SC],
                             sg5[:, p2:p2 + SC], lngb[:, dc:dc + 1],
                             lngb[:, NC_ + dc:NC_ + dc + 1], ALU.mult, ALU.add),
                             inc="sDVE", mark=f"b5:xnl{sc}_{dc}")
                    c.wait(eng, "sPE", f"b5:lgm{sc}")
                    c.op(eng, lambda e: e.tensor_copy(lg5[0:8, 0:SC],
                                                      Q[4][0:8, 0:SC]),
                         inc="sDVE", mark=f"b5:lge{sc}")
                    for tt_ in range(4):
                        pq = Q[4 + tt_ % 2]
                        c.wait(eng, "sPE", f"b5:tr{sc}_{tt_}")
                        c.op(eng, lambda e, pq=pq, tt_=tt_: e.tensor_copy(
                             gt5[:, tt_ * 8:(tt_ + 1) * 8], pq[:, 0:8]),
                             inc="sDVE", mark=f"b5:gte{sc}_{tt_}")
                        c.wait(eng, "sDVE", f"b5:gte{sc}_{tt_}")
                        g0 = lambda col: gs5[:, col:col + 1]
                        tile = lambda: gt5[:, tt_ * 8:(tt_ + 1) * 8]
                        c.op(eng, lambda e: e.reduce_max(g0(8), tile(), axis=AX,
                                                         negate=True), inc="sDVE",
                             mark=f"b5:c1_{sc}_{tt_}")
                        c.wait(eng, "sACT", f"b5:cexp{sc}_{tt_}")
                        c.op(eng, lambda e: e.reduce_sum(g0(9), tile(), axis=AX),
                             inc="sDVE", mark=f"b5:c2_{sc}_{tt_}")
                        c.wait(eng, "sDVE", f"b5:c2_{sc}_{tt_}")
                        c.op(eng, lambda e: e.reciprocal(g0(10), g0(9)),
                             inc="sDVE", mark=f"b5:c3_{sc}_{tt_}")
                        c.wait(eng, "sDVE", f"b5:c3_{sc}_{tt_}")
                        c.op(eng, lambda e: e.tensor_scalar(tile(), tile(),
                             g0(10), None, ALU.mult), inc="sDVE",
                             mark=f"b5:c4_{sc}_{tt_}")
                        c.wait(eng, "sDVE", f"b5:c4_{sc}_{tt_}")
                        c.op(eng, lambda e: e.reduce_max(g0(11), tile(), axis=AX),
                             inc="sDVE", mark=f"b5:c5_{sc}_{tt_}")
                        c.wait(eng, "sDVE", f"b5:c5_{sc}_{tt_}")
                        c.op(eng, lambda e: e.tensor_scalar(gq5[:, 0:8], tile(),
                             g0(11), None, ALU.is_ge), inc="sDVE",
                             mark=f"b5:c6_{sc}_{tt_}")
                        c.wait(eng, "sDVE", f"b5:c6_{sc}_{tt_}")
                        c.op(eng, lambda e: e.scalar_tensor_tensor(
                             gq5[:, 8:16], gq5[:, 0:8], -1e30, tile(),
                             ALU.mult, ALU.add), inc="sDVE", mark=f"b5:c7_{sc}_{tt_}")
                        c.wait(eng, "sDVE", f"b5:c7_{sc}_{tt_}")
                        c.op(eng, lambda e: e.reduce_max(g0(12), gq5[:, 8:16],
                             axis=AX), inc="sDVE", mark=f"b5:c8_{sc}_{tt_}")
                        c.wait(eng, "sDVE", f"b5:c8_{sc}_{tt_}")
                        c.op(eng, lambda e: e.tensor_scalar(gq5[:, 8:16],
                             gq5[:, 8:16], g0(12), None, ALU.is_ge), inc="sDVE",
                             mark=f"b5:c9_{sc}_{tt_}")
                        c.wait(eng, "sDVE", f"b5:c9_{sc}_{tt_}")
                        c.op(eng, lambda e: e.tensor_tensor(gq5[:, 0:8],
                             gq5[:, 0:8], gq5[:, 8:16], ALU.add), inc="sDVE",
                             mark=f"b5:c10_{sc}_{tt_}")
                        c.op(eng, lambda e: e.tensor_tensor(g0(13), g0(11),
                             g0(12), ALU.add), inc="sDVE", mark=f"b5:c11_{sc}_{tt_}")
                        c.wait(eng, "sDVE", f"b5:c11_{sc}_{tt_}")
                        c.op(eng, lambda e: e.reciprocal(g0(14), g0(13)),
                             inc="sDVE", mark=f"b5:c12_{sc}_{tt_}")
                        c.wait(eng, "sDVE", f"b5:c12_{sc}_{tt_}")
                        c.op(eng, lambda e: e.tensor_scalar(gq5[:, 8:16], tile(),
                             g0(14), None, ALU.mult), inc="sDVE",
                             mark=f"b5:c13_{sc}_{tt_}")
                        c.wait(eng, "sDVE", f"b5:c13_{sc}_{tt_}")
                        c.op(eng, lambda e, sc=sc, tt_=tt_: e.tensor_tensor(
                             comb_tm[:, (sc * 4 + tt_) * 8:(sc * 4 + tt_ + 1) * 8],
                             gq5[:, 8:16], gq5[:, 0:8], ALU.mult), inc="sDVE",
                             mark=f"b5:comb{sc}_{tt_}")
                c.wait(eng, "sDVE", f"b5:comb{NSC - 1}_3")
                c.wait(eng, "sPLc", "b5:cst")
                for kt in range(16):
                    c.op(eng, lambda ee, kt=kt: ee.tensor_copy(
                         rhs3[:, 3 * kt:3 * kt + 1], idsc[:, kt:kt + 1]),
                         inc="sDVE")
                    c.op(eng, lambda ee, kt=kt: ee.tensor_copy(
                         rhs3[:, 3 * kt + 2:3 * kt + 3], ones_f[:, 0:1]),
                         inc="sDVE")
                for e in range(8):
                    for st in range(16):
                        c.op(eng, lambda ee, e=e, st=st: ee.tensor_scalar(
                             mloc[:, st:st + 1],
                             comb_tm[:, st * 8 + e: st * 8 + e + 1], 0.0, None,
                             ALU.is_gt), inc="sDVE",
                             mark=(f"b5:me{e}" if st == 15 else None))
                    c.wait(eng, "sDVE", f"b5:me{e}")
                    c.wait(eng, "sPE", f"b5:pfm{e}")
                    c.op(eng, lambda ee: ee.tensor_copy(strow[0:1, 0:16],
                                                        Q[1][0:1, 0:16]),
                         inc="sDVE", mark=f"b5:tot{e}")
                    c.wait(eng, "sDVE", f"b5:tot{e}")
                    c.op(eng, lambda ee: ee.tensor_tensor_scan(
                         mloc[0:1, 16:32], onesrow_f[0:1, 0:16],
                         strow[0:1, 0:16], 0.0, ALU.mult, ALU.add),
                         inc="sDVE", mark=f"b5:scn{e}")
                    c.wait(eng, "sDVE", f"b5:scn{e}")
                    c.op(eng, lambda ee: ee.tensor_tensor(
                         strow[0:1, 0:16], mloc[0:1, 16:32], strow[0:1, 0:16],
                         ALU.subtract), inc="sDVE", mark=f"b5:exb{e}")
                    c.wait(eng, "sDVE", f"b5:exb{e}")
                    c.op(eng, lambda ee: ee.tensor_copy(mloc[:, 16:32],
                                                        Q[0][:, 0:16]),
                         inc="sDVE", mark=f"b5:pfe{e}")
                    c.wait(eng, "sPE", f"b5:bcb{e}")
                    c.wait(eng, "sDVE", f"b5:pfe{e}")
                    c.op(eng, lambda ee: ee.scalar_tensor_tensor(
                         mloc[:, 32:48], mloc[:, 16:32], -1.0, Q[2][:, 0:16],
                         ALU.add, ALU.add), inc="sDVE", mark=f"b5:loce{e}")
                    c.wait(eng, "sDVE", f"b5:loce{e}")
                    c.op(eng, lambda ee: ee.tensor_scalar(
                         mloc[:, 48:64], mloc[:, 32:48], 768.0, None,
                         ALU.subtract), inc="sDVE", mark=f"b5:ml1{e}")
                    c.wait(eng, "sDVE", f"b5:ml1{e}")
                    c.op(eng, lambda ee: ee.tensor_tensor(
                         mloc[:, 48:64], mloc[:, 48:64], mloc[:, 0:16],
                         ALU.mult), inc="sDVE", mark=f"b5:ml2{e}")
                    c.wait(eng, "sDVE", f"b5:ml2{e}")
                    c.op(eng, lambda ee: ee.tensor_scalar(
                         mloc[:, 48:64], mloc[:, 48:64], 768.0, None,
                         ALU.add), inc="sDVE", mark=f"b5:ml3{e}")
                    c.wait(eng, "sDVE", f"b5:ml3{e}")
                    c.op(eng, lambda ee: ee.tensor_scalar(
                         mloc[:, 48:64], mloc[:, 48:64], 768.0, None,
                         ALU.min), inc="sDVE", mark=f"b5:ml4{e}")
                    c.wait(eng, "sDVE", f"b5:ml4{e}")
                    for kt in range(16):
                        if kt == 0 and e >= 1:
                            c.wait(eng, "sPE", f"b5:tm{e - 1}_{JC - 1}_{15}")
                        c.op(eng, lambda ee, e=e, kt=kt: ee.tensor_copy(
                             rhs3[:, 3 * kt + 1:3 * kt + 2],
                             comb_tm[:, kt * 8 + e: kt * 8 + e + 1]),
                             inc="sDVE",
                             mark=(f"b5:rh{e}" if kt == 15 else None))
                    for sc5 in range(JC):
                        c.op(eng, lambda ee, sc5=sc5: ee.tensor_scalar(
                             mltmp[:, 0:16], mloc[:, 48:64],
                             float(-sc5 * 128), None, ALU.add), inc="sDVE",
                             mark=f"b5:mls{e}_{sc5}")
                        c.wait(eng, "sDVE", f"b5:mls{e}_{sc5}")
                        for kt in range(16):
                            g4 = (e * JC + sc5) * 16 + kt
                            if g4 >= 4:
                                pg4 = g4 - 4
                                c.wait(eng, "sPE",
                                       f"b5:tm{pg4 // 96}_{(pg4 // 16) % JC}"
                                       f"_{pg4 % 16}")
                            c.op(eng, lambda ee, g4=g4, kt=kt: ee.tensor_scalar(
                                 Pbuf[:, (g4 % 4) * 128:(g4 % 4 + 1) * 128],
                                 iot[:, :], mltmp[:, kt:kt + 1], None,
                                 ALU.is_equal), inc="sDVE",
                                 mark=f"b5:pb{e}_{sc5}_{kt}")
                        tcpc = (sc5 % 2) * 3
                        c.wait(eng, "sPE", f"b5:tm{e}_{sc5}_{15}")
                        c.op(eng, lambda ee, e=e, sc5=sc5, tcpc=tcpc:
                             ee.tensor_scalar(
                             tcs[:, e * JC * 2 + sc5 * 2:
                                 e * JC * 2 + sc5 * 2 + 1],
                             Q[4][:, tcpc + 2:tcpc + 3], -2048.0, 2048.0,
                             ALU.mult, ALU.add), inc="sDVE",
                             mark=f"b5:tca{e}_{sc5}")
                        c.wait(eng, "sDVE", f"b5:tca{e}_{sc5}")
                        c.op(eng, lambda ee, e=e, sc5=sc5, tcpc=tcpc:
                             ee.tensor_tensor(
                             tcs[:, e * JC * 2 + sc5 * 2:
                                 e * JC * 2 + sc5 * 2 + 1],
                             tcs[:, e * JC * 2 + sc5 * 2:
                                 e * JC * 2 + sc5 * 2 + 1],
                             Q[4][:, tcpc:tcpc + 1], ALU.add), inc="sDVE",
                             mark=f"b5:tcb{e}_{sc5}")
                        c.op(eng, lambda ee, e=e, sc5=sc5, tcpc=tcpc:
                             ee.tensor_copy(
                             tcs[:, e * JC * 2 + sc5 * 2 + 1:
                                 e * JC * 2 + sc5 * 2 + 2],
                             Q[4][:, tcpc + 1:tcpc + 2]), inc="sDVE",
                             mark=f"b5:tcv{e}_{sc5}")
                    c.wait(eng, "sDVE", f"b5:tcv{e}_{JC - 1}")
                    c.op(eng, lambda ee, e=e: ee.tensor_copy(
                         tidi5[:, e * JC:(e + 1) * JC]
                         .rearrange("p (c k) -> p c k", k=1),
                         tcs[:, e * JC * 2:(e + 1) * JC * 2]
                         .rearrange("p (cc k) -> p cc k", cc=JC)[:, :, 0:1]),
                         inc="sDVE", mark=f"b5:tidc{e}")
                qb = QBS
                for fcl in range(8):
                    for sc in range(NSC):
                        gg = fcl * NSC + sc
                        pg = Q[gg % 2]
                        pu = Q[2 + gg % 2]
                        c.wait(eng, "sPE", f"b5:gm8_0_{gg}")
                        c.op(eng, lambda e, pg=pg, gg=gg: e.tensor_scalar(
                             cl5[:, (gg % 2) * SC:(gg % 2 + 1) * SC],
                             pg[:, 0:SC], 10.0, -10.0, ALU.min, ALU.max),
                             inc="sDVE", mark=f"b5:cl8_0_{gg}")
                        c.wait(eng, "sACT", f"b5:sil8_0_{gg}")
                        c.wait(eng, "sPE", f"b5:um8_0_{gg}")
                        c.op(eng, lambda e, pu=pu, gg=gg, fcl=fcl, sc=sc:
                             e.tensor_tensor(
                             bfA[:, fcl * T + sc * SC: fcl * T + (sc + 1) * SC],
                             sg5[:, (gg % 2) * SC:(gg % 2 + 1) * SC],
                             pu[:, 0:SC], ALU.mult), inc="sDVE",
                             mark=f"b5:hm8_0_{gg}")
                for st in range(16):
                    for ddh in range(2):
                        gd = st * 2 + ddh
                        pd = Q[4 + gd % 2]
                        c.wait(eng, "sPE", f"b5:dn8_0_{gd}")
                        if gd >= 2:
                            c.wait(eng, f"sSH{gd % 2}", f"b5:shs{gd - 2}")
                        c.op(eng, lambda e, pd=pd, gd=gd: e.tensor_copy(
                             shstg[:, (gd % 2) * 512:(gd % 2 + 1) * 512],
                             pd[:, 0:512]), inc="sDVE", mark=f"b5:she{gd}")
                for e_ in range(8):
                    for jc in range(JC):
                        for dc in range(NC_):
                            g = jc * NC_ + dc
                            pq = QT[g % 2]
                            c.wait(eng, "sPE", f"b5:xgt{e_}_{g}")
                            c.op(eng, lambda ee, pq=pq, jc=jc, dc=dc: ee.tensor_copy(
                                 bfA[:, 12288 + dc * CAP + jc * 128:
                                     12288 + dc * CAP + (jc + 1) * 128],
                                 pq[:, 0:128]), inc="sDVE",
                                 mark=f"b5:xfe{e_}_{g}")
                    for fb in range(4):
                        qb = qb_of(e_, fb)
                        for fcl in range(8):
                            for sub in range(2):
                                coff, cw = SUBS[sub]
                                gg = fcl * 2 + sub
                                c.wait(eng, "sPE", f"b5:gm{e_}_{fb}_{fcl}_{sub}")
                                c.op(eng, lambda ee, sub=sub, gg=gg, cw=cw:
                                     ee.tensor_scalar(
                                     cl5[:, (gg % 2) * SC:(gg % 2) * SC + cw],
                                     Q[0 if sub == 0 else 1][:, 0:cw],
                                     10.0, -10.0, ALU.min, ALU.max),
                                     inc="sDVE", mark=f"b5:cl{e_}_{fb}_{fcl}_{sub}")
                                c.wait(eng, "sACT", f"b5:sil{e_}_{fb}_{fcl}_{sub}")
                                c.wait(eng, "sPE", f"b5:um{e_}_{fb}_{fcl}_{sub}")
                                c.op(eng, lambda ee, sub=sub, gg=gg, fcl=fcl,
                                     coff=coff, cw=cw: ee.tensor_tensor(
                                     bfA[:, 18432 + fcl * CAP + coff:
                                         18432 + fcl * CAP + coff + cw],
                                     sg5[:, (gg % 2) * SC:(gg % 2) * SC + cw],
                                     Q[2 if sub == 0 else 3][:, 0:cw],
                                     ALU.mult), inc="sDVE",
                                     mark=f"b5:hm{e_}_{fb}_{fcl}_{sub}")
                        for jsl in range(JC):
                            for ddh in range(2):
                                gd = jsl * 2 + ddh
                                pd = Q[4 + gd % 2]
                                c.wait(eng, "sPE", f"b5:dn{e_}_{fb}_{gd}")
                                if fb == 0:
                                    if e_ >= 1 and gd == 0:
                                        c.wait(eng, "sSA",
                                               f"b5:sa{e_ - 1}_{JC - 1}")
                                    c.op(eng, lambda ee, pd=pd, jsl=jsl, ddh=ddh:
                                         ee.tensor_copy(
                                         y_tm[:, jsl * 1024 + ddh * 512:
                                              jsl * 1024 + (ddh + 1) * 512],
                                         pd[:, 0:512]), inc="sDVE",
                                         mark=f"b5:ye{e_}_{fb}_{gd}")
                                else:
                                    c.op(eng, lambda ee, pd=pd, jsl=jsl, ddh=ddh:
                                         ee.tensor_tensor(
                                         y_tm[:, jsl * 1024 + ddh * 512:
                                              jsl * 1024 + (ddh + 1) * 512],
                                         pd[:, 0:512],
                                         y_tm[:, jsl * 1024 + ddh * 512:
                                              jsl * 1024 + (ddh + 1) * 512],
                                         ALU.add), inc="sDVE",
                                         mark=f"b5:ye{e_}_{fb}_{gd}")
                    for jc in range(JC):
                        c.wait(eng, "sDVE", f"b5:ye{e_}_3_{jc * 2 + 1}")
                        c.op(eng, lambda ee, e_=e_, jc=jc: ee.tensor_scalar(
                             y_tm[:, jc * 1024:(jc + 1) * 1024],
                             y_tm[:, jc * 1024:(jc + 1) * 1024],
                             tcs[:, e_ * JC * 2 + 2 * jc + 1:
                                 e_ * JC * 2 + 2 * jc + 2],
                             None, ALU.mult), inc="sDVE",
                             mark=f"b5:ysc{e_}_{jc}")
                for ddc in range(NC_):
                    for st in range(16):
                        pq = Q[4 + (ddc * 16 + st) % 2]
                        c.wait(eng, "sPE", f"b5:gtr{ddc}_{st}")
                        c.op(eng, lambda e, pq=pq, st=st: e.tensor_copy(
                             bfA[:, st * 128:(st + 1) * 128], pq[0:128, 0:128]),
                             inc="sDVE", mark=f"b5:gfe{ddc}_{st}")
                    for sc in range(NSC):
                        gy = ddc * NSC + sc
                        c.wait(eng, "sDVE", f"b5:gfe{ddc}_{(sc + 1) * 4 - 1}")
                        c.wait(eng, f"sST{gy % 2}", f"b5:s1l{gy}")
                        c.op(eng, lambda e, sc=sc, gy=gy: e.tensor_tensor(
                             xt5[:, (gy % 2) * SC:(gy % 2 + 1) * SC],
                             bfA[:, sc * SC:(sc + 1) * SC],
                             bfA[:, 2048 + (gy % 2) * SC: 2048 + (gy % 2 + 1) * SC],
                             ALU.mult),
                             inc="sDVE", mark=f"b5:cm1_{ddc}_{sc}")
                        c.wait(eng, ["sAC0", "sAC1"][gy % 2], f"b5:x1l{gy}")
                        c.wait(eng, "sDVE", f"b5:cm1_{ddc}_{sc}")
                        if gy >= 2:
                            c.wait(eng, f"sYS{gy % 2}", f"b5:ys{gy - 2}")
                        c.op(eng, lambda e, gy=gy: e.tensor_tensor(
                             stg5[:, 1280 + (gy % 2) * SC: 1280 + (gy % 2 + 1) * SC],
                             xt5[:, (gy % 2) * SC:(gy % 2 + 1) * SC],
                             stg5[:, 256 + (gy % 2) * SC: 256 + (gy % 2 + 1) * SC],
                             ALU.add),
                             inc="sDVE", mark=f"b5:yv{ddc}_{sc}")
                        c.mark(f"b5:cm2_{ddc}_{sc}", "sDVE")

            def b5_act(c, eng):
                b5_drain(c, eng)
                for sc in range(NSC):
                    s0 = sc * SC
                    for dc in range(NC_):
                        gdc2 = sc * NC_ + dc
                        if gdc2 >= 2:
                            psc, pdc = divmod(gdc2 - 2, NC_)
                            c.wait(eng, "sPE", f"b5:sqm{psc}_{pdc}")
                        c.op(eng, lambda e, dc=dc, s0=s0: e.activation(
                             sqt5[:, (dc % 2) * SC:(dc % 2 + 1) * SC],
                             nrm[:, dc * T + s0: dc * T + s0 + SC], AF.Square),
                             inc="sACT", mark=f"b5:sq{sc}_{dc}")
                    c.wait(eng, "sDVE", f"b5:var{sc}")
                    c.op(eng, lambda e: e.activation(
                         st5[0:1, SC:2 * SC], st5[0:1, 3 * SC:4 * SC], AF.Sqrt,
                         bias=cvals[0:1, 0:1]), inc="sACT", mark=f"b5:sqr{sc}")
                    for tt_ in range(4):
                        c.wait(eng, "sDVE", f"b5:c1_{sc}_{tt_}")
                        c.op(eng, lambda e, tt_=tt_: e.activation(
                             gt5[:, tt_ * 8:(tt_ + 1) * 8],
                             gt5[:, tt_ * 8:(tt_ + 1) * 8], AF.Exp,
                             bias=gs5[:, 8:9]), inc="sACT",
                             mark=f"b5:cexp{sc}_{tt_}")
                for st in range(16):
                    if st >= 2:
                        c.wait(eng, f"sXT{st % 2}", f"b5:xts{st - 2}")
                    for dc in range(NC_):
                        g = st * NC_ + dc
                        pq = QT[g % 2]
                        c.wait(eng, "sPE", f"b5:xtr{st}_{dc}")
                        c.op(eng, lambda ee, pq=pq, st=st, dc=dc: ee.copy(
                             xstg[:, (st % 2) * 1024 + dc * 128:
                                  (st % 2) * 1024 + (dc + 1) * 128],
                             pq[:, 0:128]), inc="sACT", mark=f"b5:xge{st}_{dc}")
                    c.wait(eng, "sACT", f"b5:xge{st}_{NC_ - 1}")
                    c.dma(eng, d_xnl[st * 128:(st + 1) * 128, :],
                          xstg[:, (st % 2) * 1024:(st % 2 + 1) * 1024],
                          f"sXT{st % 2}", mark=f"b5:xts{st}")
                for fcl in range(8):
                    for sc in range(NSC):
                        gg = fcl * NSC + sc
                        c.wait(eng, "sDVE", f"b5:cl8_0_{gg}")
                        if gg >= 2:
                            c.wait(eng, "sDVE", f"b5:hm8_0_{gg - 2}")
                        c.op(eng, lambda e, gg=gg: e.activation(
                             sg5[:, (gg % 2) * SC:(gg % 2 + 1) * SC],
                             cl5[:, (gg % 2) * SC:(gg % 2 + 1) * SC],
                             AF.Silu), inc="sACT",
                             mark=f"b5:sil8_0_{gg}")
                for st in range(16):
                    for ddh in range(2):
                        gd = st * 2 + ddh
                        c.wait(eng, "sDVE", f"b5:she{gd}")
                        c.dma(eng, d_acc[st * 128:(st + 1) * 128,
                                         ddh * 512:(ddh + 1) * 512],
                              shstg[:, (gd % 2) * 512:(gd % 2 + 1) * 512],
                              f"sSH{gd % 2}", mark=f"b5:shs{gd}")
                for e_ in range(8):
                    for fb in range(4):
                        for fcl in range(8):
                            for sub in range(2):
                                gg = fcl * 2 + sub
                                cw = SUBS[sub][1]
                                c.wait(eng, "sDVE", f"b5:cl{e_}_{fb}_{fcl}_{sub}")
                                if gg >= 2:
                                    pfcl, psub = divmod(gg - 2, 2)
                                    c.wait(eng, "sDVE",
                                           f"b5:hm{e_}_{fb}_{pfcl}_{psub}")
                                c.op(eng, lambda e, gg=gg, cw=cw: e.activation(
                                     sg5[:, (gg % 2) * SC:(gg % 2) * SC + cw],
                                     cl5[:, (gg % 2) * SC:(gg % 2) * SC + cw],
                                     AF.Silu), inc="sACT",
                                     mark=f"b5:sil{e_}_{fb}_{fcl}_{sub}")
                for ddc in range(NC_):
                    c.wait(eng, ["sAC0", "sAC1"][ddc % 2], f"b5:accl{ddc}")
                    for st in range(16):
                        gb = ddc * 16 + st
                        if gb >= 4:
                            c.wait(eng, "sPE",
                                   f"b5:gtr{(gb - 4) // 16}_{(gb - 4) % 16}")
                        c.op(eng, lambda e, ddc=ddc, st=st: e.activation(
                             silb[:, (st % 4) * 128:(st % 4 + 1) * 128],
                             y_tm[:, (ddc % 3) * 2048 + st * 128:
                                  (ddc % 3) * 2048 + (st + 1) * 128],
                             AF.Silu), inc="sACT", mark=f"b5:gsil{ddc}_{st}")

            ctx.run_block([
                ("sync", b5_sp),
                ("gpsimd", b5_cv),
                ("gpsimd", b5_pl),
                ("tensor", b5_pe),
                ("vector", b5_dve),
                ("scalar", b5_act),
            ])
        elif "xn" not in dbg_outs:
            def fin(c, eng):
                for dc in range(NC_):
                    c.dma(eng, y_out[dc * 128:(dc + 1) * 128, :],
                          nrm[:, dc * T:(dc + 1) * T], "sPL", mark=f"y{dc}")
                c.wait(eng, "sPL", f"y{NC_ - 1}")
            ctx.run_block([("gpsimd", fin)])

    return nc



def _stage_shared(inputs):
    f = np.float32
    cat = np.concatenate
    s = {}
    s["w1cat"] = np.ascontiguousarray(cat([
        inputs["L0_attn_qk_Wq1"], inputs["L0_attn_qk_Wm1"],
        inputs["L0_attn_qk_Wk1"], inputs["L0_attn_qks_Wk1"],
        inputs["L0_attn_qks_Wq1"], inputs["L0_attn_qks_Wm1"],
        inputs["L0_attn_vo_Wv1"], inputs["L0_attn_vo_Wm1"]], axis=1), f)
    s["w2pair0"] = np.ascontiguousarray(cat([
        inputs["L0_attn_qk_Wq2"], inputs["L0_attn_qk_Wm2"]], axis=0), f)
    s["w2pair1"] = np.ascontiguousarray(cat([
        inputs["L0_attn_qk_Wk2"], inputs["L0_attn_qks_Wk2"]], axis=0), f)
    s["w2pair2"] = np.ascontiguousarray(cat([
        inputs["L0_attn_qks_Wq2"], inputs["L0_attn_qks_Wm2"]], axis=0), f)
    s["w2pair3"] = np.ascontiguousarray(cat([
        inputs["L0_attn_vo_Wv2"], inputs["L0_attn_vo_Wm2"]], axis=0), f)
    s["wacat"] = np.ascontiguousarray(cat([
        inputs["L0_attn_sm1_Wa"], inputs["L0_attn_sm1s_Wa"],
        inputs["L0_attn_sm2_Wa"]], axis=1), f)
    s["bdcat"] = np.ascontiguousarray(cat([
        inputs["L0_attn_sm1_Wb"], inputs["L0_attn_sm1s_Wb"],
        inputs["L0_attn_sm2_Wb"], inputs["L0_attn_sm1_Wd"],
        inputs["L0_attn_sm1s_Wd"], inputs["L0_attn_sm2_Wd"]], axis=1), f)
    b16 = lambda a: np.ascontiguousarray(np.asarray(a, f).astype(BF16NP))
    s["woT"] = b16(np.asarray(inputs["L0_attn_vo_Wo"], f).T)
    s["wg"] = b16(inputs["L0_ffn_Wg"])
    s["w1"] = b16(inputs["L0_ffn_w1"])
    s["w2"] = b16(inputs["L0_ffn_w2"])
    s["l1w0"] = b16(np.asarray(inputs["L1_attn_W_delta"], f).T)
    s["l1w1"] = b16(np.asarray(inputs["L1_attn_W_select"], f).T)
    s["l1w2"] = b16(np.asarray(inputs["L1_attn_W_in"], f).T)
    s["l1w3"] = b16(np.asarray(inputs["L1_attn_W_gate"], f).T)
    s["woutT"] = b16(np.asarray(inputs["L1_attn_W_out"], f).T)
    s["log_a"] = np.ascontiguousarray(
        np.asarray(inputs["L1_attn_log_a"], f).reshape(NC_, 128).T, f)
    s["ln_g"] = np.ascontiguousarray(
        np.asarray(inputs["L1_ffn_ln_gamma"], f).reshape(NC_, 128).T, f)
    s["ln_b"] = np.ascontiguousarray(
        np.asarray(inputs["L1_ffn_ln_beta"], f).reshape(NC_, 128).T, f)
    s["router"] = np.ascontiguousarray(inputs["L1_ffn_router_w"], f)
    s["guw"] = np.ascontiguousarray(
        np.asarray(inputs["L1_ffn_gate_up_w"], np.float32).astype(BF16NP))
    s["dnw"] = np.ascontiguousarray(
        np.asarray(inputs["L1_ffn_down_w"], np.float32).astype(BF16NP))
    s["sguw"] = np.ascontiguousarray(
        np.asarray(inputs["L1_ffn_shared_gate_up_w"], np.float32).astype(BF16NP))
    s["sdnw"] = np.ascontiguousarray(
        np.asarray(inputs["L1_ffn_shared_down_w"], np.float32).astype(BF16NP))
    return s


def kernel(**inputs):
    x = np.asarray(inputs["L0_attn_x"], np.float32)
    shared = _stage_shared(inputs)
    nc = build_nc()
    in_maps = []
    for b in range(8):
        m = dict(shared)
        m["xT"] = np.ascontiguousarray(x[b].T)
        in_maps.append(m)
    res = run_bass_kernel_spmd(nc, in_maps, core_ids=list(range(8)))
    out = np.stack([np.asarray(res.results[b]["y"]).T for b in range(8)], axis=0)
    return out.astype(np.float32)


if __name__ == "__main__":
    pass


# version 15 (final)
# speedup vs baseline: 1.0130x; 1.0130x over previous
import contextlib
import numpy as np
import ml_dtypes
import concourse.bass as bass
import concourse.mybir as mybir
from concourse.bass_utils import run_bass_kernel_spmd

F32 = mybir.dt.float32
BF16 = mybir.dt.bfloat16
I32 = mybir.dt.int32
AF = mybir.ActivationFunctionType
ALU = mybir.AluOpType
AX = mybir.AxisListType.X

D = 1024
T = 2048
NC_ = 8
R = 64
RS = 16
WI = 256
NIB = T // WI
SC = 512
NSC = T // SC
F32EPS = float(np.finfo(np.float32).eps)

BF16NP = ml_dtypes.bfloat16
B2_NIB = NIB
B2_LVL = 6
B2_PATHS = 2
B3_ON = True
B4_ON = True
B5_ON = True
B5_SER = False
CAP = 768
JC = CAP // 128
SUBS = ((0, 512), (512, 256))
TAG_B2_FULL = True
NIBG = None



class Ctx:

    def __init__(self, nc):
        self.nc = nc
        self.cnt = {}
        self.sems = {}
        self.marks = {}
        self.emitting = False
        self.cur = None

    def sem(self, name):
        h = self.nc.alloc_semaphore(name)
        self.sems[name] = h
        self.cnt[name] = 0
        return h

    def op(self, eng, fn, inc=None, n=1, mark=None):
        if not self.emitting:
            if inc is not None:
                self.cnt[inc] += n
                if mark is not None:
                    self.marks[mark] = self.cnt[inc]
                if self.cur is not None:
                    self.cur.append(("inc", inc, n, mark))
            return None
        instr = fn(eng)
        if inc is not None:
            self.cnt[inc] += n
            instr.then_inc(self.sems[inc], n)
        return instr

    def dma(self, eng, out, in_, inc, mark=None):
        return self.op(eng, lambda e: e.dma_start(out=out, in_=in_),
                       inc=inc, n=16, mark=mark)

    def wait(self, eng, sem, mark=None, value=None):
        if not self.emitting:
            if self.cur is not None:
                self.cur.append(("wait", mark if mark is not None else value,
                                 sem, mark))
            return
        v = value if value is not None else self.marks[mark]
        eng.wait_ge(self.sems[sem], v)

    def mark(self, key, sem):
        if not self.emitting:
            self.marks[key] = self.cnt[sem]

    def run_block(self, bodies):
        saved = dict(self.cnt)
        self.emitting = False
        self.rec = []
        self.cur = None
        for name, fn in bodies:
            self.cur = []
            self.rec.append((name, self.cur))
            fn(self, None)
        self.cur = None
        self._check_deadlock(saved)
        p1end = dict(self.cnt)
        self.cnt = saved
        self.emitting = True
        with self.nc.Block() as block:
            for (name, fn), (_, ev) in zip(bodies, self.rec):
                if not ev:
                    continue
                reg = getattr(block, name)
                reg(lambda eng, fn=fn: fn(self, eng))
        self.emitting = False
        if self.cnt != p1end:
            diff = {k: (p1end.get(k), self.cnt.get(k))
                    for k in set(p1end) | set(self.cnt)
                    if p1end.get(k) != self.cnt.get(k)}
            raise RuntimeError(f"pass1/pass2 count divergence: {diff}")

    def _check_deadlock(self, base):
        streams = {}
        for name, ev in self.rec:
            streams.setdefault(name, []).extend(ev)
        pos = {k: 0 for k in streams}
        val = dict(base)
        progress = True
        while progress:
            progress = False
            for k, evs in streams.items():
                while pos[k] < len(evs):
                    e = evs[pos[k]]
                    if e[0] == "inc":
                        val[e[1]] = val.get(e[1], 0) + e[2]
                        pos[k] += 1
                        progress = True
                    else:
                        want = self.marks[e[1]] if isinstance(e[1], str) else e[1]
                        sem = e[2]
                        if val.get(sem, 0) >= want:
                            pos[k] += 1
                            progress = True
                        else:
                            break
        stuck = {k: pos[k] for k in streams if pos[k] < len(streams[k])}
        if stuck:
            lines = []
            for k in stuck:
                e = streams[k][pos[k]]
                want = self.marks[e[1]] if isinstance(e[1], str) else e[1]
                lines.append(f"  {k}@{pos[k]}/{len(streams[k])}: waiting "
                             f"{e[2]} >= {want} (now {val.get(e[2], 0)}) "
                             f"mark={e[3] if len(e) > 3 else ''}")
            raise RuntimeError("DEADLOCK detected in block:\n" + "\n".join(lines))



def build_nc(taps=()):
    global NIBG
    NIBG = B2_NIB * NC_
    nc = bass.Bass("TRN2", target_bir_lowering=False)
    ctx = Ctx(nc)
    for s in ["sSP", "sPL", "sPE", "sDVE", "sACT", "sPLc",
              "sLD0", "sLD1", "sST0", "sST1", "sPLr", "sPLd",
              "sYS0", "sYS1", "sPOOL", "sIDX", "sGA", "sSA",
              "sSH0", "sSH1", "sXT0", "sXT1", "sAC0", "sAC1", "sIN", "sRT"]:
        ctx.sem(s)

    dt_in = {}

    def din(name, shape, dtype=F32):
        dt_in[name] = nc.dram_tensor(name, shape, dtype, kind="ExternalInput").ap()
        return dt_in[name]

    xT = din("xT", [D, T])
    w1cat = din("w1cat", [D, 512])
    w2pair = [din(f"w2pair{i}", [128, D]) for i in range(4)]
    wacat = din("wacat", [D, 3 * RS])
    bdcat = din("bdcat", [D, 51])
    woT = din("woT", [D, D], BF16)
    wg = din("wg", [D, 8], BF16)
    w1 = din("w1", [D, 4096], BF16)
    w2 = din("w2", [4096, D], BF16)
    l1w = [din(f"l1w{i}", [D, D], BF16) for i in range(4)]
    woutT = din("woutT", [D, D], BF16)
    log_a = din("log_a", [128, NC_])
    ln_g = din("ln_g", [128, NC_])
    ln_b = din("ln_b", [128, NC_])
    router = din("router", [D, 8])
    guw = din("guw", [8, D, 8192], BF16)
    dnw = din("dnw", [8, 4096, D], BF16)
    sguw = din("sguw", [D, 2048], BF16)
    sdnw = din("sdnw", [D, D], BF16)

    y_out = nc.dram_tensor("y", [D, T], F32, kind="ExternalOutput").ap()

    d_spre = nc.dram_tensor("d_spre", [D, T], BF16, kind="Internal").ap()
    d_x1 = nc.dram_tensor("d_x1", [D, T], F32, kind="Internal").ap()
    d_s1 = nc.dram_tensor("d_s1", [D, T], BF16, kind="Internal").ap()
    d_recipn = nc.dram_tensor("d_recipn", [1, T], F32, kind="Internal").ap()
    d_xnl = nc.dram_tensor("d_xnl", [T + 1, D], BF16, kind="Internal").ap()
    d_acc = nc.dram_tensor("d_acc", [T + 1, D], F32, kind="Internal").ap()

    ident_np = np.eye(128, dtype=BF16NP)
    ident_t = nc.inline_tensor(ident_np, name="identbf")
    ones_f_t = nc.inline_tensor(np.ones((128, 1), np.float32), name="onesf")
    ones_b_t = nc.inline_tensor(np.ones((128, 1), BF16NP), name="onesb")
    onesrow_f_t = nc.inline_tensor(np.ones((1, 128), np.float32), name="onesrowf")
    cvals_np = np.broadcast_to(
        np.array([1e-5, 1e-8, 1e-4, 10.0], np.float32), (128, 4)).copy()
    cvals_t = nc.inline_tensor(cvals_np, name="cvals_src")
    identf_t = nc.inline_tensor(np.eye(128, dtype=np.float32), name="identf_src")
    utf_t = nc.inline_tensor(
        np.triu(np.ones((128, 128), np.float32)).T.copy() if False else
        (np.arange(128)[:, None] <= np.arange(128)[None, :]).astype(np.float32),
        name="utf_src")
    idsc_t = nc.inline_tensor(
        (np.arange(16)[None, :] * 128 + np.arange(128)[:, None])
        .astype(np.float32).copy(), name="idsc_src")
    iot_t = nc.inline_tensor(
        np.broadcast_to(np.arange(128, dtype=np.float32), (128, 128)).copy(),
        name="iot_src")
    zrow_t = nc.inline_tensor(np.zeros((1, 1024), BF16NP), name="zrow_src")
    from concourse.bass import IndirectOffsetOnAxis

    tap_bufs = {}
    dbg_outs = {}
    for tname, shape, dtype in taps:
        dbg_outs[tname] = nc.dram_tensor("dbg_" + tname, shape, dtype,
                                         kind="ExternalOutput").ap()

    with contextlib.ExitStack() as st_top:
        sb = lambda st, name, shape, dt: st.enter_context(
            nc.sbuf_tensor(name, shape, dt))
        ps = lambda st, name, shape: st.enter_context(
            nc.psum_tensor(name, shape, F32))
        ident = sb(st_top, "ident", [128, 128], BF16)
        ones_f = sb(st_top, "ones_f", [128, 1], F32)
        ones_b = sb(st_top, "ones_b", [128, 1], BF16)
        onesrow_f = sb(st_top, "onesrow_f", [1, 128], F32)
        cvals = sb(st_top, "cvals", [128, 4], F32)
        nrm = sb(st_top, "nrm", [128, NC_ * T], BF16)
        identf = sb(st_top, "identf", [128, 128], F32)
        with contextlib.ExitStack() as st0:
            xfull = sb(st0, "xfull", [128, NC_ * T], F32)
            sqf = sb(st0, "sqf", [128, NC_ * T], BF16)
            arow = sb(st0, "arow", [1, T], F32)
            brow = sb(st0, "brow", [1, T], F32)
            stat = sb(st0, "stat", [1, 4 * SC], F32)
            ab_sb = sb(st0, "ab_sb", [128, SC], F32)
            bb_sb = sb(st0, "bb_sb", [128, SC], F32)
            xnt = sb(st0, "xnt", [128, 2 * SC], F32)
            ps_sum = ps(st0, "ps_sum", [128, 512])
            ps_sq = ps(st0, "ps_sq", [128, 512])
            ps_a = ps(st0, "ps_a", [128, 512])
            ps_b = ps(st0, "ps_b", [128, 512])
            def b0_sp(c, eng):
                c.dma(eng, ident[:, :], ident_t.ap(), "sSP")
                c.dma(eng, ones_f[:, :], ones_f_t.ap(), "sSP")
                c.dma(eng, ones_b[:, :], ones_b_t.ap(), "sSP")
                c.dma(eng, onesrow_f[:, :], onesrow_f_t.ap(), "sSP")
                c.dma(eng, cvals[:, :], cvals_t.ap(), "sSP")
                for dc in range(NC_):
                    c.dma(eng, xfull[:, dc * T:(dc + 1) * T],
                          xT[dc * 128:(dc + 1) * 128, :], "sSP",
                          mark=("b0:all" if dc == NC_ - 1 else None))

            def b0_act(c, eng):
                c.wait(eng, "sSP", "b0:all")
                for dc in range(NC_):
                    c.op(eng, lambda e, dc=dc: e.activation(
                        sqf[:, dc * T:(dc + 1) * T], xfull[:, dc * T:(dc + 1) * T],
                        AF.Square), inc="sACT", mark=f"sq{dc}")
                for sc in range(NSC):
                    c.wait(eng, "sDVE", f"var{sc}")
                    c.op(eng, lambda e, sc=sc: e.activation(
                        stat[0:1, 3 * SC:4 * SC], stat[0:1, 2 * SC:3 * SC],
                        AF.Sqrt, bias=cvals[0:1, 0:1]), inc="sACT", mark=f"sqrt{sc}")

            def b0_pe(c, eng):
                for sc in range(NSC):
                    s0 = sc * SC
                    if sc > 0:
                        c.wait(eng, "sDVE", f"mean{sc - 1}")
                        c.wait(eng, "sDVE", f"var{sc - 1}")
                        c.wait(eng, "sDVE", f"absb{sc - 1}")
                    c.wait(eng, "sSP", "b0:all")
                    for dc in range(NC_):
                        c.op(eng, lambda e, dc=dc, s0=s0: e.matmul(
                            ps_sum[0:1, 0:SC], ones_f[:, 0:1],
                            xfull[:, dc * T + s0: dc * T + s0 + SC],
                            start=(dc == 0), stop=(dc == NC_ - 1)),
                            inc="sPE", mark=(f"sum{sc}" if dc == NC_ - 1 else None))
                    for dc in range(NC_):
                        c.wait(eng, "sACT", f"sq{dc}")
                        c.op(eng, lambda e, dc=dc, s0=s0: e.matmul(
                            ps_sq[0:1, 0:SC], ones_b[:, 0:1],
                            sqf[:, dc * T + s0: dc * T + s0 + SC],
                            start=(dc == 0), stop=(dc == NC_ - 1)),
                            inc="sPE", mark=(f"sumsq{sc}" if dc == NC_ - 1 else None))
                    c.wait(eng, "sDVE", f"ab{sc}")
                    c.op(eng, lambda e, s0=s0: e.matmul(
                        ps_a[:, 0:SC], onesrow_f[0:1, :], arow[0:1, s0:s0 + SC],
                        start=True, stop=True), inc="sPE")
                    c.op(eng, lambda e, s0=s0: e.matmul(
                        ps_b[:, 0:SC], onesrow_f[0:1, :], brow[0:1, s0:s0 + SC],
                        start=True, stop=True), inc="sPE", mark=f"bc{sc}")

            def b0_dve(c, eng):
                for sc in range(NSC):
                    s0 = sc * SC
                    c.wait(eng, "sPE", f"sum{sc}")
                    c.op(eng, lambda e: e.tensor_scalar(
                        stat[0:1, 0:SC], ps_sum[0:1, 0:SC], 1.0 / D, None,
                        ALU.mult), inc="sDVE", mark=f"mean{sc}")
                    c.wait(eng, "sDVE", f"mean{sc}")
                    c.op(eng, lambda e: e.tensor_tensor(
                        stat[0:1, SC:2 * SC], stat[0:1, 0:SC], stat[0:1, 0:SC],
                        ALU.mult), inc="sDVE", mark=f"m2_{sc}")
                    c.wait(eng, "sPE", f"sumsq{sc}")
                    c.wait(eng, "sDVE", f"m2_{sc}")
                    c.op(eng, lambda e: e.scalar_tensor_tensor(
                        stat[0:1, 2 * SC:3 * SC], ps_sq[0:1, 0:SC], 1.0 / D,
                        stat[0:1, SC:2 * SC], ALU.mult, ALU.subtract), inc="sDVE",
                        mark=f"var{sc}")
                    c.wait(eng, "sACT", f"sqrt{sc}")
                    c.op(eng, lambda e, s0=s0: e.reciprocal(
                        arow[0:1, s0:s0 + SC], stat[0:1, 3 * SC:4 * SC]),
                        inc="sDVE", mark=f"r{sc}")
                    c.wait(eng, "sDVE", f"r{sc}")
                    c.op(eng, lambda e, s0=s0: e.scalar_tensor_tensor(
                        brow[0:1, s0:s0 + SC], stat[0:1, 0:SC], -1.0,
                        arow[0:1, s0:s0 + SC], ALU.mult, ALU.mult), inc="sDVE",
                        mark=f"ab{sc}")
                    c.wait(eng, "sPE", f"bc{sc}")
                    c.op(eng, lambda e: e.tensor_copy(ab_sb[:, 0:SC], ps_a[:, 0:SC]),
                         inc="sDVE")
                    c.op(eng, lambda e: e.tensor_copy(bb_sb[:, 0:SC], ps_b[:, 0:SC]),
                         inc="sDVE", mark=f"absb{sc}")
                    c.wait(eng, "sDVE", f"absb{sc}")
                    for dc in range(NC_):
                        o = dc * T + s0
                        p = (dc % 2) * SC
                        c.op(eng, lambda e, o=o, p=p: e.tensor_tensor(
                            xnt[:, p:p + SC], xfull[:, o:o + SC], ab_sb[:, 0:SC],
                            ALU.mult), inc="sDVE", mark=f"xm{sc}_{dc}")
                        c.wait(eng, "sDVE", f"xm{sc}_{dc}")
                        c.op(eng, lambda e, o=o, p=p: e.tensor_tensor(
                            nrm[:, o:o + SC], xnt[:, p:p + SC], bb_sb[:, 0:SC],
                            ALU.add), inc="sDVE", mark=f"xn{sc}_{dc}")

            ctx.run_block([
                ("sync", b0_sp),
                ("scalar", b0_act),
                ("tensor", b0_pe),
                ("vector", b0_dve),
            ])

            if "xn" in dbg_outs:
                def tdma(c, eng):
                    c.dma(eng, dbg_outs["xn"], nrm[:, :], "sPL", mark="tapxn")
                    c.wait(eng, "sPL", "tapxn")
                ctx.run_block([("gpsimd", tdma)])

        if "xn" not in dbg_outs:
          with contextlib.ExitStack() as stL0:
            p1_all = sb(stL0, "p1_all", [128, 4 * T], BF16)
            kb = sb(stL0, "kb", [128, NC_ * T], BF16)
            ksb = sb(stL0, "ksb", [128, NC_ * T], BF16)
            v_tm = sb(stL0, "v_tm", [128, 16 * D], BF16)
            w2p = sb(stL0, "w2p", [128, 4 * D], BF16)
            apack = sb(stL0, "apack", [128, T], BF16)
            bm_tm = sb(stL0, "bm_tm", [128, 16 * 48], BF16)
            d_tm = sb(stL0, "d_tm", [128, 16 * 3], F32)
            zbp = sb(stL0, "zbp", [128, WI], BF16)
            zbufT = sb(stL0, "zbufT", [128, 16 * WI], BF16)
            zbufT2 = sb(stL0, "zbufT2", [128, 16 * WI], BF16)
            pbuf = sb(stL0, "pbuf", [128, NC_ * WI], BF16)
            scr = sb(stL0, "scr", [128, 2048], F32)
            sqb = sb(stL0, "sqb", [128, 8 * WI], BF16)
            spre_t = sb(stL0, "spre_t", [128, 2 * WI], BF16)
            nstat = sb(stL0, "nstat", [1, 3 * WI], F32)
            spb = sb(stL0, "spb", [128, 4 * WI], BF16)
            spb2 = sb(stL0, "spb2", [128, 4 * WI], BF16)
            P0 = ps(stL0, "P0", [128, 512])
            P1 = ps(stL0, "P1", [128, 512])
            P2 = ps(stL0, "P2", [128, 512])
            P3 = ps(stL0, "P3", [128, 512])
            P4 = ps(stL0, "P4", [128, 512])
            P5 = ps(stL0, "P5", [128, 512])
            P6 = ps(stL0, "P6", [128, 512])
            P7 = ps(stL0, "P7", [128, 512])
            if True:
                w1b = v_tm
                wab = pbuf
                BDO = 512
                def b1_sp(c, eng):
                    pieces = []
                    for i in range(4):
                        pieces.append([(w1cat[(2 * i) * 128:(2 * i + 1) * 128, :], 0, 512),
                                       (w1cat[(2 * i + 1) * 128:(2 * i + 2) * 128, :], 512, 512)])
                    for p in range(4):
                        pieces.append([(w2pair[p], 0, 1024)])
                    pieces.append([(wacat[dc * 128:(dc + 1) * 128, :], dc * 48, 48)
                                   for dc in range(NC_)])
                    pieces.append([(bdcat[dc * 128:(dc + 1) * 128, :], dc * 51, 51)
                                   for dc in range(NC_)])
                    for i, dmas in enumerate(pieces):
                        h = (i % 2) * 1024
                        lane = f"sLD{i % 2}"
                        if i >= 2:
                            c.wait(eng, "sDVE", f"b1:cv{i - 2}")
                        for src_ap, off, w in dmas:
                            c.dma(eng, scr[:, h + off:h + off + w], src_ap,
                                  lane, mark=f"b1:ld{i}")

                def b1_cv(c, eng):
                    dsts = [
                        (w1b, 0, 1024), (w1b, 1024, 1024),
                        (w1b, 2048, 1024), (w1b, 3072, 1024),
                        (w2p, 0, 1024), (w2p, 1024, 1024),
                        (w2p, 2048, 1024), (w2p, 3072, 1024),
                        (wab, 0, 384), (pbuf, BDO, 408),
                    ]
                    for i, (dst, off, w) in enumerate(dsts):
                        h = (i % 2) * 1024
                        c.wait(eng, f"sLD{i % 2}", f"b1:ld{i}")
                        c.op(eng, lambda e, dst=dst, off=off, w=w, h=h:
                             e.tensor_copy(dst[:, off:off + w], scr[:, h:h + w]),
                             inc="sDVE", mark=f"b1:cv{i}")
                    if True:
                        c.mark("b1:w1b", "sDVE")
                        if not c.emitting:
                            c.marks["b1:w1b"] = c.marks["b1:cv3"]
                            c.marks["b1:w2p"] = c.marks["b1:cv7"]
                            c.marks["b1:wab"] = c.marks["b1:cv8"]
                            c.marks["b1:bdb"] = c.marks["b1:cv9"]

                def b1_pe(c, eng):
                    c.wait(eng, "sDVE", "b1:w1b")
                    gi = 0
                    for mc in range(4):
                        for sc in range(NSC):
                            pb = [P0, P1][gi % 2]
                            if gi >= 2:
                                c.wait(eng, "sDVE", f"b1:p1e{gi - 2}")
                            for dc in range(NC_):
                                c.op(eng, lambda e, pb=pb, mc=mc, sc=sc, dc=dc:
                                     e.matmul(
                                     pb[:, 0:512],
                                     w1b[:, dc * 512 + mc * 128: dc * 512 + (mc + 1) * 128],
                                     nrm[:, dc * T + sc * 512: dc * T + (sc + 1) * 512],
                                     start=(dc == 0), stop=(dc == NC_ - 1)),
                                     inc="sPE",
                                     mark=(f"b1:p1m{gi}" if dc == NC_ - 1 else None))
                            gi += 1
                    c.wait(eng, "sDVE", "b1:w2p")
                    c.wait(eng, "sDVE", "b1:p1e7")
                    gi = 0
                    for half in range(2):
                        r0 = half * 64
                        for oc in range(NC_):
                            for sc in range(NSC):
                                pb = [P2, P3][gi % 2]
                                if gi >= 2:
                                    c.wait(eng, "sDVE", f"b1:ke{gi - 2}")
                                c.op(eng, lambda e, pb=pb, r0=r0, oc=oc, sc=sc:
                                     e.matmul(
                                     pb[:, 0:512],
                                     w2p[r0:r0 + 64, D + oc * 128: D + (oc + 1) * 128],
                                     p1_all[r0:r0 + 64, T + sc * 512: T + (sc + 1) * 512],
                                     start=True, stop=True),
                                     inc="sPE", mark=f"b1:km{gi}")
                                gi += 1
                    c.wait(eng, "sDVE", "b1:p1e15")
                    gi = 64
                    for jc in range(16):
                        for half in range(2):
                            pb = [P2, P3][gi % 2]
                            c.wait(eng, "sDVE", f"b1:ke{gi - 2}")
                            c.op(eng, lambda e, pb=pb, jc=jc, half=half: e.matmul(
                                 pb[:, 0:512],
                                 p1_all[0:64, 3 * T + jc * 128: 3 * T + (jc + 1) * 128],
                                 w2p[0:64, 3 * D + half * 512: 3 * D + (half + 1) * 512],
                                 start=True, stop=True),
                                 inc="sPE", mark=f"b1:km{gi}")
                            gi += 1
                    c.wait(eng, "sDVE", "b1:wab")
                    gi = 0
                    for path in range(3):
                        base = [0, 32, 64][path]
                        for sc in range(NSC):
                            pb = [P4, P5][gi % 2]
                            if gi >= 2:
                                c.wait(eng, "sDVE", f"b1:ae{gi - 2}")
                            for dc in range(NC_):
                                c.op(eng, lambda e, pb=pb, base=base, path=path,
                                     sc=sc, dc=dc: e.matmul(
                                     pb[base:base + 16, 0:512],
                                     wab[:, dc * 48 + path * 16: dc * 48 + (path + 1) * 16],
                                     nrm[:, dc * T + sc * 512: dc * T + (sc + 1) * 512],
                                     start=(dc == 0), stop=(dc == NC_ - 1)),
                                     inc="sPE",
                                     mark=(f"b1:am{gi}" if dc == NC_ - 1 else None))
                            gi += 1
                    c.wait(eng, "sDVE", "b1:bdb")
                    for jc in range(16):
                        pb = [P6, P7][jc % 2]
                        if jc >= 2:
                            c.wait(eng, "sDVE", f"b1:bde{jc - 2}")
                        for dc in range(NC_):
                            c.op(eng, lambda e, pb=pb, jc=jc, dc=dc: e.matmul(
                                 pb[:, 0:51],
                                 nrm[:, dc * T + jc * 128: dc * T + (jc + 1) * 128],
                                 pbuf[:, BDO + dc * 51: BDO + dc * 51 + 51],
                                 start=(dc == 0), stop=(dc == NC_ - 1)),
                                 inc="sPE",
                                 mark=(f"b1:bdm{jc}" if dc == NC_ - 1 else None))

                def b1_dve(c, eng):
                    gi = 0
                    for mc in range(4):
                        for sc in range(NSC):
                            pb = [P0, P1][gi % 2]
                            c.wait(eng, "sPE", f"b1:p1m{gi}")
                            c.op(eng, lambda e, pb=pb, mc=mc, sc=sc: e.tensor_copy(
                                 p1_all[:, mc * T + sc * 512: mc * T + (sc + 1) * 512],
                                 pb[:, 0:512]), inc="sDVE", mark=f"b1:p1e{gi}")
                            gi += 1
                    gi = 0
                    for half in range(2):
                        dst = [kb, ksb][half]
                        for oc in range(NC_):
                            for sc in range(NSC):
                                pb = [P2, P3][gi % 2]
                                c.wait(eng, "sPE", f"b1:km{gi}")
                                c.op(eng, lambda e, pb=pb, dst=dst, oc=oc, sc=sc:
                                     e.tensor_copy(
                                     dst[:, oc * T + sc * 512: oc * T + (sc + 1) * 512],
                                     pb[:, 0:512]), inc="sDVE", mark=f"b1:ke{gi}")
                                gi += 1
                    for jc in range(16):
                        for half in range(2):
                            pb = [P2, P3][gi % 2]
                            c.wait(eng, "sPE", f"b1:km{gi}")
                            c.op(eng, lambda e, pb=pb, jc=jc, half=half:
                                 e.tensor_copy(
                                 v_tm[:, jc * D + half * 512: jc * D + (half + 1) * 512],
                                 pb[:, 0:512]), inc="sDVE", mark=f"b1:ke{gi}")
                            gi += 1
                    gi = 0
                    for path in range(3):
                        base = [0, 32, 64][path]
                        for sc in range(NSC):
                            pb = [P4, P5][gi % 2]
                            c.wait(eng, "sPE", f"b1:am{gi}")
                            c.op(eng, lambda e, pb=pb, base=base, sc=sc:
                                 e.tensor_copy(
                                 apack[base:base + 16, sc * 512:(sc + 1) * 512],
                                 pb[base:base + 16, 0:512]), inc="sDVE",
                                 mark=f"b1:ae{gi}")
                            gi += 1
                    for jc in range(16):
                        pb = [P6, P7][jc % 2]
                        c.wait(eng, "sPE", f"b1:bdm{jc}")
                        c.op(eng, lambda e, pb=pb, jc=jc: e.tensor_copy(
                             bm_tm[:, jc * 48:(jc + 1) * 48], pb[:, 0:48]),
                             inc="sDVE")
                        c.op(eng, lambda e, pb=pb, jc=jc: e.tensor_copy(
                             d_tm[:, jc * 3:(jc + 1) * 3], pb[:, 48:51]),
                             inc="sDVE", mark=f"b1:bde{jc}")

                def b1_vec(c, eng):
                    b1_cv(c, eng)
                    b1_dve(c, eng)

                ctx.run_block([
                    ("sync", b1_sp),
                    ("vector", b1_vec),
                    ("tensor", b1_pe),
                ])

            if "kb" in dbg_outs:
                def tdmab1(c, eng):
                    c.dma(eng, dbg_outs["kb"], kb[:, :], "sPL", mark="t1")
                    c.dma(eng, dbg_outs["ksb"], ksb[:, :], "sPL", mark="t2")
                    c.dma(eng, dbg_outs["v_tm"], v_tm[:, :], "sPL", mark="t3")
                    c.dma(eng, dbg_outs["p1_all"], p1_all[:, :], "sPL", mark="t4")
                    c.dma(eng, dbg_outs["apack"], apack[:, :], "sPL", mark="t5")
                    c.dma(eng, dbg_outs["bm_tm"], bm_tm[:, :], "sPL", mark="t6")
                    c.dma(eng, dbg_outs["d_tm"], d_tm[:, :], "sPL", mark="t7")
                    c.wait(eng, "sPL", "t7")
                ctx.run_block([("gpsimd", tdmab1)])

            def b2_pe(c, eng):
                if B2_LVL < 1:
                    return
                for ib in range(B2_NIB):
                    i0 = ib * WI
                    for path in range(B2_PATHS):
                        mc = [2, 0][path]
                        pr = ["s", "m"][path]
                        kbuf = [ksb, kb][path]
                        zdst = [zbufT2, zbufT][path]
                        if path == 0 and ib > 0:
                            if B2_LVL >= 6:
                                c.wait(eng, "sDVE", f"{ib - 1}:spre{NC_ - 1}")
                                c.wait(eng, "sACT", f"{ib - 1}:mvoe{NC_ - 1}")
                            else:
                                lm = {1: ("sDVE", f"{ib - 1}:mpb{NC_ - 1}"),
                                      2: ("sACT", f"{ib - 1}:mze15"),
                                      3: ("sDVE", f"{ib - 1}:mh15"),
                                      4: ("sDVE", f"{ib - 1}:nrec"),
                                      5: ("sACT", f"{ib - 1}:mvoe{NC_ - 1}")}[B2_LVL]
                                c.wait(eng, lm[0], lm[1])
                        if path == 1:
                            if B2_LVL >= 2:
                                c.wait(eng, "sACT", f"{ib}:sze15")
                            else:
                                c.wait(eng, "sDVE", f"{ib}:spb{NC_ - 1}")
                        for oc in range(NC_):
                            if oc >= 2:
                                c.wait(eng, "sDVE", f"{ib}:{pr}pb{oc - 2}")
                            pq = [P0, P1][oc % 2]
                            pm = [P2, P3][oc % 2]
                            c.op(eng, lambda e, pq=pq, mc=mc, oc=oc, i0=i0:
                                 e.matmul(
                                 pq[:, 0:WI],
                                 w2p[0:64, mc * D + oc * 128: mc * D + (oc + 1) * 128],
                                 p1_all[0:64, mc * T + i0: mc * T + i0 + WI],
                                 start=True, stop=True), inc="sPE",
                                 mark=f"{ib}:{pr}q{oc}")
                            c.op(eng, lambda e, pm=pm, mc=mc, oc=oc, i0=i0:
                                 e.matmul(
                                 pm[:, 0:WI],
                                 w2p[64:128, mc * D + oc * 128: mc * D + (oc + 1) * 128],
                                 p1_all[64:128, mc * T + i0: mc * T + i0 + WI],
                                 start=True, stop=True), inc="sPE",
                                 mark=f"{ib}:{pr}g{oc}")
                        if B2_LVL < 2:
                            continue
                        c.wait(eng, "sDVE", f"{ib}:{pr}pb{NC_ - 1}")
                        for jc in range(16):
                            pz = [P0, P1][jc % 2]
                            if jc >= 2:
                                c.wait(eng, "sACT", f"{ib}:{pr}ze{jc - 2}")
                            for dc in range(NC_):
                                c.op(eng, lambda e, pz=pz, kbuf=kbuf, jc=jc, dc=dc:
                                     e.matmul(
                                     pz[:, 0:WI],
                                     kbuf[:, dc * T + jc * 128: dc * T + (jc + 1) * 128],
                                     pbuf[:, dc * WI:(dc + 1) * WI],
                                     start=(dc == 0), stop=(dc == NC_ - 1)),
                                     inc="sPE",
                                     mark=(f"{ib}:{pr}z{jc}" if dc == NC_ - 1 else None))
                        if B2_LVL < 3:
                            continue
                        abase = [32, 0][path]
                        bsel = [1, 0][path]
                        c.wait(eng, "sACT", f"{ib}:{pr}ze15")
                        for jc in range(16):
                            c.op(eng, lambda e, jc=jc, bsel=bsel, zdst=zdst,
                                 abase=abase: e.matmul(
                                 P4[abase:abase + 16, 0:WI],
                                 bm_tm[:, jc * 48 + bsel * 16: jc * 48 + (bsel + 1) * 16],
                                 zdst[:, jc * WI:(jc + 1) * WI],
                                 start=(jc == 0), stop=(jc == 15)),
                                 inc="sPE",
                                 mark=(f"{ib}:{pr}zbm" if jc == 15 else None))
                        c.wait(eng, "sACT", f"{ib}:{pr}zbe")

                        def nacc_emit(jcn, ib=ib):
                            c.wait(eng, "sACT", f"{ib}:nsq{jcn}")
                            c.op(eng, lambda e, jcn=jcn: e.matmul(
                                 P7[0:1, 0:WI], ones_b[:, 0:1],
                                 sqb[:, (jcn % 8) * WI:(jcn % 8 + 1) * WI],
                                 start=(jcn == 0), stop=(jcn == 15)),
                                 inc="sPE", mark=f"{ib}:nacc{jcn}")
                        for jc in range(16):
                            pa = [P5, P6][jc % 2]
                            if jc >= 2:
                                c.wait(eng, "sDVE", f"{ib}:{pr}h{jc - 2}")
                            c.op(eng, lambda e, pa=pa, abase=abase, jc=jc:
                                 e.matmul(
                                 pa[:, 0:WI],
                                 apack[abase:abase + 16, jc * 128:(jc + 1) * 128],
                                 zbp[abase:abase + 16, 0:WI],
                                 start=True, stop=True), inc="sPE",
                                 mark=f"{ib}:{pr}at{jc}")
                            if path == 1 and B2_LVL >= 4 and jc >= 8:
                                nacc_emit(jc - 8)
                        if path == 1 and B2_LVL >= 4:
                            for jcn in range(8, 16):
                                nacc_emit(jcn)
                    if B2_LVL < 5:
                        continue
                    c.wait(eng, "sDVE", f"{ib}:h15")
                    c.wait(eng, "sACT", f"{ib}:mzbe")
                    for jc in range(16):
                        c.op(eng, lambda e, jc=jc: e.matmul(
                             P4[64:80, 0:WI],
                             bm_tm[:, jc * 48 + 32: jc * 48 + 48],
                             zbufT[:, jc * WI:(jc + 1) * WI],
                             start=(jc == 0), stop=(jc == 15)),
                             inc="sPE", mark=(f"{ib}:zb2m" if jc == 15 else None))
                    c.wait(eng, "sACT", f"{ib}:zb2e")
                    for jc in range(16):
                        pa = [P5, P6][jc % 2]
                        if jc >= 2:
                            c.wait(eng, "sDVE", f"{ib}:ro{jc - 2}")
                        c.op(eng, lambda e, pa=pa, jc=jc: e.matmul(
                             pa[:, 0:WI],
                             apack[64:80, jc * 128:(jc + 1) * 128],
                             zbp[64:80, 0:WI],
                             start=True, stop=True), inc="sPE",
                             mark=f"{ib}:rat{jc}")
                    for oc in range(NC_):
                        if oc >= 2:
                            c.wait(eng, "sACT", f"{ib}:mvoe{oc - 2}")
                        pq = [P0, P1][oc % 2]
                        c.op(eng, lambda e, pq=pq, oc=oc, i0=i0: e.matmul(
                             pq[:, 0:WI],
                             w2p[64:128, 3 * D + oc * 128: 3 * D + (oc + 1) * 128],
                             p1_all[64:128, 3 * T + i0: 3 * T + i0 + WI],
                             start=True, stop=True), inc="sPE",
                             mark=f"{ib}:mvo{oc}")
                    if B2_LVL < 6:
                        continue
                    c.wait(eng, "sDVE", f"{ib}:ro15")
                    for dc in range(NC_):
                        pz = [P2, P3][dc % 2]
                        if dc >= 2:
                            c.wait(eng, "sDVE", f"{ib}:spre{dc - 2}")
                        for jc in range(16):
                            c.op(eng, lambda e, pz=pz, dc=dc, jc=jc: e.matmul(
                                 pz[:, 0:WI],
                                 v_tm[:, jc * D + dc * 128: jc * D + (dc + 1) * 128],
                                 zbufT2[:, jc * WI:(jc + 1) * WI],
                                 start=(jc == 0), stop=(jc == 15)),
                                 inc="sPE",
                                 mark=(f"{ib}:ctx{dc}" if jc == 15 else None))

            def b2_dve(c, eng):
                if B2_LVL < 1:
                    return
                for ib in range(B2_NIB):
                    i0 = ib * WI
                    for path in range(B2_PATHS):
                        pr = ["s", "m"][path]
                        zdst = [zbufT2, zbufT][path]
                        dcol = [1, 0][path]
                        for oc in range(NC_):
                            pq = [P0, P1][oc % 2]
                            pm = [P2, P3][oc % 2]
                            so = (oc % 2) * WI
                            c.wait(eng, "sPE", f"{ib}:{pr}q{oc}")
                            c.op(eng, lambda e, pq=pq, so=so: e.tensor_copy(
                                 scr[:, so:so + WI], pq[:, 0:WI]), inc="sDVE",
                                 mark=f"{ib}:{pr}qc{oc}")
                            c.wait(eng, "sPE", f"{ib}:{pr}g{oc}")
                            c.wait(eng, "sDVE", f"{ib}:{pr}qc{oc}")
                            c.op(eng, lambda e, pm=pm, so=so, oc=oc: e.tensor_tensor(
                                 pbuf[:, oc * WI:(oc + 1) * WI], scr[:, so:so + WI],
                                 pm[:, 0:WI], ALU.mult), inc="sDVE",
                                 mark=f"{ib}:{pr}pb{oc}")
                        if B2_LVL < 3:
                            continue
                        for jc in range(16):
                            pa = [P5, P6][jc % 2]
                            c.wait(eng, "sPE", f"{ib}:{pr}at{jc}")
                            c.op(eng, lambda e, pa=pa, jc=jc, zdst=zdst, dcol=dcol:
                                 e.scalar_tensor_tensor(
                                 zdst[:, jc * WI:(jc + 1) * WI],
                                 zdst[:, jc * WI:(jc + 1) * WI],
                                 d_tm[:, jc * 3 + dcol: jc * 3 + dcol + 1],
                                 pa[:, 0:WI], ALU.mult, ALU.add), inc="sDVE",
                                 mark=f"{ib}:{pr}h{jc}")
                        if B2_LVL < 4:
                            continue
                        if path == 0:
                            pass
                        else:
                            for jc in range(16):
                                c.wait(eng, "sDVE", f"{ib}:mh{jc}")
                                c.wait(eng, "sACT", f"{ib}:nsq{jc}")
                                c.wait(eng, "sPOOL", f"sp_sp{ib * 16 + jc}")
                                c.op(eng, lambda e, jc=jc: e.scalar_tensor_tensor(
                                     zbufT[:, jc * WI:(jc + 1) * WI],
                                     zbufT[:, jc * WI:(jc + 1) * WI], 0.0,
                                     zbufT2[:, jc * WI:(jc + 1) * WI],
                                     ALU.max, ALU.mult), inc="sDVE",
                                     mark=f"{ib}:h{jc}")
                            c.wait(eng, "sACT", f"{ib}:nsqrt")
                            c.op(eng, lambda e: e.tensor_scalar(
                                 nstat[0:1, WI:2 * WI], nstat[0:1, 0:WI], 1e-8,
                                 None, ALU.add), inc="sDVE", mark=f"{ib}:nadd")
                            c.wait(eng, "sDVE", f"{ib}:nadd")
                            if ib > 0:
                                c.wait(eng, "sPLr", f"recdma{ib - 1}")
                            c.op(eng, lambda e: e.reciprocal(
                                 nstat[0:1, 2 * WI:3 * WI], nstat[0:1, WI:2 * WI]),
                                 inc="sDVE", mark=f"{ib}:nrec")
                    if B2_LVL < 5:
                        continue
                    for jc in range(16):
                        pa = [P5, P6][jc % 2]
                        c.wait(eng, "sPE", f"{ib}:rat{jc}")
                        c.op(eng, lambda e, pa=pa, jc=jc: e.scalar_tensor_tensor(
                             zbufT2[:, jc * WI:(jc + 1) * WI],
                             zbufT[:, jc * WI:(jc + 1) * WI],
                             d_tm[:, jc * 3 + 2: jc * 3 + 3],
                             pa[:, 0:WI], ALU.mult, ALU.add), inc="sDVE",
                             mark=f"{ib}:ro{jc}")
                    if B2_LVL < 6:
                        continue
                    for dc in range(NC_):
                        pz = [P2, P3][dc % 2]
                        gdc = ib * NC_ + dc
                        c.wait(eng, "sACT", f"{ib}:mvoe{dc}")
                        c.wait(eng, "sPE", f"{ib}:ctx{dc}")
                        if gdc >= 2:
                            c.wait(eng, f"sST{gdc % 2}", f"spdma{gdc - 2}")
                        c.op(eng, lambda e, pz=pz, dc=dc: e.tensor_tensor(
                             spre_t[:, (dc % 2) * WI:(dc % 2 + 1) * WI],
                             pz[:, 0:WI], pbuf[:, dc * WI:(dc + 1) * WI],
                             ALU.mult), inc="sDVE", mark=f"{ib}:spre{dc}")

            def b2_act(c, eng):
                if B2_LVL < 2:
                    return
                for ib in range(B2_NIB):
                    for path in range(B2_PATHS):
                        pr = ["s", "m"][path]
                        zdst = [zbufT2, zbufT][path]
                        for jc in range(16):
                            pz = [P0, P1][jc % 2]
                            c.wait(eng, "sPE", f"{ib}:{pr}z{jc}")
                            c.op(eng, lambda e, pz=pz, jc=jc, zdst=zdst:
                                 e.copy(zdst[:, jc * WI:(jc + 1) * WI],
                                        pz[:, 0:WI]),
                                 inc="sACT", mark=f"{ib}:{pr}ze{jc}")
                        if B2_LVL < 3:
                            continue
                        abase2 = [32, 0][path]
                        c.wait(eng, "sPE", f"{ib}:{pr}zbm")
                        c.op(eng, lambda e, abase2=abase2: e.copy(
                             zbp[abase2:abase2 + 16, 0:WI],
                             P4[abase2:abase2 + 16, 0:WI]),
                             inc="sACT", mark=f"{ib}:{pr}zbe")
                        if B2_LVL < 4:
                            continue
                        if path == 0:
                            for jc in range(16):
                                seq = ib * 16 + jc
                                o = (jc % 2) * 2 * WI
                                c.wait(eng, "sDVE", f"{ib}:sh{jc}")
                                if seq > 1:
                                    c.wait(eng, "sPOOL", f"sp_sp{seq - 2}")
                                c.op(eng, lambda e, jc=jc, o=o: e.activation(
                                     spb[:, o:o + WI],
                                     zbufT2[:, jc * WI:(jc + 1) * WI],
                                     AF.Abs), inc="sACT", mark=f"sp_u{seq}")
                                c.op(eng, lambda e, jc=jc, o=o: e.activation(
                                     spb[:, o + WI:o + 2 * WI],
                                     zbufT2[:, jc * WI:(jc + 1) * WI],
                                     AF.Relu), inc="sACT", mark=f"sp_m{seq}")
                                c.wait(eng, "sACT", f"sp_u{seq}")
                                c.op(eng, lambda e, o=o: e.activation(
                                     spb2[:, o:o + WI], spb[:, o:o + WI], AF.Exp,
                                     scale=-1.0), inc="sACT", mark=f"sp_e{seq}")
                                c.wait(eng, "sACT", f"sp_e{seq}")
                                c.op(eng, lambda e, o=o: e.activation(
                                     spb2[:, o + WI:o + 2 * WI], spb2[:, o:o + WI],
                                     AF.Ln, bias=1.0), inc="sACT",
                                     mark=f"sp_l{seq}")
                        else:
                            for jc in range(16):
                                c.wait(eng, "sDVE", f"{ib}:mh{jc}")
                                if ib * 16 + jc >= 8:
                                    pjc = jc - 8 if jc >= 8 else 8 + jc
                                    pib = ib if jc >= 8 else ib - 1
                                    c.wait(eng, "sPE", f"{pib}:nacc{pjc}")
                                c.op(eng, lambda e, jc=jc: e.activation(
                                     sqb[:, (jc % 8) * WI:(jc % 8 + 1) * WI],
                                     zbufT[:, jc * WI:(jc + 1) * WI], AF.Square),
                                     inc="sACT", mark=f"{ib}:nsq{jc}")
                            c.wait(eng, "sPE", f"{ib}:nacc15")
                            c.op(eng, lambda e: e.activation(
                                 nstat[0:1, 0:WI], P7[0:1, 0:WI], AF.Sqrt),
                                 inc="sACT", mark=f"{ib}:nsqrt")
                    if B2_LVL < 5:
                        continue
                    c.wait(eng, "sPE", f"{ib}:zb2m")
                    c.op(eng, lambda e: e.copy(
                         zbp[64:80, 0:WI], P4[64:80, 0:WI]),
                         inc="sACT", mark=f"{ib}:zb2e")
                    if B2_LVL < 6:
                        continue
                    for oc in range(NC_):
                        pq = [P0, P1][oc % 2]
                        c.wait(eng, "sPE", f"{ib}:mvo{oc}")
                        c.op(eng, lambda e, pq=pq, oc=oc: e.copy(
                             pbuf[:, oc * WI:(oc + 1) * WI], pq[:, 0:WI]),
                             inc="sACT", mark=f"{ib}:mvoe{oc}")

            def b2_pl(c, eng):
                for ib in range(B2_NIB):
                    i0 = ib * WI
                    if B2_LVL < 4:
                        continue
                    if B2_PATHS >= 1:
                        for jc in range(16):
                            seq = ib * 16 + jc
                            o = (jc % 2) * 2 * WI
                            c.wait(eng, "sACT", f"sp_l{seq}")
                            c.op(eng, lambda e, jc=jc, o=o: e.tensor_tensor(
                                 zbufT2[:, jc * WI:(jc + 1) * WI],
                                 spb[:, o + WI:o + 2 * WI],
                                 spb2[:, o + WI:o + 2 * WI],
                                 ALU.add), inc="sPOOL", mark=f"sp_sp{seq}")
                    c.wait(eng, "sDVE", f"{ib}:nrec")
                    c.dma(eng, d_recipn[0:1, i0:i0 + WI],
                          nstat[0:1, 2 * WI:3 * WI], "sPLr", mark=f"recdma{ib}")
                    if B2_LVL < 6:
                        continue
                    for dc in range(NC_):
                        gdc = ib * NC_ + dc
                        c.wait(eng, "sDVE", f"{ib}:spre{dc}")
                        c.dma(eng, d_spre[dc * 128:(dc + 1) * 128, i0:i0 + WI],
                              spre_t[:, (dc % 2) * WI:(dc % 2 + 1) * WI],
                              f"sST{gdc % 2}", mark=f"spdma{gdc}")

            if "kb" not in dbg_outs:
                ctx.run_block([
                    ("tensor", b2_pe),
                    ("vector", b2_dve),
                    ("scalar", b2_act),
                    ("gpsimd", b2_pl),
                ])

            if "spre" in dbg_outs:
                def tdma2(c, eng):
                    if B2_LVL >= 6:
                        c.wait(eng, "sST0", f"spdma{NIBG - 2}")
                        c.wait(eng, "sST1", f"spdma{NIBG - 1}")
                    if B2_LVL >= 4:
                        c.wait(eng, "sPLr", f"recdma{B2_NIB - 1}")
                    for dc in range(NC_):
                        c.dma(eng, kb[:, 0:T], d_spre[dc * 128:(dc + 1) * 128, :],
                              "sPL", mark=f"tl{dc}")
                        c.wait(eng, "sPL", f"tl{dc}")
                        c.dma(eng, dbg_outs["spre"][dc * 128:(dc + 1) * 128, :],
                              kb[:, 0:T], "sPL", mark=f"ts{dc}")
                        c.wait(eng, "sPL", f"ts{dc}")
                    c.dma(eng, scr[0:1, 0:T], d_recipn, "sPL", mark="trl")
                    c.wait(eng, "sPL", "trl")
                    c.dma(eng, dbg_outs["recipn"], scr[0:1, 0:T], "sPL",
                          mark="taprecipn")
                    c.wait(eng, "sPL", "taprecipn")
                ctx.run_block([("gpsimd", tdma2)])

        if "xn" not in dbg_outs and B3_ON:
          with contextlib.ExitStack() as stB3:
            wotb = sb(stB3, "wotb", [128, NC_ * 1024], BF16)
            wgb = sb(stB3, "wgb", [128, NC_ * 8], BF16)
            stg = sb(stB3, "stg", [128, T], F32)
            wsb = sb(stB3, "wsb", [128, 4 * 4096], BF16)
            hid = sb(stB3, "hid", [128, 32 * SC], BF16)
            s_sb = sb(stB3, "s_sb", [128, NC_ * SC], BF16)
            sprl = sb(stB3, "sprl", [128, NC_ * SC], BF16)
            xrl = sb(stB3, "xrl", [128, 2 * SC], F32)
            rcp_sb = sb(stB3, "rcp_sb", [1, T], F32)
            lg_sb = sb(stB3, "lg_sb", [8, SC], F32)
            gt = sb(stB3, "gt", [128, 4 * 8], F32)
            gsc = sb(stB3, "gsc", [128, 16], F32)
            srow = sb(stB3, "srow", [1, SC], F32)
            scb = sb(stB3, "scb", [128, SC], F32)
            rnb = sb(stB3, "rnb", [128, SC], F32)
            ab2 = sb(stB3, "ab2", [128, 2 * SC], F32)
            x1sc = sb(stB3, "x1sc", [128, NC_ * SC], F32)
            sqt = sb(stB3, "sqt", [128, 2 * SC], BF16)
            st2 = sb(stB3, "st2", [1, 4 * SC], F32)
            gct = sb(stB3, "gct", [128, 4 * SC], F32)
            Q = [ps(stB3, f"Q{i}", [128, 512]) for i in range(8)]

            def b3_drain(c, eng):
                if TAG_B2_FULL:
                    c.wait(eng, "sST0", f"spdma{NIBG - 2}")
                    c.wait(eng, "sST1", f"spdma{NIBG - 1}")
                    c.wait(eng, "sPLr", f"recdma{B2_NIB - 1}")

            def slab_src(g):
                sc, j = divmod(g, 16)
                if j < 8:
                    return w1.rearrange("(dc p) f -> p dc f", p=128)[
                        :, :, j * 512:(j + 1) * 512]
                jj = j - 8
                return w2.rearrange("(fc p) d -> p fc d", p=128)[
                    :, 4 * jj:4 * jj + 4, :]

            def b3_sp(c, eng):
                b3_drain(c, eng)
                items = [("wot0", woT.rearrange("(dc p) f -> p dc f", p=128)[:, 0:4, :]),
                         ("wot1", woT.rearrange("(dc p) f -> p dc f", p=128)[:, 4:8, :]),
                         ("wg", wg.rearrange("(dc p) f -> p dc f", p=128))]
                for g in range(16 * NSC):
                    items.append((f"sl{g}", slab_src(g)))
                for i, (nm, srcap) in enumerate(items):
                    lane = f"sLD{i % 2}"
                    w = 8 if nm == "wg" else (1024 if (nm.startswith("wot") or
                         (nm.startswith("sl") and int(nm[2:]) % 16 >= 8)) else 512)
                    nch = 4 if (nm.startswith("wot") or
                                (nm.startswith("sl") and int(nm[2:]) % 16 >= 8))                         else 8
                    if nm == "wot0":
                        dst = wotb[:, 0:4096]
                    elif nm == "wot1":
                        dst = wotb[:, 4096:8192]
                    elif nm == "wg":
                        dst = wgb[:, 0:64]
                    else:
                        if i >= 7:
                            c.wait(eng, "sPE", f"b3:wdone{i - 4}")
                        dst = wsb[:, (i % 4) * 4096:(i % 4) * 4096 + nch * w]
                    c.dma(eng, dst.rearrange("p (a b) -> p a b", a=nch),
                          srcap, lane, mark=f"b3:ld{i}")

            def b3_pl(c, eng):
                if TAG_B2_FULL:
                    c.wait(eng, "sST0", f"spdma{NIBG - 2}")
                    c.wait(eng, "sST1", f"spdma{NIBG - 1}")
                    c.wait(eng, "sPLr", f"recdma{B2_NIB - 1}")
                c.dma(eng, rcp_sb[0:1, :], d_recipn, "sPLc", mark="b3:rcp")
                for sc in range(NSC):
                    if sc > 0:
                        c.wait(eng, "sPE", f"b3:sundone{sc - 1}")
                    c.dma(eng, sprl.rearrange("p (a b) -> p a b", a=NC_),
                          d_spre.rearrange("(dc p) t -> p dc t", p=128)[
                              :, :, sc * SC:(sc + 1) * SC],
                          "sPLr", mark=f"b3:sp{sc}")
                    for dd in range(NC_):
                        gx = sc * NC_ + dd
                        if gx >= 2:
                            c.wait(eng, "sDVE", f"b3:cmb{(gx - 2) // NC_}_{(gx - 2) % NC_}")
                        c.dma(eng, xrl[:, (dd % 2) * SC:(dd % 2 + 1) * SC],
                              xT[dd * 128:(dd + 1) * 128, sc * SC:(sc + 1) * SC],
                              f"sST{gx % 2}", mark=f"b3:x{gx}")
                    for dc in range(NC_):
                        c.wait(eng, "sDVE", f"b3:cmb{sc}_{dc}")
                        c.dma(eng, d_x1[dc * 128:(dc + 1) * 128, sc * SC:(sc + 1) * SC],
                              x1sc[:, dc * SC:(dc + 1) * SC], "sPL",
                              mark=f"b3:x1s{sc}_{dc}")

            def b3_pe(c, eng):
                b3_drain(c, eng)
                for sc in range(NSC):
                    s0 = sc * SC
                    if sc > 0:
                        c.wait(eng, "sDVE", f"b3:xn2{sc - 1}_{NC_ - 1}")
                        c.wait(eng, "sPL", f"b3:x1s{sc - 1}_{NC_ - 1}")
                    for j in range(8):
                        gi = 3 + sc * 16 + j
                        c.wait(eng, f"sLD{gi % 2}", f"b3:ld{gi}")
                        for fcw in range(4):
                            fc = j * 4 + fcw
                            pq = Q[fc % 2]
                            if fc >= 2:
                                c.wait(eng, "sACT", f"b3:he{sc}_{fc - 2}")
                            for dc in range(NC_):
                                c.op(eng, lambda e, pq=pq, gi=gi, fcw=fcw, dc=dc,
                                     s0=s0: e.matmul(
                                     pq[:, 0:SC],
                                     wsb[:, (gi % 4) * 4096 + dc * 512 + fcw * 128:
                                         (gi % 4) * 4096 + dc * 512 + (fcw + 1) * 128],
                                     nrm[:, dc * T + s0: dc * T + s0 + SC],
                                     start=(dc == 0), stop=(dc == NC_ - 1)),
                                     inc="sPE",
                                     mark=(f"b3:hm{sc}_{fc}" if dc == NC_ - 1 else None))
                        c.mark(f"b3:wdone{gi}", "sPE")
                    c.wait(eng, "sLD0", "b3:ld2")
                    for dc in range(NC_):
                        c.op(eng, lambda e, dc=dc, s0=s0: e.matmul(
                             Q[2][0:8, 0:SC], wgb[:, dc * 8:(dc + 1) * 8],
                             nrm[:, dc * T + s0: dc * T + s0 + SC],
                             start=(dc == 0), stop=(dc == NC_ - 1)),
                             inc="sPE", mark=(f"b3:lgm{sc}" if dc == NC_ - 1 else None))
                    c.wait(eng, "sDVE", f"b3:lge{sc}")
                    for tt_ in range(4):
                        pq = Q[3 + tt_ % 2]
                        if tt_ >= 2:
                            c.wait(eng, "sDVE", f"b3:gte{sc}_{tt_ - 2}")
                        c.op(eng, lambda e, pq=pq, tt_=tt_: e.transpose(
                             pq[:, 0:8], lg_sb[0:8, tt_ * 128:(tt_ + 1) * 128],
                             identf[0:8, 0:8]), inc="sPE", mark=f"b3:tr{sc}_{tt_}")
                    for tt_ in range(4):
                        c.wait(eng, "sDVE", f"b3:scl{sc}_{tt_}")
                        c.op(eng, lambda e, tt_=tt_: e.transpose(
                             Q[5][0:1, tt_ * 128:(tt_ + 1) * 128],
                             gsc[:, tt_: tt_ + 1], identf[:, :]),
                             inc="sPE", mark=(f"b3:str{sc}" if tt_ == 3 else None))
                    c.wait(eng, "sACT", f"b3:sre{sc}")
                    c.op(eng, lambda e: e.matmul(
                         Q[6][:, 0:SC], onesrow_f[0:1, :], srow[0:1, 0:SC],
                         start=True, stop=True), inc="sPE", mark=f"b3:bc1{sc}")
                    c.wait(eng, "sPLc", "b3:rcp")
                    c.op(eng, lambda e, s0=s0: e.matmul(
                         Q[7][:, 0:SC], onesrow_f[0:1, :], rcp_sb[0:1, s0:s0 + SC],
                         start=True, stop=True), inc="sPE", mark=f"b3:bc2{sc}")
                    c.wait(eng, "sPLr", f"b3:sp{sc}")
                    c.wait(eng, "sLD0", "b3:ld0")
                    c.wait(eng, "sLD1", "b3:ld1")
                    c.wait(eng, "sDVE", f"b3:rnbe{sc}")
                    c.wait(eng, "sACT", f"b3:he{sc}_{30}")
                    c.wait(eng, "sACT", f"b3:he{sc}_{31}")
                    for dd in range(NC_):
                        pq = Q[dd % 2]
                        if dd >= 2:
                            c.wait(eng, "sDVE", f"b3:sre2{sc}_{dd - 2}")
                        for dc in range(NC_):
                            c.op(eng, lambda e, pq=pq, dd=dd, dc=dc: e.matmul(
                                 pq[:, 0:SC],
                                 wotb[:, dc * 1024 + dd * 128: dc * 1024 + (dd + 1) * 128],
                                 sprl[:, dc * SC:(dc + 1) * SC],
                                 start=(dc == 0), stop=(dc == NC_ - 1)),
                                 inc="sPE",
                                 mark=(f"b3:sun{sc}_{dd}" if dc == NC_ - 1 else None))
                    c.mark(f"b3:sundone{sc}", "sPE")
                    c.wait(eng, "sDVE", f"b3:sre2{sc}_{NC_ - 2}")
                    c.wait(eng, "sDVE", f"b3:sre2{sc}_{NC_ - 1}")
                    for j8 in range(8):
                        gi = 3 + sc * 16 + 8 + j8
                        c.wait(eng, f"sLD{gi % 2}", f"b3:ld{gi}")
                        for fcw in range(4):
                            fc = j8 * 4 + fcw
                            for dd in range(NC_):
                                c.op(eng, lambda e, gi=gi, fcw=fcw, dd=dd, fc=fc:
                                     e.matmul(
                                     Q[dd][:, 0:SC],
                                     wsb[:, (gi % 4) * 4096 + fcw * 1024 + dd * 128:
                                         (gi % 4) * 4096 + fcw * 1024 + (dd + 1) * 128],
                                     hid[:, fc * SC:(fc + 1) * SC],
                                     start=(fc == 0), stop=(fc == 31)),
                                     inc="sPE",
                                     mark=(f"b3:dn{sc}_{dd}" if fc == 31 else None))
                        c.mark(f"b3:wdone{gi}", "sPE")
                    c.wait(eng, "sDVE", f"b3:cmb{sc}_{NC_ - 1}")
                    for dc in range(NC_):
                        c.op(eng, lambda e, dc=dc: e.matmul(
                             Q[0][0:1, 0:SC], ones_f[:, 0:1],
                             x1sc[:, dc * SC:(dc + 1) * SC],
                             start=(dc == 0), stop=(dc == NC_ - 1)),
                             inc="sPE", mark=(f"b3:ssum{sc}" if dc == NC_ - 1 else None))
                    for dc in range(NC_):
                        c.wait(eng, "sACT", f"b3:sq2{sc}_{dc}")
                        c.op(eng, lambda e, dc=dc: e.matmul(
                             Q[1][0:1, 0:SC], ones_b[:, 0:1],
                             sqt[:, (dc % 2) * SC:(dc % 2 + 1) * SC],
                             start=(dc == 0), stop=(dc == NC_ - 1)),
                             inc="sPE", mark=f"b3:sqm{sc}_{dc}")
                    c.mark(f"b3:ssq{sc}", "sPE")
                    c.wait(eng, "sDVE", f"b3:ab2{sc}")
                    c.op(eng, lambda e: e.matmul(
                         Q[2][:, 0:SC], onesrow_f[0:1, :], st2[0:1, 0:SC],
                         start=True, stop=True), inc="sPE")
                    c.op(eng, lambda e: e.matmul(
                         Q[3][:, 0:SC], onesrow_f[0:1, :], st2[0:1, SC:2 * SC],
                         start=True, stop=True), inc="sPE", mark=f"b3:bc3{sc}")

            def b3_dve(c, eng):
                b3_drain(c, eng)
                for sc in range(NSC):
                    s0 = sc * SC
                    c.wait(eng, "sPE", f"b3:lgm{sc}")
                    c.op(eng, lambda e: e.tensor_copy(lg_sb[0:8, 0:SC],
                                                      Q[2][0:8, 0:SC]),
                         inc="sDVE", mark=f"b3:lge{sc}")
                    for tt_ in range(4):
                        pq = Q[3 + tt_ % 2]
                        c.wait(eng, "sPE", f"b3:tr{sc}_{tt_}")
                        c.op(eng, lambda e, pq=pq, tt_=tt_: e.tensor_copy(
                             gt[:, tt_ * 8:(tt_ + 1) * 8], pq[:, 0:8]),
                             inc="sDVE", mark=f"b3:gte{sc}_{tt_}")
                        c.wait(eng, "sDVE", f"b3:gte{sc}_{tt_}")
                        g0 = lambda col: gsc[:, col:col + 1]
                        tile = lambda: gt[:, tt_ * 8:(tt_ + 1) * 8]
                        c.op(eng, lambda e: e.reduce_max(g0(8), tile(), axis=AX,
                                                         negate=True), inc="sDVE",
                             mark=f"b3:g1_{sc}_{tt_}")
                        c.wait(eng, "sACT", f"b3:gexp{sc}_{tt_}")
                        c.op(eng, lambda e: e.reduce_sum(g0(9), tile(), axis=AX),
                             inc="sDVE", mark=f"b3:g2_{sc}_{tt_}")
                        c.wait(eng, "sDVE", f"b3:g2_{sc}_{tt_}")
                        c.op(eng, lambda e: e.reciprocal(g0(10), g0(9)), inc="sDVE",
                             mark=f"b3:g3_{sc}_{tt_}")
                        c.wait(eng, "sDVE", f"b3:g3_{sc}_{tt_}")
                        c.op(eng, lambda e: e.tensor_scalar(tile(), tile(), g0(10),
                             None, ALU.mult), inc="sDVE", mark=f"b3:g4_{sc}_{tt_}")
                        c.wait(eng, "sDVE", f"b3:g4_{sc}_{tt_}")
                        c.op(eng, lambda e: e.reduce_max(g0(11), tile(), axis=AX),
                             inc="sDVE", mark=f"b3:g5_{sc}_{tt_}")
                        c.wait(eng, "sDVE", f"b3:g5_{sc}_{tt_}")
                        c.op(eng, lambda e: e.tensor_scalar(gct[:, 0:8], tile(),
                             g0(11), None, ALU.is_ge), inc="sDVE",
                             mark=f"b3:g6_{sc}_{tt_}")
                        c.wait(eng, "sDVE", f"b3:g6_{sc}_{tt_}")
                        c.op(eng, lambda e: e.scalar_tensor_tensor(
                             gct[:, 8:16], gct[:, 0:8], -1e30, tile(),
                             ALU.mult, ALU.add), inc="sDVE", mark=f"b3:g7_{sc}_{tt_}")
                        c.wait(eng, "sDVE", f"b3:g7_{sc}_{tt_}")
                        c.op(eng, lambda e: e.reduce_max(g0(12), gct[:, 8:16],
                             axis=AX), inc="sDVE", mark=f"b3:g8_{sc}_{tt_}")
                        c.wait(eng, "sDVE", f"b3:g8_{sc}_{tt_}")
                        c.op(eng, lambda e: e.tensor_tensor(g0(13), g0(11), g0(12),
                             ALU.add), inc="sDVE", mark=f"b3:g9_{sc}_{tt_}")
                        c.wait(eng, "sDVE", f"b3:g9_{sc}_{tt_}")
                        c.op(eng, lambda e: e.tensor_scalar(g0(14), g0(13),
                             -F32EPS, F32EPS, ALU.add, ALU.max) if False else
                             e.tensor_scalar(g0(14), g0(13), F32EPS, None,
                                             ALU.max), inc="sDVE") if False else None
                        c.op(eng, lambda e: e.tensor_scalar(g0(14), g0(13),
                             F32EPS, F32EPS, ALU.max, ALU.bypass), inc="sDVE")                             if False else None
                        c.op(eng, lambda e: e.tensor_scalar(g0(14), g0(13),
                             F32EPS, 0.0, ALU.max, ALU.add), inc="sDVE",
                             mark=f"b3:g10_{sc}_{tt_}")
                        c.wait(eng, "sDVE", f"b3:g10_{sc}_{tt_}")
                        c.op(eng, lambda e: e.reciprocal(g0(15), g0(14)), inc="sDVE",
                             mark=f"b3:g11_{sc}_{tt_}")
                        c.wait(eng, "sDVE", f"b3:g11_{sc}_{tt_}")
                        c.op(eng, lambda e, tt_=tt_: e.tensor_tensor(
                             gsc[:, tt_: tt_ + 1], g0(13), g0(15), ALU.mult),
                             inc="sDVE", mark=f"b3:scl{sc}_{tt_}")
                    c.wait(eng, "sPE", f"b3:bc1{sc}")
                    c.op(eng, lambda e: e.tensor_copy(scb[:, 0:SC], Q[6][:, 0:SC]),
                         inc="sDVE", mark=f"b3:scbe{sc}")
                    c.wait(eng, "sPE", f"b3:bc2{sc}")
                    c.op(eng, lambda e: e.tensor_copy(rnb[:, 0:SC], Q[7][:, 0:SC]),
                         inc="sDVE", mark=f"b3:rnbe{sc}")
                    for dd in range(NC_):
                        pq = Q[dd % 2]
                        c.wait(eng, "sPE", f"b3:sun{sc}_{dd}")
                        c.op(eng, lambda e, pq=pq, dd=dd: e.tensor_tensor(
                             s_sb[:, dd * SC:(dd + 1) * SC], pq[:, 0:SC],
                             rnb[:, 0:SC], ALU.mult), inc="sDVE",
                             mark=f"b3:sre2{sc}_{dd}")
                    for dd in range(NC_):
                        gx = sc * NC_ + dd
                        c.wait(eng, "sPE", f"b3:dn{sc}_{dd}")
                        c.op(eng, lambda e, dd=dd: e.tensor_tensor(
                             gct[:, (dd % 2) * SC:(dd % 2 + 1) * SC],
                             Q[dd][:, 0:SC], scb[:, 0:SC], ALU.mult), inc="sDVE",
                             mark=f"b3:t1_{sc}_{dd}")
                        c.wait(eng, "sACT", f"b3:sil{sc}_{dd}")
                        c.op(eng, lambda e, dd=dd: e.tensor_tensor(
                             gct[:, (dd % 2) * SC:(dd % 2 + 1) * SC],
                             gct[:, (2 + dd % 2) * SC:(3 + dd % 2) * SC],
                             s_sb[:, dd * SC:(dd + 1) * SC], ALU.mult), inc="sDVE",
                             mark=f"b3:t3_{sc}_{dd}")
                        c.wait(eng, f"sST{gx % 2}", f"b3:x{gx}")
                        c.wait(eng, "sDVE", f"b3:t3_{sc}_{dd}")
                        c.op(eng, lambda e, dd=dd: e.tensor_tensor(
                             x1sc[:, dd * SC:(dd + 1) * SC],
                             gct[:, (dd % 2) * SC:(dd % 2 + 1) * SC],
                             xrl[:, (dd % 2) * SC:(dd % 2 + 1) * SC], ALU.add),
                             inc="sDVE", mark=f"b3:cmb{sc}_{dd}")
                    c.wait(eng, "sPE", f"b3:ssum{sc}")
                    c.op(eng, lambda e: e.tensor_scalar(
                         st2[0:1, 2 * SC:3 * SC], Q[0][0:1, 0:SC], 1.0 / D, None,
                         ALU.mult), inc="sDVE", mark=f"b3:mn{sc}")
                    c.wait(eng, "sDVE", f"b3:mn{sc}")
                    c.op(eng, lambda e: e.tensor_tensor(
                         st2[0:1, 3 * SC:4 * SC], st2[0:1, 2 * SC:3 * SC],
                         st2[0:1, 2 * SC:3 * SC], ALU.mult), inc="sDVE",
                         mark=f"b3:m2{sc}")
                    c.wait(eng, "sPE", f"b3:ssq{sc}")
                    c.wait(eng, "sDVE", f"b3:m2{sc}")
                    c.op(eng, lambda e: e.scalar_tensor_tensor(
                         st2[0:1, 3 * SC:4 * SC], Q[1][0:1, 0:SC], 1.0 / D,
                         st2[0:1, 3 * SC:4 * SC], ALU.mult, ALU.subtract),
                         inc="sDVE", mark=f"b3:var{sc}")
                    c.wait(eng, "sACT", f"b3:sqr{sc}")
                    c.op(eng, lambda e: e.reciprocal(st2[0:1, 0:SC],
                                                     st2[0:1, SC:2 * SC]),
                         inc="sDVE", mark=f"b3:rs{sc}")
                    c.wait(eng, "sDVE", f"b3:rs{sc}")
                    c.op(eng, lambda e: e.scalar_tensor_tensor(
                         st2[0:1, SC:2 * SC], st2[0:1, 2 * SC:3 * SC], -1.0,
                         st2[0:1, 0:SC], ALU.mult, ALU.mult), inc="sDVE",
                         mark=f"b3:ab2{sc}")
                    c.wait(eng, "sPE", f"b3:bc3{sc}")
                    c.op(eng, lambda e: e.tensor_copy(ab2[:, 0:SC], Q[2][:, 0:SC]),
                         inc="sDVE")
                    c.op(eng, lambda e: e.tensor_copy(ab2[:, SC:2 * SC],
                                                      Q[3][:, 0:SC]),
                         inc="sDVE", mark=f"b3:abe{sc}")
                    c.wait(eng, "sDVE", f"b3:abe{sc}")
                    for dc in range(NC_):
                        p2 = (2 + dc % 2) * SC
                        c.op(eng, lambda e, dc=dc, p2=p2: e.tensor_tensor(
                             gct[:, p2:p2 + SC], x1sc[:, dc * SC:(dc + 1) * SC],
                             ab2[:, 0:SC], ALU.mult), inc="sDVE",
                             mark=f"b3:xa{sc}_{dc}")
                        c.wait(eng, "sDVE", f"b3:xa{sc}_{dc}")
                        c.op(eng, lambda e, dc=dc, s0=s0, p2=p2: e.tensor_tensor(
                             nrm[:, dc * T + s0: dc * T + s0 + SC],
                             gct[:, p2:p2 + SC], ab2[:, SC:2 * SC], ALU.add),
                             inc="sDVE", mark=f"b3:xn2{sc}_{dc}")

            def b3_act(c, eng):
                b3_drain(c, eng)
                for sc in range(NSC):
                    for fc in range(32):
                        pq = Q[fc % 2]
                        c.wait(eng, "sPE", f"b3:hm{sc}_{fc}")
                        c.op(eng, lambda e, pq=pq, fc=fc: e.activation(
                             hid[:, fc * SC:(fc + 1) * SC], pq[:, 0:SC], AF.Relu),
                             inc="sACT", mark=f"b3:he{sc}_{fc}")
                    for tt_ in range(4):
                        c.wait(eng, "sDVE", f"b3:g1_{sc}_{tt_}")
                        c.op(eng, lambda e, tt_=tt_: e.activation(
                             gt[:, tt_ * 8:(tt_ + 1) * 8],
                             gt[:, tt_ * 8:(tt_ + 1) * 8], AF.Exp,
                             bias=gsc[:, 8:9]), inc="sACT",
                             mark=f"b3:gexp{sc}_{tt_}")
                    c.wait(eng, "sPE", f"b3:str{sc}")
                    c.op(eng, lambda e: e.copy(srow[0:1, 0:SC], Q[5][0:1, 0:SC]),
                         inc="sACT", mark=f"b3:sre{sc}")
                    for dd in range(NC_):
                        c.wait(eng, "sDVE", f"b3:t1_{sc}_{dd}")
                        c.op(eng, lambda e, dd=dd: e.activation(
                             gct[:, (2 + dd % 2) * SC:(3 + dd % 2) * SC],
                             gct[:, (dd % 2) * SC:(dd % 2 + 1) * SC], AF.Silu),
                             inc="sACT", mark=f"b3:sil{sc}_{dd}")
                    for dc in range(NC_):
                        c.wait(eng, "sDVE", f"b3:cmb{sc}_{dc}")
                        gdc2 = sc * NC_ + dc
                        if gdc2 >= 2:
                            psc, pdc = divmod(gdc2 - 2, NC_)
                            c.wait(eng, "sPE", f"b3:sqm{psc}_{pdc}")
                        c.op(eng, lambda e, dc=dc: e.activation(
                             sqt[:, (dc % 2) * SC:(dc % 2 + 1) * SC],
                             x1sc[:, dc * SC:(dc + 1) * SC], AF.Square),
                             inc="sACT", mark=f"b3:sq2{sc}_{dc}")
                    c.wait(eng, "sDVE", f"b3:var{sc}")
                    c.op(eng, lambda e: e.activation(
                         st2[0:1, SC:2 * SC], st2[0:1, 3 * SC:4 * SC], AF.Sqrt,
                         bias=cvals[0:1, 0:1]), inc="sACT", mark=f"b3:sqr{sc}")

            def b3_init(c, eng):
                c.dma(eng, identf[:, :], identf_t.ap(), "sPL", mark="b3:idf")

            def b3_pe_pre(c, eng):
                c.wait(eng, "sPL", "b3:idf")

            ctx.run_block([
                ("gpsimd", b3_init),
                ("gpsimd", b3_pl),
                ("sync", b3_sp),
                ("tensor", b3_pe_pre),
                ("tensor", b3_pe),
                ("vector", b3_dve),
                ("scalar", b3_act),
            ])

            if "x1" in dbg_outs:
                def tdma3(c, eng):
                    c.wait(eng, "sPL", f"b3:x1s{NSC - 1}_{NC_ - 1}")
                    for dc in range(NC_):
                        c.dma(eng, stg[:, 0:T], d_x1[dc * 128:(dc + 1) * 128, :],
                              "sPL", mark=f"tx{dc}")
                        c.wait(eng, "sPL", f"tx{dc}")
                        c.dma(eng, dbg_outs["x1"][dc * 128:(dc + 1) * 128, :],
                              stg[:, 0:T], "sPL", mark=f"txo{dc}")
                        c.wait(eng, "sPL", f"txo{dc}")
                    c.dma(eng, dbg_outs["xn2"], nrm[:, :], "sPL", mark="txn2")
                    c.wait(eng, "sPL", "txn2")
                ctx.run_block([("gpsimd", tdma3)])

        if "xn" not in dbg_outs and B4_ON:
          with contextlib.ExitStack() as stB4:
            out_g = sb(stB4, "out_g", [128, NC_ * T], BF16)
            decay = sb(stB4, "decay", [128, T], F32)
            upd = sb(stB4, "upd", [128, T], F32)
            states = sb(stB4, "states", [128, T], F32)
            gatee = sb(stB4, "gatee", [128, T], F32)
            wsb4 = sb(stB4, "wsb4", [128, 4 * 1024], BF16)
            esc = sb(stB4, "esc", [128, 2 * SC], F32)
            spu4 = sb(stB4, "spu4", [128, 2 * SC], BF16)
            spe4 = sb(stB4, "spe4", [128, 2 * SC], BF16)
            ab4 = sb(stB4, "ab4", [128, 2 * NC_], F32)
            s1t = sb(stB4, "s1t", [128, 2 * SC], BF16)
            Q = [ps(stB4, f"QB{i}", [128, 512]) for i in range(4)]

            def b4_drain(c, eng):
                c.wait(eng, "sPL", f"b3:x1s{NSC - 1}_{NC_ - 1}")

            def b4_sp(c, eng):
                b4_drain(c, eng)
                for g in range(40):
                    lane = f"sLD{g % 2}"
                    if g >= 4:
                        c.wait(eng, "sPE", f"b4:wd{g - 4}")
                    if g < 32:
                        ec, wi = divmod(g, 4)
                        srcap = l1w[wi].rearrange("(dc p) e -> p dc e", p=128)[
                            :, :, ec * 128:(ec + 1) * 128]
                    else:
                        e2c = g - 32
                        srcap = woutT.rearrange("(ec p) e -> p ec e", p=128)[
                            :, :, e2c * 128:(e2c + 1) * 128]
                    c.dma(eng, wsb4[:, (g % 4) * 1024:(g % 4 + 1) * 1024]
                          .rearrange("p (a b) -> p a b", a=NC_),
                          srcap, lane, mark=f"b4:ld{g}")

            def b4_cv(c, eng):
                b4_drain(c, eng)
                c.dma(eng, ab4[:, NC_:2 * NC_], log_a, "sPLc", mark="b4:la")
                for e2c in range(NC_):
                    for sc in range(NSC):
                        gy = e2c * NSC + sc
                        c.wait(eng, "sDVE", f"b4:s1e{e2c}_{sc}")
                        c.dma(eng, d_s1[e2c * 128:(e2c + 1) * 128,
                                        sc * SC:(sc + 1) * SC],
                              s1t[:, (gy % 2) * SC:(gy % 2 + 1) * SC],
                              f"sST{gy % 2}", mark=f"b4:s1s{gy}")

            def b4_pe(c, eng):
                b4_drain(c, eng)
                for ec in range(NC_):
                    for wi in range(4):
                        g = ec * 4 + wi
                        c.wait(eng, f"sLD{g % 2}", f"b4:ld{g}")
                        for sc in range(NSC):
                            gy4 = g * NSC + sc
                            pq = Q[gy4 % 3]
                            if gy4 >= 3:
                                pg, psc = divmod(gy4 - 3, NSC)
                                c.wait(eng, "sACT", f"b4:ev{pg}_{psc}")
                            for dc in range(NC_):
                                c.op(eng, lambda e, pq=pq, g=g, dc=dc, sc=sc:
                                     e.matmul(
                                     pq[:, 0:SC],
                                     wsb4[:, (g % 4) * 1024 + dc * 128:
                                          (g % 4) * 1024 + (dc + 1) * 128],
                                     nrm[:, dc * T + sc * SC: dc * T + (sc + 1) * SC],
                                     start=(dc == 0), stop=(dc == NC_ - 1)),
                                     inc="sPE",
                                     mark=(f"b4:mm{g}_{sc}" if dc == NC_ - 1 else None))
                        c.mark(f"b4:wd{g}", "sPE")
                c.wait(eng, "sDVE", f"b4:og{NC_ - 1}")
                for e2c in range(NC_):
                    g = 32 + e2c
                    c.wait(eng, f"sLD{g % 2}", f"b4:ld{g}")
                    for sc in range(NSC):
                        gy = e2c * NSC + sc
                        pq = Q[2 + gy % 2]
                        if gy >= 2:
                            c.wait(eng, "sDVE", f"b4:s1e{(gy - 2) // NSC}_{(gy - 2) % NSC}")
                        for ec in range(NC_):
                            c.op(eng, lambda e, pq=pq, g=g, ec=ec, sc=sc: e.matmul(
                                 pq[:, 0:SC],
                                 wsb4[:, (g % 4) * 1024 + ec * 128:
                                      (g % 4) * 1024 + (ec + 1) * 128],
                                 out_g[:, ec * T + sc * SC: ec * T + (sc + 1) * SC],
                                 start=(ec == 0), stop=(ec == NC_ - 1)),
                                 inc="sPE",
                                 mark=(f"b4:ym{e2c}_{sc}" if ec == NC_ - 1 else None))
                    c.mark(f"b4:wd{g}", "sPE")

            def b4_dve(c, eng):
                b4_drain(c, eng)
                c.wait(eng, "sACT", "b4:sig_a")
                c.op(eng, lambda e: e.tensor_scalar(
                     ab4[:, 0:NC_], ab4[:, NC_:2 * NC_], -1.0, None, ALU.mult),
                     inc="sDVE", mark="b4:nega")
                for ec in range(NC_):
                    for wi in range(4):
                        g = ec * 4 + wi
                        if wi == 0:
                            for sc in range(NSC):
                                c.wait(eng, "sACT", f"b4:l_{ec}_{sc}")
                                c.op(eng, lambda e, sc=sc: e.tensor_tensor(
                                     esc[:, (sc % 2) * SC:(sc % 2 + 1) * SC],
                                     esc[:, (sc % 2) * SC:(sc % 2 + 1) * SC],
                                     spe4[:, SC:2 * SC], ALU.add), inc="sDVE",
                                     mark=f"b4:d0_{ec}_{sc}")
                                c.wait(eng, "sDVE", f"b4:d0_{ec}_{sc}")
                                c.op(eng, lambda e, sc=sc: e.tensor_scalar(
                                     esc[:, (sc % 2) * SC:(sc % 2 + 1) * SC],
                                     esc[:, (sc % 2) * SC:(sc % 2 + 1) * SC],
                                     1e-4, 0.0, ALU.max, ALU.add), inc="sDVE",
                                     mark=f"b4:dm_{ec}_{sc}")
                        if wi == 2:
                            for sc in range(NSC):
                                c.wait(eng, "sACT", f"b4:tanh_{ec}_{sc}")
                                c.op(eng, lambda e, sc=sc: e.tensor_tensor(
                                     upd[:, sc * SC:(sc + 1) * SC],
                                     upd[:, sc * SC:(sc + 1) * SC],
                                     esc[:, (sc % 2) * SC:(sc % 2 + 1) * SC],
                                     ALU.mult), inc="sDVE",
                                     mark=f"b4:um_{ec}_{sc}")
                    c.wait(eng, "sACT", f"b4:dec_{ec}_{NSC - 1}")
                    c.wait(eng, "sDVE", f"b4:um_{ec}_{NSC - 1}")
                    c.op(eng, lambda e: e.tensor_tensor_scan(
                         states[:, 0:T], decay[:, 0:T], upd[:, 0:T], 0.0,
                         ALU.mult, ALU.add), inc="sDVE", mark=f"b4:scan{ec}")
                    c.wait(eng, "sDVE", f"b4:scan{ec}")
                    c.wait(eng, "sACT", f"b4:ev{ec * 4 + 3}_{NSC - 1}")
                    c.op(eng, lambda e, ec=ec: e.tensor_tensor(
                         out_g[:, ec * T:(ec + 1) * T], gatee[:, 0:T],
                         states[:, 0:T], ALU.mult), inc="sDVE",
                         mark=f"b4:og{ec}")
                for e2c in range(NC_):
                    for sc in range(NSC):
                        gy = e2c * NSC + sc
                        pq = Q[2 + gy % 2]
                        c.wait(eng, "sPE", f"b4:ym{e2c}_{sc}")
                        if gy >= 2:
                            c.wait(eng, f"sST{gy % 2}", f"b4:s1s{gy - 2}")
                        c.op(eng, lambda e, pq=pq, gy=gy: e.tensor_copy(
                             s1t[:, (gy % 2) * SC:(gy % 2 + 1) * SC], pq[:, 0:SC]),
                             inc="sDVE", mark=f"b4:s1e{e2c}_{sc}")

            def b4_act(c, eng):
                b4_drain(c, eng)
                c.wait(eng, "sPLc", "b4:la")
                c.op(eng, lambda e: e.activation(
                     ab4[:, NC_:2 * NC_], ab4[:, NC_:2 * NC_], AF.Sigmoid),
                     inc="sACT", mark="b4:sig_a")
                for ec in range(NC_):
                    for wi in range(4):
                        g = ec * 4 + wi
                        for sc in range(NSC):
                            pq = Q[(g * NSC + sc) % 3]
                            if wi == 0 and sc == 0 and ec > 0:
                                c.wait(eng, "sDVE", f"b4:og{ec - 1}")
                            c.wait(eng, "sPE", f"b4:mm{g}_{sc}")
                            if wi == 0:
                                c.op(eng, lambda e, pq=pq, sc=sc: e.activation(
                                     spu4[:, (sc % 2) * SC:(sc % 2 + 1) * SC],
                                     pq[:, 0:SC], AF.Abs), inc="sACT",
                                     mark=f"b4:u_{ec}_{sc}")
                                c.op(eng, lambda e, pq=pq, sc=sc: e.activation(
                                     esc[:, (sc % 2) * SC:(sc % 2 + 1) * SC],
                                     pq[:, 0:SC], AF.Relu), inc="sACT",
                                     mark=f"b4:ev{g}_{sc}")
                                c.wait(eng, "sACT", f"b4:u_{ec}_{sc}")
                                c.op(eng, lambda e, sc=sc: e.activation(
                                     spe4[:, 0:SC],
                                     spu4[:, (sc % 2) * SC:(sc % 2 + 1) * SC],
                                     AF.Exp, scale=-1.0), inc="sACT",
                                     mark=f"b4:e_{ec}_{sc}")
                                c.wait(eng, "sACT", f"b4:e_{ec}_{sc}")
                                c.op(eng, lambda e: e.activation(
                                     spe4[:, SC:2 * SC], spe4[:, 0:SC],
                                     AF.Ln, bias=1.0), inc="sACT",
                                     mark=f"b4:l_{ec}_{sc}")
                                c.wait(eng, "sDVE", f"b4:dm_{ec}_{sc}")
                                c.op(eng, lambda e, ec=ec, sc=sc: e.activation(
                                     decay[:, sc * SC:(sc + 1) * SC],
                                     esc[:, (sc % 2) * SC:(sc % 2 + 1) * SC],
                                     AF.Exp, scale=ab4[:, ec:ec + 1]), inc="sACT",
                                     mark=f"b4:dec_{ec}_{sc}")
                            elif wi == 1:
                                c.op(eng, lambda e, pq=pq, sc=sc: e.activation(
                                     upd[:, sc * SC:(sc + 1) * SC], pq[:, 0:SC],
                                     AF.Sigmoid), inc="sACT",
                                     mark=f"b4:ev{g}_{sc}")
                            elif wi == 2:
                                if sc >= 2:
                                    c.wait(eng, "sDVE", f"b4:um_{ec}_{sc - 2}")
                                c.op(eng, lambda e, pq=pq, sc=sc: e.activation(
                                     esc[:, (sc % 2) * SC:(sc % 2 + 1) * SC],
                                     pq[:, 0:SC], AF.Tanh), inc="sACT",
                                     mark=f"b4:ev{g}_{sc}")
                                c.mark(f"b4:tanh_{ec}_{sc}", "sACT")
                            else:
                                c.op(eng, lambda e, pq=pq, sc=sc: e.activation(
                                     gatee[:, sc * SC:(sc + 1) * SC], pq[:, 0:SC],
                                     AF.Sigmoid), inc="sACT",
                                     mark=f"b4:ev{g}_{sc}")

            ctx.run_block([
                ("sync", b4_sp),
                ("gpsimd", b4_cv),
                ("tensor", b4_pe),
                ("vector", b4_dve),
                ("scalar", b4_act),
            ])

            if "s1" in dbg_outs:
                def tdma4(c, eng):
                    c.wait(eng, "sST0", f"b4:s1s{NC_ * NSC - 2}")
                    c.wait(eng, "sST1", f"b4:s1s{NC_ * NSC - 1}")
                    for dc in range(NC_):
                        c.dma(eng, out_g[:, 0:T], d_s1[dc * 128:(dc + 1) * 128, :],
                              "sPL", mark=f"t4{dc}")
                        c.wait(eng, "sPL", f"t4{dc}")
                        c.dma(eng, dbg_outs["s1"][dc * 128:(dc + 1) * 128, :],
                              out_g[:, 0:T], "sPL", mark=f"t4o{dc}")
                        c.wait(eng, "sPL", f"t4o{dc}")
                ctx.run_block([("gpsimd", tdma4)])

        if "xn" not in dbg_outs and B5_ON:
          with contextlib.ExitStack() as stB5:
            bfA = sb(stB5, "bfA", [128, 24576], BF16)
            stg5 = sb(stB5, "stg5", [128, 2304], F32)
            wsb5 = sb(stB5, "wsb5", [128, 2 * 4096], BF16)
            wsd5 = sb(stB5, "wsd5", [128, 2 * 4096], BF16)
            comb_tm = sb(stB5, "comb_tm", [128, 16 * 8], F32)
            routerb = sb(stB5, "routerb", [128, NC_ * 8], BF16)
            lg5 = sb(stB5, "lg5", [8, SC], F32)
            lngb = sb(stB5, "lngb", [128, 2 * NC_], F32)
            gt5 = sb(stB5, "gt5", [128, 4 * 8], F32)
            gs5 = sb(stB5, "gs5", [128, 16], F32)
            gq5 = sb(stB5, "gq5", [128, 2 * 8], F32)
            sqt5 = sb(stB5, "sqt5", [128, 2 * SC], BF16)
            st5 = sb(stB5, "st5", [1, 4 * SC], F32)
            ab5 = sb(stB5, "ab5", [128, 2 * SC], F32)
            cl5 = sb(stB5, "cl5", [128, 2 * SC], F32)
            sg5 = sb(stB5, "sg5", [128, 2 * SC], F32)
            xt5 = sb(stB5, "xt5", [128, 2 * SC], F32)
            y_tm = sb(stB5, "y_tm", [128, JC * 1024], F32)
            xstg = sb(stB5, "xstg", [128, 2 * 1024], BF16)
            shstg = sb(stB5, "shstg", [128, 2 * 512], F32)
            mloc = sb(stB5, "mloc", [128, 64], F32)
            mltmp = sb(stB5, "mltmp", [128, 16], F32)
            silb = sb(stB5, "silb", [128, 4 * 128], F32)
            iot = sb(stB5, "iot", [128, 128], F32)
            Pbuf = sb(stB5, "Pbuf", [128, 4 * 128], F32)
            rhs3 = sb(stB5, "rhs3", [128, 3 * 16], F32)
            tcs = sb(stB5, "tcs", [128, 8 * JC * 2], F32)
            tidi5 = sb(stB5, "tidi5", [128, 8 * JC], I32)
            utf = sb(stB5, "utf", [128, 128], F32)
            idsc = sb(stB5, "idsc", [128, 16], F32)
            strow = sb(stB5, "strow", [1, 16], F32)
            Q = [ps(stB5, f"QC{i}", [128, 512]) for i in range(6)]
            QT = [stB5.enter_context(nc.psum_tensor(f"QT{i}", [128, 512], BF16))
                  for i in range(2)]

            items5 = [{"nm": "router", "cls": "r"}]
            items5 += [{"nm": f"sh_gu{p}", "cls": "gu"} for p in range(4)]
            items5 += [{"nm": "sh_d0", "cls": "d"}, {"nm": "sh_d1", "cls": "d"}]
            for e in range(8):
                for fb in range(4):
                    items5 += [{"nm": f"gu{e}_{fb}_{p}", "cls": "gu"}
                               for p in range(4)]
                    items5 += [{"nm": f"d{e}_{fb}_0", "cls": "d"},
                               {"nm": f"d{e}_{fb}_1", "cls": "d"}]
            hist5 = {"gu": [], "d": []}
            for q, it in enumerate(items5):
                if it["cls"] == "r":
                    it["gate"] = None
                    continue
                h = hist5[it["cls"]]
                it["slot"] = len(h) % 2
                it["gate"] = h[-2] if len(h) >= 2 else None
                h.append(q)

            def qb_of(e, fb):
                return 7 + 6 * (e * 4 + fb)
            QBS = 1

            def item_src(nm):
                if nm == "router":
                    return [(router.rearrange("(dc p) f -> p dc f", p=128),
                             8, 0, 8)]
                if nm.startswith("sh_gu"):
                    p_ = int(nm[5])
                    sgv = sguw.rearrange("(dc p) f -> p dc f", p=128)
                    return [(sgv[:, :, p_ * 256:(p_ + 1) * 256], 8, 0, 256),
                            (sgv[:, :, 1024 + p_ * 256:1024 + (p_ + 1) * 256],
                             8, 256, 512)]
                if nm.startswith("sh_d"):
                    half = int(nm[4])
                    return [(sdnw.rearrange("(fc p) d -> p fc d", p=128)[
                        :, half * 4:(half + 1) * 4, :], 4, 0, 1024)]
                if nm.startswith("gu"):
                    e, fb, p_ = nm[2:].split("_")
                    e, fb, p_ = int(e), int(fb), int(p_)
                    gv = guw[e].rearrange("(dc p) f -> p dc f", p=128)
                    c0 = fb * 1024 + p_ * 256
                    return [(gv[:, :, c0:c0 + 256], 8, 0, 256),
                            (gv[:, :, 4096 + c0:4096 + c0 + 256],
                             8, 256, 512)]
                e, fb, half = nm[1:].split("_")
                e, fb, half = int(e), int(fb), int(half)
                r0 = fb * 8 + half * 4
                return [(dnw[e].rearrange("(fc p) d -> p fc d", p=128)[
                    :, r0:r0 + 4, :], 4, 0, 1024)]

            def b5_drain(c, eng):
                c.wait(eng, "sST0", f"b4:s1s{NC_ * NSC - 2}")
                c.wait(eng, "sST1", f"b4:s1s{NC_ * NSC - 1}")
                c.wait(eng, "sPL", f"b3:x1s{NSC - 1}_{NC_ - 1}")

            def b5_sp(c, eng):
                b5_drain(c, eng)
                for q, it in enumerate(items5):
                    pieces = item_src(it["nm"])
                    if q == 0:
                        srcap, a, c0, c1 = pieces[0]
                        c.dma(eng, stg5[:, 0:a * (c1 - c0)]
                              .rearrange("p (a b) -> p a b", a=a),
                              srcap, "sRT", mark=f"b5:ld{q}")
                        continue
                    if it["gate"] is not None:
                        c.wait(eng, "sPE", f"b5:wd{it['gate']}")
                    buf = wsb5 if it["cls"] == "gu" else wsd5
                    s0_ = it["slot"] * 4096
                    for srcap, a, c0, c1 in pieces:
                        c.dma(eng, buf[:, s0_:s0_ + 4096]
                              .rearrange("p (a b) -> p a b", a=a)[:, :, c0:c1],
                              srcap, f"sLD{q % 2}", mark=f"b5:ld{q}")
                for ddc in range(NC_):
                    if ddc >= 3:
                        c.wait(eng, "sACT", f"b5:gsil{ddc - 3}_{15}")
                    if ddc == 0:
                        c.wait(eng, "sSA", f"b5:sa7_{JC - 1}")
                    c.dma(eng, y_tm[:, (ddc % 3) * 2048:(ddc % 3) * 2048 + 1024]
                          .rearrange("p (st d) -> p st d", st=8),
                          d_acc[0:1024, :].rearrange("(st p) d -> p st d", p=128)[
                              :, :, ddc * 128:(ddc + 1) * 128],
                          ["sAC0", "sAC1"][ddc % 2], mark=f"b5:accl{ddc}")
                    for sc in range(NSC):
                        gy = ddc * NSC + sc
                        if gy >= 2:
                            c.wait(eng, "sDVE",
                                   f"b5:cm2_{(gy - 2) // NSC}_{(gy - 2) % NSC}")
                        c.dma(eng, stg5[:, 256 + (gy % 2) * SC:
                                         256 + (gy % 2 + 1) * SC],
                              d_x1[ddc * 128:(ddc + 1) * 128,
                                   sc * SC:(sc + 1) * SC],
                              ["sAC0", "sAC1"][gy % 2], mark=f"b5:x1l{gy}")

            def b5_cv(c, eng):
                b5_drain(c, eng)
                c.dma(eng, lngb[:, 0:NC_], ln_g, "sPLc")
                c.dma(eng, lngb[:, NC_:2 * NC_], ln_b, "sPLc", mark="b5:lng")
                c.dma(eng, utf[:, :], utf_t.ap(), "sPLc")
                c.dma(eng, iot[:, :], iot_t.ap(), "sPLc")
                c.dma(eng, idsc[:, :], idsc_t.ap(), "sPLc", mark="b5:cst")

            def b5_pl(c, eng):
                c.wait(eng, "sPLc", "b5:cst")
                c.dma(eng, d_xnl[T:T + 1, :], zrow_t.ap(), "sIN",
                      mark="b5:xz2048")
                for e in range(8):
                    c.wait(eng, "sDVE", f"b5:tidc{e}")
                    if e < 2:
                        c.wait(eng, "sIN", "b5:xz2048")
                        c.wait(eng, "sXT0", "b5:xts14")
                        c.wait(eng, "sXT1", "b5:xts15")
                        c.wait(eng, "sPE", "b5:dn8_0_31")
                    else:
                        c.wait(eng, "sPE", f"b5:xgt{e - 2}_{JC * NC_ - 1}")
                    for jc in range(JC):
                        c.op(eng, lambda ee, e=e, jc=jc: ee.indirect_dma_start(
                             out=bfA[:, (e % 2) * JC * 1024 + jc * 1024:
                                     (e % 2) * JC * 1024 + (jc + 1) * 1024],
                             out_offset=None,
                             in_=d_xnl,
                             in_offset=IndirectOffsetOnAxis(
                                 tidi5[:, e * JC + jc: e * JC + jc + 1], 0)),
                             inc="sGA", n=16, mark=f"b5:xg{e}_{jc}")
                    c.wait(eng, "sGA", f"b5:xg{e}_{JC - 1}")
                    if e >= 1:
                        pe_ = e - 1
                        if pe_ == 0:
                            c.wait(eng, "sSH0", "b5:shs30")
                            c.wait(eng, "sSH1", "b5:shs31")
                        else:
                            c.wait(eng, "sSA", f"b5:sa{pe_ - 1}_{JC - 1}")
                        c.wait(eng, "sDVE", f"b5:ysc{pe_}_{JC - 1}")
                        for jc in range(JC):
                            c.op(eng, lambda ee, pe_=pe_, jc=jc:
                                 ee.indirect_dma_start(
                                 out=d_acc,
                                 out_offset=IndirectOffsetOnAxis(
                                     tidi5[:, pe_ * JC + jc:
                                           pe_ * JC + jc + 1], 0),
                                 in_=y_tm[:, jc * 1024:(jc + 1) * 1024],
                                 in_offset=None,
                                 compute_op=ALU.add),
                                 inc="sSA", n=16, mark=f"b5:sa{pe_}_{jc}")
                            if B5_SER or jc == JC - 1:
                                c.wait(eng, "sSA", f"b5:sa{pe_}_{jc}")
                c.wait(eng, "sDVE", f"b5:ysc7_{JC - 1}")
                c.wait(eng, "sSA", f"b5:sa6_{JC - 1}")
                for jc in range(JC):
                    c.op(eng, lambda ee, jc=jc: ee.indirect_dma_start(
                         out=d_acc,
                         out_offset=IndirectOffsetOnAxis(
                             tidi5[:, 7 * JC + jc:7 * JC + jc + 1], 0),
                         in_=y_tm[:, jc * 1024:(jc + 1) * 1024],
                         in_offset=None,
                         compute_op=ALU.add),
                         inc="sSA", n=16, mark=f"b5:sa7_{jc}")
                    if B5_SER or jc == JC - 1:
                        c.wait(eng, "sSA", f"b5:sa7_{jc}")
                for ddc in range(NC_):
                    if ddc >= 3:
                        c.wait(eng, "sACT", f"b5:gsil{ddc - 3}_{15}")
                    c.dma(eng, y_tm[:, (ddc % 3) * 2048 + 1024:
                                    (ddc % 3) * 2048 + 2048]
                          .rearrange("p (st d) -> p st d", st=8),
                          d_acc[1024:2048, :].rearrange(
                              "(st p) d -> p st d", p=128)[
                              :, :, ddc * 128:(ddc + 1) * 128],
                          "sIN", mark=f"b5:accl2_{ddc}")
                    for sc in range(NSC):
                        gy = ddc * NSC + sc
                        if gy >= 2:
                            c.wait(eng, "sDVE", f"b5:cm1_{(gy - 2) // NSC}_{(gy - 2) % NSC}")
                        c.dma(eng, bfA[:, 2048 + (gy % 2) * SC:
                                        2048 + (gy % 2 + 1) * SC],
                              d_s1[ddc * 128:(ddc + 1) * 128, sc * SC:(sc + 1) * SC],
                              f"sST{gy % 2}", mark=f"b5:s1l{gy}")
                        c.wait(eng, "sDVE", f"b5:yv{ddc}_{sc}")
                        c.dma(eng, y_out[ddc * 128:(ddc + 1) * 128,
                                         sc * SC:(sc + 1) * SC],
                              stg5[:, 1280 + (gy % 2) * SC:
                                   1280 + (gy % 2 + 1) * SC],
                              f"sYS{gy % 2}", mark=f"b5:ys{gy}")
                c.wait(eng, "sYS0", f"b5:ys{NC_ * NSC - 2}")
                c.wait(eng, "sYS1", f"b5:ys{NC_ * NSC - 1}")

            def b5_pe(c, eng):
                b5_drain(c, eng)
                for sc in range(NSC):
                    s0 = sc * SC
                    if sc > 0:
                        c.wait(eng, "sDVE", f"b5:xnl{sc - 1}_{NC_ - 1}")
                    for dc in range(NC_):
                        c.op(eng, lambda e, dc=dc, s0=s0: e.matmul(
                             Q[0][0:1, 0:SC], ones_b[:, 0:1],
                             nrm[:, dc * T + s0: dc * T + s0 + SC],
                             start=(dc == 0), stop=(dc == NC_ - 1)),
                             inc="sPE", mark=(f"b5:ssum{sc}" if dc == NC_ - 1 else None))
                    for dc in range(NC_):
                        c.wait(eng, "sACT", f"b5:sq{sc}_{dc}")
                        c.op(eng, lambda e, dc=dc: e.matmul(
                             Q[1][0:1, 0:SC], ones_b[:, 0:1],
                             sqt5[:, (dc % 2) * SC:(dc % 2 + 1) * SC],
                             start=(dc == 0), stop=(dc == NC_ - 1)),
                             inc="sPE", mark=f"b5:sqm{sc}_{dc}")
                    c.wait(eng, "sDVE", f"b5:ab{sc}")
                    c.op(eng, lambda e: e.matmul(
                         Q[2][:, 0:SC], onesrow_f[0:1, :], st5[0:1, 0:SC],
                         start=True, stop=True), inc="sPE")
                    c.op(eng, lambda e: e.matmul(
                         Q[3][:, 0:SC], onesrow_f[0:1, :], st5[0:1, SC:2 * SC],
                         start=True, stop=True), inc="sPE", mark=f"b5:bc{sc}")
                    c.wait(eng, "sDVE", "b5:cv0")
                    c.wait(eng, "sDVE", f"b5:xnl{sc}_{NC_ - 1}")
                    for dc in range(NC_):
                        c.op(eng, lambda e, dc=dc, s0=s0: e.matmul(
                             Q[4][0:8, 0:SC], routerb[:, dc * 8:(dc + 1) * 8],
                             nrm[:, dc * T + s0: dc * T + s0 + SC],
                             start=(dc == 0), stop=(dc == NC_ - 1)),
                             inc="sPE", mark=(f"b5:lgm{sc}" if dc == NC_ - 1 else None))
                    c.wait(eng, "sDVE", f"b5:lge{sc}")
                    for tt_ in range(4):
                        pq = Q[4 + tt_ % 2]
                        if tt_ >= 2:
                            c.wait(eng, "sDVE", f"b5:gte{sc}_{tt_ - 2}")
                        c.op(eng, lambda e, pq=pq, tt_=tt_: e.transpose(
                             pq[:, 0:8],
                             lg5[0:8, tt_ * 128:(tt_ + 1) * 128],
                             identf[0:8, 0:8]), inc="sPE",
                             mark=f"b5:tr{sc}_{tt_}")
                    if sc == NSC - 1:
                        c.mark("b5:wd0", "sPE")
                c.wait(eng, "sPLc", "b5:cst")
                for e in range(8):
                    c.wait(eng, "sDVE", f"b5:me{e}")
                    if e > 0:
                        c.wait(eng, "sDVE", f"b5:loce{e - 1}")
                    c.op(eng, lambda ee, e=e: ee.matmul(
                         Q[0][:, 0:16], utf[:, :], mloc[:, 0:16],
                         start=True, stop=True), inc="sPE")
                    c.op(eng, lambda ee, e=e: ee.matmul(
                         Q[1][0:1, 0:16], ones_f[:, 0:1], mloc[:, 0:16],
                         start=True, stop=True), inc="sPE", mark=f"b5:pfm{e}")
                    c.wait(eng, "sDVE", f"b5:exb{e}")
                    c.op(eng, lambda ee, e=e: ee.matmul(
                         Q[2][:, 0:16], onesrow_f[0:1, :], strow[0:1, 0:16],
                         start=True, stop=True), inc="sPE", mark=f"b5:bcb{e}")
                    c.wait(eng, "sDVE", f"b5:rh{e}")
                    for sc5 in range(JC):
                        gs = e * JC + sc5
                        if gs >= 2:
                            c.wait(eng, "sDVE",
                                   f"b5:tcv{(gs - 2) // JC}_{(gs - 2) % JC}")
                        for kt in range(16):
                            g4 = gs * 16 + kt
                            c.wait(eng, "sDVE", f"b5:pb{e}_{sc5}_{kt}")
                            c.op(eng, lambda ee, e=e, sc5=sc5, kt=kt, g4=g4:
                                 ee.matmul(
                                 Q[4][:, (sc5 % 2) * 3:(sc5 % 2) * 3 + 3],
                                 Pbuf[:, (g4 % 4) * 128:(g4 % 4 + 1) * 128],
                                 rhs3[:, 3 * kt:3 * kt + 3],
                                 start=(kt == 0), stop=(kt == 15)),
                                 inc="sPE", mark=f"b5:tm{e}_{sc5}_{kt}")
                c.wait(eng, "sDVE", f"b5:loce7")
                for st in range(16):
                    for dc in range(NC_):
                        g = st * NC_ + dc
                        pq = QT[g % 2]
                        if g >= 2:
                            c.wait(eng, "sACT", f"b5:xge{(g - 2) // NC_}_{(g - 2) % NC_}")
                        c.op(eng, lambda ee, pq=pq, st=st, dc=dc: ee.transpose(
                             pq[:, 0:128],
                             nrm[:, dc * T + st * 128: dc * T + (st + 1) * 128],
                             ident[:, :]), inc="sPE", mark=f"b5:xtr{st}_{dc}")
                qb = QBS
                for fcl in range(8):
                    qt = qb + fcl // 2
                    sl = ((fcl // 2) % 2) * 4096
                    if fcl % 2 == 0:
                        c.wait(eng, f"sLD{qt % 2}", f"b5:ld{qt}")
                    for sc in range(NSC):
                        gg = fcl * NSC + sc
                        pg = Q[gg % 2]
                        pu = Q[2 + gg % 2]
                        if gg >= 2:
                            c.wait(eng, "sDVE", f"b5:cl8_0_{gg - 2}")
                            c.wait(eng, "sDVE", f"b5:hm8_0_{gg - 2}")
                        for dc in range(NC_):
                            c.op(eng, lambda e, pg=pg, sl=sl, fcl=fcl, dc=dc,
                                 sc=sc: e.matmul(
                                 pg[:, 0:SC],
                                 wsb5[:, sl + dc * 512 +
                                      (fcl % 2) * 128:
                                      sl + dc * 512 +
                                      (fcl % 2 + 1) * 128],
                                 nrm[:, dc * T + sc * SC: dc * T + (sc + 1) * SC],
                                 start=(dc == 0), stop=(dc == NC_ - 1)),
                                 inc="sPE",
                                 mark=(f"b5:gm8_0_{gg}" if dc == NC_ - 1 else None))
                        for dc in range(NC_):
                            c.op(eng, lambda e, pu=pu, sl=sl, fcl=fcl, dc=dc,
                                 sc=sc: e.matmul(
                                 pu[:, 0:SC],
                                 wsb5[:, sl + dc * 512 + 256 +
                                      (fcl % 2) * 128:
                                      sl + dc * 512 + 256 +
                                      (fcl % 2 + 1) * 128],
                                 nrm[:, dc * T + sc * SC: dc * T + (sc + 1) * SC],
                                 start=(dc == 0), stop=(dc == NC_ - 1)),
                                 inc="sPE",
                                 mark=(f"b5:um8_0_{gg}" if dc == NC_ - 1 else None))
                    if fcl % 2 == 1:
                        c.mark(f"b5:wd{qt}", "sPE")
                c.wait(eng, f"sLD{qb % 2}", f"b5:ld{qb + 4}")
                c.wait(eng, f"sLD{(qb + 1) % 2}", f"b5:ld{qb + 5}")
                c.wait(eng, "sDVE", f"b5:hm8_0_{31}")
                for st in range(16):
                    for ddh in range(2):
                        gd = st * 2 + ddh
                        pd = Q[4 + gd % 2]
                        if gd >= 2:
                            c.wait(eng, "sDVE", f"b5:she{gd - 2}")
                        for fcl in range(8):
                            c.op(eng, lambda e, pd=pd, fcl=fcl, st=st,
                                 ddh=ddh: e.matmul(
                                 pd[:, 0:SC],
                                 bfA[:, fcl * T + st * 128:
                                      fcl * T + (st + 1) * 128],
                                 wsd5[:, (fcl // 4) * 4096 + (fcl % 4) * 1024 +
                                      ddh * 512:
                                      (fcl // 4) * 4096 + (fcl % 4) * 1024 +
                                      (ddh + 1) * 512],
                                 start=(fcl == 0), stop=(fcl == 7)),
                                 inc="sPE",
                                 mark=(f"b5:dn8_0_{gd}" if fcl == 7 else None))
                c.mark(f"b5:wd{qb + 4}", "sPE")
                c.mark(f"b5:wd{qb + 5}", "sPE")
                for e_ in range(8):
                    c.wait(eng, "sGA", f"b5:xg{e_}_{JC - 1}")
                    for jc in range(JC):
                        for dc in range(NC_):
                            g = jc * NC_ + dc
                            pq = QT[g % 2]
                            if g >= 2:
                                c.wait(eng, "sDVE", f"b5:xfe{e_}_{g - 2}")
                            elif e_ > 0:
                                c.wait(eng, "sDVE",
                                       f"b5:xfe{e_ - 1}_{JC * NC_ - 2 + g}")
                            else:
                                c.wait(eng, "sACT", f"b5:xge15_{NC_ - 2 + g}")
                            c.op(eng, lambda ee, pq=pq, e_=e_, jc=jc, dc=dc:
                                 ee.transpose(
                                 pq[:, 0:128],
                                 bfA[:, (e_ % 2) * JC * 1024 + jc * 1024 +
                                     dc * 128:
                                     (e_ % 2) * JC * 1024 + jc * 1024 +
                                     (dc + 1) * 128],
                                 ident[:, :]), inc="sPE",
                                 mark=f"b5:xgt{e_}_{jc * NC_ + dc}")
                    c.wait(eng, "sDVE", f"b5:xfe{e_}_{JC * NC_ - 1}")
                    for fb in range(4):
                        qb = qb_of(e_, fb)
                        for fcl in range(8):
                            qt = qb + fcl // 2
                            sl = ((fcl // 2) % 2) * 4096
                            if fcl % 2 == 0:
                                c.wait(eng, f"sLD{qt % 2}", f"b5:ld{qt}")
                            for sub in range(2):
                                coff, cw = SUBS[sub]
                                if fcl > 0:
                                    c.wait(eng, "sDVE", f"b5:cl{e_}_{fb}_{fcl - 1}_{sub}")
                                    c.wait(eng, "sDVE", f"b5:hm{e_}_{fb}_{fcl - 1}_{sub}")
                                elif fb > 0:
                                    c.wait(eng, "sDVE", f"b5:cl{e_}_{fb - 1}_7_{sub}")
                                    c.wait(eng, "sDVE", f"b5:hm{e_}_{fb - 1}_7_{sub}")
                                elif e_ > 0:
                                    c.wait(eng, "sDVE", f"b5:cl{e_ - 1}_3_7_{sub}")
                                    c.wait(eng, "sDVE", f"b5:hm{e_ - 1}_3_7_{sub}")
                                elif sub == 0:
                                    c.wait(eng, "sDVE", "b5:she31")
                                for dc in range(NC_):
                                    c.op(eng, lambda ee, sl=sl, fcl=fcl, dc=dc,
                                         coff=coff, cw=cw: ee.matmul(
                                         Q[0 if coff == 0 else 1][:, 0:cw],
                                         wsb5[:, sl + dc * 512 +
                                              (fcl % 2) * 128:
                                              sl + dc * 512 +
                                              (fcl % 2 + 1) * 128],
                                         bfA[:, 12288 + dc * CAP + coff:
                                             12288 + dc * CAP + coff + cw],
                                         start=(dc == 0), stop=(dc == NC_ - 1)),
                                         inc="sPE",
                                         mark=(f"b5:gm{e_}_{fb}_{fcl}_{sub}"
                                               if dc == NC_ - 1 else None))
                                for dc in range(NC_):
                                    c.op(eng, lambda ee, sl=sl, fcl=fcl, dc=dc,
                                         coff=coff, cw=cw: ee.matmul(
                                         Q[2 if coff == 0 else 3][:, 0:cw],
                                         wsb5[:, sl + dc * 512 +
                                              256 + (fcl % 2) * 128:
                                              sl + dc * 512 +
                                              256 + (fcl % 2 + 1) * 128],
                                         bfA[:, 12288 + dc * CAP + coff:
                                             12288 + dc * CAP + coff + cw],
                                         start=(dc == 0), stop=(dc == NC_ - 1)),
                                         inc="sPE",
                                         mark=(f"b5:um{e_}_{fb}_{fcl}_{sub}"
                                               if dc == NC_ - 1 else None))
                            if fcl % 2 == 1:
                                c.mark(f"b5:wd{qt}", "sPE")
                        c.wait(eng, f"sLD{qb % 2}", f"b5:ld{qb + 4}")
                        c.wait(eng, f"sLD{(qb + 1) % 2}", f"b5:ld{qb + 5}")
                        c.wait(eng, "sDVE", f"b5:hm{e_}_{fb}_7_1")
                        for jsl in range(JC):
                            for ddh in range(2):
                                gd = jsl * 2 + ddh
                                pd = Q[4 + gd % 2]
                                if gd >= 2:
                                    c.wait(eng, "sDVE",
                                           f"b5:ye{e_}_{fb}_{gd - 2}")
                                for fcl in range(8):
                                    c.op(eng, lambda ee, pd=pd, fcl=fcl,
                                         jsl=jsl, ddh=ddh: ee.matmul(
                                         pd[:, 0:512],
                                         bfA[:, 18432 + fcl * CAP + jsl * 128:
                                             18432 + fcl * CAP + (jsl + 1) * 128],
                                         wsd5[:, (fcl // 4) * 4096 +
                                              (fcl % 4) * 1024 + ddh * 512:
                                              (fcl // 4) * 4096 +
                                              (fcl % 4) * 1024 + (ddh + 1) * 512],
                                         start=(fcl == 0), stop=(fcl == 7)),
                                         inc="sPE",
                                         mark=(f"b5:dn{e_}_{fb}_{gd}"
                                               if fcl == 7 else None))
                        c.mark(f"b5:wd{qb + 4}", "sPE")
                        c.mark(f"b5:wd{qb + 5}", "sPE")
                for ddc in range(NC_):
                    for st in range(16):
                        gb = ddc * 16 + st
                        pq = Q[4 + gb % 2]
                        c.wait(eng, "sACT", f"b5:gsil{ddc}_{st}")
                        if gb >= 2:
                            c.wait(eng, "sDVE", f"b5:gfe{(gb - 2) // 16}_{(gb - 2) % 16}")
                        c.op(eng, lambda e, pq=pq, st=st: e.transpose(
                             pq[0:128, 0:128],
                             silb[:, (st % 4) * 128:(st % 4 + 1) * 128],
                             identf[:, :]), inc="sPE", mark=f"b5:gtr{ddc}_{st}")

            def b5_dve(c, eng):
                b5_drain(c, eng)
                c.wait(eng, "sRT", "b5:ld0")
                c.op(eng, lambda e: e.tensor_copy(routerb[:, 0:64],
                                                  stg5[:, 0:64]),
                     inc="sDVE", mark="b5:cv0")
                for sc in range(NSC):
                    s0 = sc * SC
                    c.wait(eng, "sPE", f"b5:ssum{sc}")
                    c.op(eng, lambda e: e.tensor_scalar(
                         st5[0:1, 2 * SC:3 * SC], Q[0][0:1, 0:SC], 1.0 / D, None,
                         ALU.mult), inc="sDVE", mark=f"b5:mn{sc}")
                    c.wait(eng, "sDVE", f"b5:mn{sc}")
                    c.op(eng, lambda e: e.tensor_tensor(
                         st5[0:1, 3 * SC:4 * SC], st5[0:1, 2 * SC:3 * SC],
                         st5[0:1, 2 * SC:3 * SC], ALU.mult), inc="sDVE",
                         mark=f"b5:m2{sc}")
                    c.wait(eng, "sPE", f"b5:sqm{sc}_{NC_ - 1}")
                    c.wait(eng, "sDVE", f"b5:m2{sc}")
                    c.op(eng, lambda e: e.scalar_tensor_tensor(
                         st5[0:1, 3 * SC:4 * SC], Q[1][0:1, 0:SC], 1.0 / D,
                         st5[0:1, 3 * SC:4 * SC], ALU.mult, ALU.subtract),
                         inc="sDVE", mark=f"b5:var{sc}")
                    c.wait(eng, "sACT", f"b5:sqr{sc}")
                    c.op(eng, lambda e: e.reciprocal(st5[0:1, 0:SC],
                                                     st5[0:1, SC:2 * SC]),
                         inc="sDVE", mark=f"b5:rs{sc}")
                    c.wait(eng, "sDVE", f"b5:rs{sc}")
                    c.op(eng, lambda e: e.scalar_tensor_tensor(
                         st5[0:1, SC:2 * SC], st5[0:1, 2 * SC:3 * SC], -1.0,
                         st5[0:1, 0:SC], ALU.mult, ALU.mult), inc="sDVE",
                         mark=f"b5:ab{sc}")
                    c.wait(eng, "sPE", f"b5:bc{sc}")
                    c.op(eng, lambda e: e.tensor_copy(ab5[:, 0:SC], Q[2][:, 0:SC]),
                         inc="sDVE")
                    c.op(eng, lambda e: e.tensor_copy(ab5[:, SC:2 * SC],
                                                      Q[3][:, 0:SC]),
                         inc="sDVE", mark=f"b5:abe{sc}")
                    c.wait(eng, "sDVE", f"b5:abe{sc}")
                    c.wait(eng, "sPLc", "b5:cst")
                    for dc in range(NC_):
                        p2 = (dc % 2) * SC
                        c.op(eng, lambda e, dc=dc, p2=p2, s0=s0: e.tensor_tensor(
                             cl5[:, p2:p2 + SC], nrm[:, dc * T + s0: dc * T + s0 + SC],
                             ab5[:, 0:SC], ALU.mult), inc="sDVE",
                             mark=f"b5:xa{sc}_{dc}")
                        c.wait(eng, "sDVE", f"b5:xa{sc}_{dc}")
                        c.op(eng, lambda e, dc=dc, p2=p2: e.tensor_tensor(
                             sg5[:, p2:p2 + SC], cl5[:, p2:p2 + SC],
                             ab5[:, SC:2 * SC], ALU.add), inc="sDVE",
                             mark=f"b5:xb{sc}_{dc}")
                        c.wait(eng, "sDVE", f"b5:xb{sc}_{dc}")
                        c.op(eng, lambda e, dc=dc, p2=p2, s0=s0: e.tensor_scalar(
                             nrm[:, dc * T + s0: dc * T + s0 + SC],
                             sg5[:, p2:p2 + SC], lngb[:, dc:dc + 1],
                             lngb[:, NC_ + dc:NC_ + dc + 1], ALU.mult, ALU.add),
                             inc="sDVE", mark=f"b5:xnl{sc}_{dc}")
                    c.wait(eng, "sPE", f"b5:lgm{sc}")
                    c.op(eng, lambda e: e.tensor_copy(lg5[0:8, 0:SC],
                                                      Q[4][0:8, 0:SC]),
                         inc="sDVE", mark=f"b5:lge{sc}")
                    for tt_ in range(4):
                        pq = Q[4 + tt_ % 2]
                        c.wait(eng, "sPE", f"b5:tr{sc}_{tt_}")
                        c.op(eng, lambda e, pq=pq, tt_=tt_: e.tensor_copy(
                             gt5[:, tt_ * 8:(tt_ + 1) * 8], pq[:, 0:8]),
                             inc="sDVE", mark=f"b5:gte{sc}_{tt_}")
                        c.wait(eng, "sDVE", f"b5:gte{sc}_{tt_}")
                        g0 = lambda col: gs5[:, col:col + 1]
                        tile = lambda: gt5[:, tt_ * 8:(tt_ + 1) * 8]
                        c.op(eng, lambda e: e.reduce_max(g0(8), tile(), axis=AX,
                                                         negate=True), inc="sDVE",
                             mark=f"b5:c1_{sc}_{tt_}")
                        c.wait(eng, "sACT", f"b5:cexp{sc}_{tt_}")
                        c.op(eng, lambda e: e.reduce_sum(g0(9), tile(), axis=AX),
                             inc="sDVE", mark=f"b5:c2_{sc}_{tt_}")
                        c.wait(eng, "sDVE", f"b5:c2_{sc}_{tt_}")
                        c.op(eng, lambda e: e.reciprocal(g0(10), g0(9)),
                             inc="sDVE", mark=f"b5:c3_{sc}_{tt_}")
                        c.wait(eng, "sDVE", f"b5:c3_{sc}_{tt_}")
                        c.op(eng, lambda e: e.tensor_scalar(tile(), tile(),
                             g0(10), None, ALU.mult), inc="sDVE",
                             mark=f"b5:c4_{sc}_{tt_}")
                        c.wait(eng, "sDVE", f"b5:c4_{sc}_{tt_}")
                        c.op(eng, lambda e: e.reduce_max(g0(11), tile(), axis=AX),
                             inc="sDVE", mark=f"b5:c5_{sc}_{tt_}")
                        c.wait(eng, "sDVE", f"b5:c5_{sc}_{tt_}")
                        c.op(eng, lambda e: e.tensor_scalar(gq5[:, 0:8], tile(),
                             g0(11), None, ALU.is_ge), inc="sDVE",
                             mark=f"b5:c6_{sc}_{tt_}")
                        c.wait(eng, "sDVE", f"b5:c6_{sc}_{tt_}")
                        c.op(eng, lambda e: e.scalar_tensor_tensor(
                             gq5[:, 8:16], gq5[:, 0:8], -1e30, tile(),
                             ALU.mult, ALU.add), inc="sDVE", mark=f"b5:c7_{sc}_{tt_}")
                        c.wait(eng, "sDVE", f"b5:c7_{sc}_{tt_}")
                        c.op(eng, lambda e: e.reduce_max(g0(12), gq5[:, 8:16],
                             axis=AX), inc="sDVE", mark=f"b5:c8_{sc}_{tt_}")
                        c.wait(eng, "sDVE", f"b5:c8_{sc}_{tt_}")
                        c.op(eng, lambda e: e.tensor_scalar(gq5[:, 8:16],
                             gq5[:, 8:16], g0(12), None, ALU.is_ge), inc="sDVE",
                             mark=f"b5:c9_{sc}_{tt_}")
                        c.wait(eng, "sDVE", f"b5:c9_{sc}_{tt_}")
                        c.op(eng, lambda e: e.tensor_tensor(gq5[:, 0:8],
                             gq5[:, 0:8], gq5[:, 8:16], ALU.add), inc="sDVE",
                             mark=f"b5:c10_{sc}_{tt_}")
                        c.op(eng, lambda e: e.tensor_tensor(g0(13), g0(11),
                             g0(12), ALU.add), inc="sDVE", mark=f"b5:c11_{sc}_{tt_}")
                        c.wait(eng, "sDVE", f"b5:c11_{sc}_{tt_}")
                        c.op(eng, lambda e: e.reciprocal(g0(14), g0(13)),
                             inc="sDVE", mark=f"b5:c12_{sc}_{tt_}")
                        c.wait(eng, "sDVE", f"b5:c12_{sc}_{tt_}")
                        c.op(eng, lambda e: e.tensor_scalar(gq5[:, 8:16], tile(),
                             g0(14), None, ALU.mult), inc="sDVE",
                             mark=f"b5:c13_{sc}_{tt_}")
                        c.wait(eng, "sDVE", f"b5:c13_{sc}_{tt_}")
                        c.op(eng, lambda e, sc=sc, tt_=tt_: e.tensor_tensor(
                             comb_tm[:, (sc * 4 + tt_) * 8:(sc * 4 + tt_ + 1) * 8],
                             gq5[:, 8:16], gq5[:, 0:8], ALU.mult), inc="sDVE",
                             mark=f"b5:comb{sc}_{tt_}")
                c.wait(eng, "sDVE", f"b5:comb{NSC - 1}_3")
                c.wait(eng, "sPLc", "b5:cst")
                for kt in range(16):
                    c.op(eng, lambda ee, kt=kt: ee.tensor_copy(
                         rhs3[:, 3 * kt:3 * kt + 1], idsc[:, kt:kt + 1]),
                         inc="sDVE")
                    c.op(eng, lambda ee, kt=kt: ee.tensor_copy(
                         rhs3[:, 3 * kt + 2:3 * kt + 3], ones_f[:, 0:1]),
                         inc="sDVE")
                for e in range(8):
                    for st in range(16):
                        c.op(eng, lambda ee, e=e, st=st: ee.tensor_scalar(
                             mloc[:, st:st + 1],
                             comb_tm[:, st * 8 + e: st * 8 + e + 1], 0.0, None,
                             ALU.is_gt), inc="sDVE",
                             mark=(f"b5:me{e}" if st == 15 else None))
                    c.wait(eng, "sDVE", f"b5:me{e}")
                    c.wait(eng, "sPE", f"b5:pfm{e}")
                    c.op(eng, lambda ee: ee.tensor_copy(strow[0:1, 0:16],
                                                        Q[1][0:1, 0:16]),
                         inc="sDVE", mark=f"b5:tot{e}")
                    c.wait(eng, "sDVE", f"b5:tot{e}")
                    c.op(eng, lambda ee: ee.tensor_tensor_scan(
                         mloc[0:1, 16:32], onesrow_f[0:1, 0:16],
                         strow[0:1, 0:16], 0.0, ALU.mult, ALU.add),
                         inc="sDVE", mark=f"b5:scn{e}")
                    c.wait(eng, "sDVE", f"b5:scn{e}")
                    c.op(eng, lambda ee: ee.tensor_tensor(
                         strow[0:1, 0:16], mloc[0:1, 16:32], strow[0:1, 0:16],
                         ALU.subtract), inc="sDVE", mark=f"b5:exb{e}")
                    c.wait(eng, "sDVE", f"b5:exb{e}")
                    c.op(eng, lambda ee: ee.tensor_copy(mloc[:, 16:32],
                                                        Q[0][:, 0:16]),
                         inc="sDVE", mark=f"b5:pfe{e}")
                    c.wait(eng, "sPE", f"b5:bcb{e}")
                    c.wait(eng, "sDVE", f"b5:pfe{e}")
                    c.op(eng, lambda ee: ee.scalar_tensor_tensor(
                         mloc[:, 32:48], mloc[:, 16:32], -1.0, Q[2][:, 0:16],
                         ALU.add, ALU.add), inc="sDVE", mark=f"b5:loce{e}")
                    c.wait(eng, "sDVE", f"b5:loce{e}")
                    c.op(eng, lambda ee: ee.tensor_scalar(
                         mloc[:, 48:64], mloc[:, 32:48], 768.0, None,
                         ALU.subtract), inc="sDVE", mark=f"b5:ml1{e}")
                    c.wait(eng, "sDVE", f"b5:ml1{e}")
                    c.op(eng, lambda ee: ee.tensor_tensor(
                         mloc[:, 48:64], mloc[:, 48:64], mloc[:, 0:16],
                         ALU.mult), inc="sDVE", mark=f"b5:ml2{e}")
                    c.wait(eng, "sDVE", f"b5:ml2{e}")
                    c.op(eng, lambda ee: ee.tensor_scalar(
                         mloc[:, 48:64], mloc[:, 48:64], 768.0, None,
                         ALU.add), inc="sDVE", mark=f"b5:ml3{e}")
                    c.wait(eng, "sDVE", f"b5:ml3{e}")
                    c.op(eng, lambda ee: ee.tensor_scalar(
                         mloc[:, 48:64], mloc[:, 48:64], 768.0, None,
                         ALU.min), inc="sDVE", mark=f"b5:ml4{e}")
                    c.wait(eng, "sDVE", f"b5:ml4{e}")
                    for kt in range(16):
                        if kt == 0 and e >= 1:
                            c.wait(eng, "sPE", f"b5:tm{e - 1}_{JC - 1}_{15}")
                        c.op(eng, lambda ee, e=e, kt=kt: ee.tensor_copy(
                             rhs3[:, 3 * kt + 1:3 * kt + 2],
                             comb_tm[:, kt * 8 + e: kt * 8 + e + 1]),
                             inc="sDVE",
                             mark=(f"b5:rh{e}" if kt == 15 else None))
                    for sc5 in range(JC):
                        c.op(eng, lambda ee, sc5=sc5: ee.tensor_scalar(
                             mltmp[:, 0:16], mloc[:, 48:64],
                             float(-sc5 * 128), None, ALU.add), inc="sDVE",
                             mark=f"b5:mls{e}_{sc5}")
                        c.wait(eng, "sDVE", f"b5:mls{e}_{sc5}")
                        for kt in range(16):
                            g4 = (e * JC + sc5) * 16 + kt
                            if g4 >= 4:
                                pg4 = g4 - 4
                                c.wait(eng, "sPE",
                                       f"b5:tm{pg4 // 96}_{(pg4 // 16) % JC}"
                                       f"_{pg4 % 16}")
                            c.op(eng, lambda ee, g4=g4, kt=kt: ee.tensor_scalar(
                                 Pbuf[:, (g4 % 4) * 128:(g4 % 4 + 1) * 128],
                                 iot[:, :], mltmp[:, kt:kt + 1], None,
                                 ALU.is_equal), inc="sDVE",
                                 mark=f"b5:pb{e}_{sc5}_{kt}")
                        tcpc = (sc5 % 2) * 3
                        c.wait(eng, "sPE", f"b5:tm{e}_{sc5}_{15}")
                        c.op(eng, lambda ee, e=e, sc5=sc5, tcpc=tcpc:
                             ee.tensor_scalar(
                             tcs[:, e * JC * 2 + sc5 * 2:
                                 e * JC * 2 + sc5 * 2 + 1],
                             Q[4][:, tcpc + 2:tcpc + 3], -2048.0, 2048.0,
                             ALU.mult, ALU.add), inc="sDVE",
                             mark=f"b5:tca{e}_{sc5}")
                        c.wait(eng, "sDVE", f"b5:tca{e}_{sc5}")
                        c.op(eng, lambda ee, e=e, sc5=sc5, tcpc=tcpc:
                             ee.tensor_tensor(
                             tcs[:, e * JC * 2 + sc5 * 2:
                                 e * JC * 2 + sc5 * 2 + 1],
                             tcs[:, e * JC * 2 + sc5 * 2:
                                 e * JC * 2 + sc5 * 2 + 1],
                             Q[4][:, tcpc:tcpc + 1], ALU.add), inc="sDVE",
                             mark=f"b5:tcb{e}_{sc5}")
                        c.op(eng, lambda ee, e=e, sc5=sc5, tcpc=tcpc:
                             ee.tensor_copy(
                             tcs[:, e * JC * 2 + sc5 * 2 + 1:
                                 e * JC * 2 + sc5 * 2 + 2],
                             Q[4][:, tcpc + 1:tcpc + 2]), inc="sDVE",
                             mark=f"b5:tcv{e}_{sc5}")
                    c.wait(eng, "sDVE", f"b5:tcv{e}_{JC - 1}")
                    c.op(eng, lambda ee, e=e: ee.tensor_copy(
                         tidi5[:, e * JC:(e + 1) * JC]
                         .rearrange("p (c k) -> p c k", k=1),
                         tcs[:, e * JC * 2:(e + 1) * JC * 2]
                         .rearrange("p (cc k) -> p cc k", cc=JC)[:, :, 0:1]),
                         inc="sDVE", mark=f"b5:tidc{e}")
                qb = QBS
                for fcl in range(8):
                    for sc in range(NSC):
                        gg = fcl * NSC + sc
                        pg = Q[gg % 2]
                        pu = Q[2 + gg % 2]
                        c.wait(eng, "sPE", f"b5:gm8_0_{gg}")
                        c.op(eng, lambda e, pg=pg, gg=gg: e.tensor_scalar(
                             cl5[:, (gg % 2) * SC:(gg % 2 + 1) * SC],
                             pg[:, 0:SC], 10.0, -10.0, ALU.min, ALU.max),
                             inc="sDVE", mark=f"b5:cl8_0_{gg}")
                        c.wait(eng, "sACT", f"b5:sil8_0_{gg}")
                        c.wait(eng, "sPE", f"b5:um8_0_{gg}")
                        c.op(eng, lambda e, pu=pu, gg=gg, fcl=fcl, sc=sc:
                             e.tensor_tensor(
                             bfA[:, fcl * T + sc * SC: fcl * T + (sc + 1) * SC],
                             sg5[:, (gg % 2) * SC:(gg % 2 + 1) * SC],
                             pu[:, 0:SC], ALU.mult), inc="sDVE",
                             mark=f"b5:hm8_0_{gg}")
                for st in range(16):
                    for ddh in range(2):
                        gd = st * 2 + ddh
                        pd = Q[4 + gd % 2]
                        c.wait(eng, "sPE", f"b5:dn8_0_{gd}")
                        if gd >= 2:
                            c.wait(eng, f"sSH{gd % 2}", f"b5:shs{gd - 2}")
                        c.op(eng, lambda e, pd=pd, gd=gd: e.tensor_copy(
                             shstg[:, (gd % 2) * 512:(gd % 2 + 1) * 512],
                             pd[:, 0:512]), inc="sDVE", mark=f"b5:she{gd}")
                for e_ in range(8):
                    for jc in range(JC):
                        for dc in range(NC_):
                            g = jc * NC_ + dc
                            pq = QT[g % 2]
                            c.wait(eng, "sPE", f"b5:xgt{e_}_{g}")
                            c.op(eng, lambda ee, pq=pq, jc=jc, dc=dc: ee.tensor_copy(
                                 bfA[:, 12288 + dc * CAP + jc * 128:
                                     12288 + dc * CAP + (jc + 1) * 128],
                                 pq[:, 0:128]), inc="sDVE",
                                 mark=f"b5:xfe{e_}_{g}")
                    for fb in range(4):
                        qb = qb_of(e_, fb)
                        for fcl in range(8):
                            for sub in range(2):
                                coff, cw = SUBS[sub]
                                gg = fcl * 2 + sub
                                c.wait(eng, "sPE", f"b5:gm{e_}_{fb}_{fcl}_{sub}")
                                c.op(eng, lambda ee, sub=sub, gg=gg, cw=cw:
                                     ee.tensor_scalar(
                                     cl5[:, (gg % 2) * SC:(gg % 2) * SC + cw],
                                     Q[0 if sub == 0 else 1][:, 0:cw],
                                     10.0, -10.0, ALU.min, ALU.max),
                                     inc="sDVE", mark=f"b5:cl{e_}_{fb}_{fcl}_{sub}")
                                c.wait(eng, "sACT", f"b5:sil{e_}_{fb}_{fcl}_{sub}")
                                c.wait(eng, "sPE", f"b5:um{e_}_{fb}_{fcl}_{sub}")
                                c.op(eng, lambda ee, sub=sub, gg=gg, fcl=fcl,
                                     coff=coff, cw=cw: ee.tensor_tensor(
                                     bfA[:, 18432 + fcl * CAP + coff:
                                         18432 + fcl * CAP + coff + cw],
                                     sg5[:, (gg % 2) * SC:(gg % 2) * SC + cw],
                                     Q[2 if sub == 0 else 3][:, 0:cw],
                                     ALU.mult), inc="sDVE",
                                     mark=f"b5:hm{e_}_{fb}_{fcl}_{sub}")
                        for jsl in range(JC):
                            for ddh in range(2):
                                gd = jsl * 2 + ddh
                                pd = Q[4 + gd % 2]
                                c.wait(eng, "sPE", f"b5:dn{e_}_{fb}_{gd}")
                                if fb == 0:
                                    if e_ >= 1 and gd == 0:
                                        c.wait(eng, "sSA",
                                               f"b5:sa{e_ - 1}_{JC - 1}")
                                    c.op(eng, lambda ee, pd=pd, jsl=jsl, ddh=ddh:
                                         ee.tensor_copy(
                                         y_tm[:, jsl * 1024 + ddh * 512:
                                              jsl * 1024 + (ddh + 1) * 512],
                                         pd[:, 0:512]), inc="sDVE",
                                         mark=f"b5:ye{e_}_{fb}_{gd}")
                                else:
                                    c.op(eng, lambda ee, pd=pd, jsl=jsl, ddh=ddh:
                                         ee.tensor_tensor(
                                         y_tm[:, jsl * 1024 + ddh * 512:
                                              jsl * 1024 + (ddh + 1) * 512],
                                         pd[:, 0:512],
                                         y_tm[:, jsl * 1024 + ddh * 512:
                                              jsl * 1024 + (ddh + 1) * 512],
                                         ALU.add), inc="sDVE",
                                         mark=f"b5:ye{e_}_{fb}_{gd}")
                    for jc in range(JC):
                        c.wait(eng, "sDVE", f"b5:ye{e_}_3_{jc * 2 + 1}")
                        c.op(eng, lambda ee, e_=e_, jc=jc: ee.tensor_scalar(
                             y_tm[:, jc * 1024:(jc + 1) * 1024],
                             y_tm[:, jc * 1024:(jc + 1) * 1024],
                             tcs[:, e_ * JC * 2 + 2 * jc + 1:
                                 e_ * JC * 2 + 2 * jc + 2],
                             None, ALU.mult), inc="sDVE",
                             mark=f"b5:ysc{e_}_{jc}")
                for ddc in range(NC_):
                    for st in range(16):
                        pq = Q[4 + (ddc * 16 + st) % 2]
                        c.wait(eng, "sPE", f"b5:gtr{ddc}_{st}")
                        c.op(eng, lambda e, pq=pq, st=st: e.tensor_copy(
                             bfA[:, st * 128:(st + 1) * 128], pq[0:128, 0:128]),
                             inc="sDVE", mark=f"b5:gfe{ddc}_{st}")
                    for sc in range(NSC):
                        gy = ddc * NSC + sc
                        c.wait(eng, "sDVE", f"b5:gfe{ddc}_{(sc + 1) * 4 - 1}")
                        c.wait(eng, f"sST{gy % 2}", f"b5:s1l{gy}")
                        c.op(eng, lambda e, sc=sc, gy=gy: e.tensor_tensor(
                             xt5[:, (gy % 2) * SC:(gy % 2 + 1) * SC],
                             bfA[:, sc * SC:(sc + 1) * SC],
                             bfA[:, 2048 + (gy % 2) * SC: 2048 + (gy % 2 + 1) * SC],
                             ALU.mult),
                             inc="sDVE", mark=f"b5:cm1_{ddc}_{sc}")
                        c.wait(eng, ["sAC0", "sAC1"][gy % 2], f"b5:x1l{gy}")
                        c.wait(eng, "sDVE", f"b5:cm1_{ddc}_{sc}")
                        if gy >= 2:
                            c.wait(eng, f"sYS{gy % 2}", f"b5:ys{gy - 2}")
                        c.op(eng, lambda e, gy=gy: e.tensor_tensor(
                             stg5[:, 1280 + (gy % 2) * SC: 1280 + (gy % 2 + 1) * SC],
                             xt5[:, (gy % 2) * SC:(gy % 2 + 1) * SC],
                             stg5[:, 256 + (gy % 2) * SC: 256 + (gy % 2 + 1) * SC],
                             ALU.add),
                             inc="sDVE", mark=f"b5:yv{ddc}_{sc}")
                        c.mark(f"b5:cm2_{ddc}_{sc}", "sDVE")

            def b5_act(c, eng):
                b5_drain(c, eng)
                for sc in range(NSC):
                    s0 = sc * SC
                    for dc in range(NC_):
                        gdc2 = sc * NC_ + dc
                        if gdc2 >= 2:
                            psc, pdc = divmod(gdc2 - 2, NC_)
                            c.wait(eng, "sPE", f"b5:sqm{psc}_{pdc}")
                        c.op(eng, lambda e, dc=dc, s0=s0: e.activation(
                             sqt5[:, (dc % 2) * SC:(dc % 2 + 1) * SC],
                             nrm[:, dc * T + s0: dc * T + s0 + SC], AF.Square),
                             inc="sACT", mark=f"b5:sq{sc}_{dc}")
                    c.wait(eng, "sDVE", f"b5:var{sc}")
                    c.op(eng, lambda e: e.activation(
                         st5[0:1, SC:2 * SC], st5[0:1, 3 * SC:4 * SC], AF.Sqrt,
                         bias=cvals[0:1, 0:1]), inc="sACT", mark=f"b5:sqr{sc}")
                    for tt_ in range(4):
                        c.wait(eng, "sDVE", f"b5:c1_{sc}_{tt_}")
                        c.op(eng, lambda e, tt_=tt_: e.activation(
                             gt5[:, tt_ * 8:(tt_ + 1) * 8],
                             gt5[:, tt_ * 8:(tt_ + 1) * 8], AF.Exp,
                             bias=gs5[:, 8:9]), inc="sACT",
                             mark=f"b5:cexp{sc}_{tt_}")
                for st in range(16):
                    if st >= 2:
                        c.wait(eng, f"sXT{st % 2}", f"b5:xts{st - 2}")
                    for dc in range(NC_):
                        g = st * NC_ + dc
                        pq = QT[g % 2]
                        c.wait(eng, "sPE", f"b5:xtr{st}_{dc}")
                        c.op(eng, lambda ee, pq=pq, st=st, dc=dc: ee.copy(
                             xstg[:, (st % 2) * 1024 + dc * 128:
                                  (st % 2) * 1024 + (dc + 1) * 128],
                             pq[:, 0:128]), inc="sACT", mark=f"b5:xge{st}_{dc}")
                    c.wait(eng, "sACT", f"b5:xge{st}_{NC_ - 1}")
                    c.dma(eng, d_xnl[st * 128:(st + 1) * 128, :],
                          xstg[:, (st % 2) * 1024:(st % 2 + 1) * 1024],
                          f"sXT{st % 2}", mark=f"b5:xts{st}")
                for fcl in range(8):
                    for sc in range(NSC):
                        gg = fcl * NSC + sc
                        c.wait(eng, "sDVE", f"b5:cl8_0_{gg}")
                        if gg >= 2:
                            c.wait(eng, "sDVE", f"b5:hm8_0_{gg - 2}")
                        c.op(eng, lambda e, gg=gg: e.activation(
                             sg5[:, (gg % 2) * SC:(gg % 2 + 1) * SC],
                             cl5[:, (gg % 2) * SC:(gg % 2 + 1) * SC],
                             AF.Silu), inc="sACT",
                             mark=f"b5:sil8_0_{gg}")
                for st in range(16):
                    for ddh in range(2):
                        gd = st * 2 + ddh
                        c.wait(eng, "sDVE", f"b5:she{gd}")
                        c.dma(eng, d_acc[st * 128:(st + 1) * 128,
                                         ddh * 512:(ddh + 1) * 512],
                              shstg[:, (gd % 2) * 512:(gd % 2 + 1) * 512],
                              f"sSH{gd % 2}", mark=f"b5:shs{gd}")
                for e_ in range(8):
                    for fb in range(4):
                        for fcl in range(8):
                            for sub in range(2):
                                gg = fcl * 2 + sub
                                cw = SUBS[sub][1]
                                c.wait(eng, "sDVE", f"b5:cl{e_}_{fb}_{fcl}_{sub}")
                                if gg >= 2:
                                    pfcl, psub = divmod(gg - 2, 2)
                                    c.wait(eng, "sDVE",
                                           f"b5:hm{e_}_{fb}_{pfcl}_{psub}")
                                c.op(eng, lambda e, gg=gg, cw=cw: e.activation(
                                     sg5[:, (gg % 2) * SC:(gg % 2) * SC + cw],
                                     cl5[:, (gg % 2) * SC:(gg % 2) * SC + cw],
                                     AF.Silu), inc="sACT",
                                     mark=f"b5:sil{e_}_{fb}_{fcl}_{sub}")
                for ddc in range(NC_):
                    c.wait(eng, ["sAC0", "sAC1"][ddc % 2], f"b5:accl{ddc}")
                    for st in range(16):
                        if st == 8:
                            c.wait(eng, "sIN", f"b5:accl2_{ddc}")
                        gb = ddc * 16 + st
                        if gb >= 4:
                            c.wait(eng, "sPE",
                                   f"b5:gtr{(gb - 4) // 16}_{(gb - 4) % 16}")
                        c.op(eng, lambda e, ddc=ddc, st=st: e.activation(
                             silb[:, (st % 4) * 128:(st % 4 + 1) * 128],
                             y_tm[:, (ddc % 3) * 2048 + st * 128:
                                  (ddc % 3) * 2048 + (st + 1) * 128],
                             AF.Silu), inc="sACT", mark=f"b5:gsil{ddc}_{st}")

            ctx.run_block([
                ("sync", b5_sp),
                ("gpsimd", b5_cv),
                ("gpsimd", b5_pl),
                ("tensor", b5_pe),
                ("vector", b5_dve),
                ("scalar", b5_act),
            ])
        elif "xn" not in dbg_outs:
            def fin(c, eng):
                for dc in range(NC_):
                    c.dma(eng, y_out[dc * 128:(dc + 1) * 128, :],
                          nrm[:, dc * T:(dc + 1) * T], "sPL", mark=f"y{dc}")
                c.wait(eng, "sPL", f"y{NC_ - 1}")
            ctx.run_block([("gpsimd", fin)])

    return nc



def _stage_shared(inputs):
    f = np.float32
    cat = np.concatenate
    s = {}
    s["w1cat"] = np.ascontiguousarray(cat([
        inputs["L0_attn_qk_Wq1"], inputs["L0_attn_qk_Wm1"],
        inputs["L0_attn_qk_Wk1"], inputs["L0_attn_qks_Wk1"],
        inputs["L0_attn_qks_Wq1"], inputs["L0_attn_qks_Wm1"],
        inputs["L0_attn_vo_Wv1"], inputs["L0_attn_vo_Wm1"]], axis=1), f)
    s["w2pair0"] = np.ascontiguousarray(cat([
        inputs["L0_attn_qk_Wq2"], inputs["L0_attn_qk_Wm2"]], axis=0), f)
    s["w2pair1"] = np.ascontiguousarray(cat([
        inputs["L0_attn_qk_Wk2"], inputs["L0_attn_qks_Wk2"]], axis=0), f)
    s["w2pair2"] = np.ascontiguousarray(cat([
        inputs["L0_attn_qks_Wq2"], inputs["L0_attn_qks_Wm2"]], axis=0), f)
    s["w2pair3"] = np.ascontiguousarray(cat([
        inputs["L0_attn_vo_Wv2"], inputs["L0_attn_vo_Wm2"]], axis=0), f)
    s["wacat"] = np.ascontiguousarray(cat([
        inputs["L0_attn_sm1_Wa"], inputs["L0_attn_sm1s_Wa"],
        inputs["L0_attn_sm2_Wa"]], axis=1), f)
    s["bdcat"] = np.ascontiguousarray(cat([
        inputs["L0_attn_sm1_Wb"], inputs["L0_attn_sm1s_Wb"],
        inputs["L0_attn_sm2_Wb"], inputs["L0_attn_sm1_Wd"],
        inputs["L0_attn_sm1s_Wd"], inputs["L0_attn_sm2_Wd"]], axis=1), f)
    b16 = lambda a: np.ascontiguousarray(np.asarray(a, f).astype(BF16NP))
    s["woT"] = b16(np.asarray(inputs["L0_attn_vo_Wo"], f).T)
    s["wg"] = b16(inputs["L0_ffn_Wg"])
    s["w1"] = b16(inputs["L0_ffn_w1"])
    s["w2"] = b16(inputs["L0_ffn_w2"])
    s["l1w0"] = b16(np.asarray(inputs["L1_attn_W_delta"], f).T)
    s["l1w1"] = b16(np.asarray(inputs["L1_attn_W_select"], f).T)
    s["l1w2"] = b16(np.asarray(inputs["L1_attn_W_in"], f).T)
    s["l1w3"] = b16(np.asarray(inputs["L1_attn_W_gate"], f).T)
    s["woutT"] = b16(np.asarray(inputs["L1_attn_W_out"], f).T)
    s["log_a"] = np.ascontiguousarray(
        np.asarray(inputs["L1_attn_log_a"], f).reshape(NC_, 128).T, f)
    s["ln_g"] = np.ascontiguousarray(
        np.asarray(inputs["L1_ffn_ln_gamma"], f).reshape(NC_, 128).T, f)
    s["ln_b"] = np.ascontiguousarray(
        np.asarray(inputs["L1_ffn_ln_beta"], f).reshape(NC_, 128).T, f)
    s["router"] = np.ascontiguousarray(inputs["L1_ffn_router_w"], f)
    s["guw"] = np.ascontiguousarray(
        np.asarray(inputs["L1_ffn_gate_up_w"], np.float32).astype(BF16NP))
    s["dnw"] = np.ascontiguousarray(
        np.asarray(inputs["L1_ffn_down_w"], np.float32).astype(BF16NP))
    s["sguw"] = np.ascontiguousarray(
        np.asarray(inputs["L1_ffn_shared_gate_up_w"], np.float32).astype(BF16NP))
    s["sdnw"] = np.ascontiguousarray(
        np.asarray(inputs["L1_ffn_shared_down_w"], np.float32).astype(BF16NP))
    return s


def kernel(**inputs):
    x = np.asarray(inputs["L0_attn_x"], np.float32)
    shared = _stage_shared(inputs)
    nc = build_nc()
    in_maps = []
    for b in range(8):
        m = dict(shared)
        m["xT"] = np.ascontiguousarray(x[b].T)
        in_maps.append(m)
    res = run_bass_kernel_spmd(nc, in_maps, core_ids=list(range(8)))
    out = np.stack([np.asarray(res.results[b]["y"]).T for b in range(8)], axis=0)
    return out.astype(np.float32)


if __name__ == "__main__":
    pass
